# version 09
# baseline (speedup 1.0000x reference)
import math
import numpy as np
import ml_dtypes

import concourse.bass as bass
import concourse.mybir as mybir
from concourse.bass_utils import run_bass_kernel_spmd

F32 = mybir.dt.float32
BF16 = mybir.dt.bfloat16
AF = mybir.ActivationFunctionType
ALU = mybir.AluOpType
AX = mybir.AxisListType
bfnp = ml_dtypes.bfloat16

NCORES = 8


class Cfg:
    def __init__(self, dm=4096, inter=11008, nh=32, nkv=8, seq=4096, batch=2,
                 xp=None):
        self.DM, self.IN, self.NH, self.NKV, self.HD = dm, inter, nh, nkv, 128
        assert nh * 128 == dm and batch == 2
        self.B, self.S = batch, seq
        self.L = seq // NCORES
        assert self.L % 128 == 0
        self.MT = self.L // 128
        self.LT = self.B * self.L
        self.W = self.L
        self.TS = self.W // 128
        self.DMC = dm // 128
        self.NIB = inter // 128
        self.NQKV = nh + 2 * nkv
        self.GRP = nh // nkv
        self.eps = 1e-5
        self.theta = 10000.0
        self.IH0 = (self.NIB + 1) // 2
        self.XP = xp if xp is not None else min(512, dm)
        self.NPC = dm // self.XP
        self.stop = "full"
        assert self.TS * (self.XP // 512) <= 4


def _img(wt):
    kin = wt.shape[0]
    return np.ascontiguousarray(
        wt.reshape(kin // 128, 128, 128).transpose(1, 0, 2).reshape(128, kin))


def host_prepare(cfg, inputs):
    f32 = np.float32
    x = np.asarray(inputs["L0_attn_x"], f32)
    w_qkv = np.asarray(inputs["L0_attn_w_qkv"], f32)
    b_qkv = np.asarray(inputs["L0_attn_b_qkv"], f32)
    w_o = np.asarray(inputs["L0_attn_w_o"], f32)
    b_o = np.asarray(inputs["L0_attn_b_o"], f32)
    w_g = np.asarray(inputs["L0_ffn_w_gate"], f32)
    b_g = np.asarray(inputs["L0_ffn_b_gate"], f32)
    w_u = np.asarray(inputs["L0_ffn_w_up"], f32)
    b_u = np.asarray(inputs["L0_ffn_b_up"], f32)
    w_d = np.asarray(inputs["L0_ffn_w_down"], f32)
    b_d = np.asarray(inputs["L0_ffn_b_down"], f32)
    DM, NH, NKV = cfg.DM, cfg.NH, cfg.NKV

    wgu = np.empty((2 * cfg.NIB, 128, DM), f32)
    for i in range(cfg.NIB):
        wgu[2 * i] = _img(np.ascontiguousarray(w_g[i * 128:(i + 1) * 128, :].T))
        wgu[2 * i + 1] = _img(np.ascontiguousarray(w_u[i * 128:(i + 1) * 128, :].T))
    wdn = np.ascontiguousarray(w_d.T).reshape(cfg.NIB, 128, DM)
    perm = np.concatenate([np.arange(0, 128, 2), np.arange(1, 128, 2)])
    wqkv = np.empty((cfg.NQKV, 128, DM), f32)
    bqkv_cols = np.empty((128, cfg.NQKV), f32)
    for j in range(cfg.NQKV):
        rows = np.arange(j * 128, (j + 1) * 128)
        if j < NH + NKV:
            rows = rows[perm]
        wqkv[j] = _img(np.ascontiguousarray(w_qkv[rows, :].T))
        bqkv_cols[:, j] = b_qkv[rows]
    bqkv_swap = np.concatenate([bqkv_cols[64:128, :], bqkv_cols[0:64, :]], 0)
    wo = np.ascontiguousarray(w_o.T).reshape(cfg.DMC, 128, DM)

    bias_cols = np.zeros((128, 2 * cfg.NIB), f32)
    for i in range(cfg.NIB):
        bias_cols[:, 2 * i] = b_g[i * 128:(i + 1) * 128]
        bias_cols[:, 2 * i + 1] = b_u[i * 128:(i + 1) * 128]
    brow_d = np.ascontiguousarray(b_d.reshape(1, DM))
    brow_o = np.ascontiguousarray(b_o.reshape(1, DM))

    in_maps = []
    inv_freq = 1.0 / cfg.theta ** (np.arange(0, 128, 2, dtype=np.float64) / 128)
    qs = 1.0 / math.sqrt(128.0)
    for c in range(NCORES):
        x_loc = np.ascontiguousarray(x[:, c::NCORES, :].reshape(cfg.LT, DM))
        pos = np.empty(cfg.LT, np.float64)
        for b in range(cfg.B):
            pos[b * cfg.L:(b + 1) * cfg.L] = (
                np.arange(cfg.L, dtype=np.float64) * NCORES + c)
        ang = inv_freq[:, None] * pos[None, :]
        cosv, sinv = np.cos(ang), np.sin(ang)
        cos_t = np.concatenate([cosv, cosv], 0).astype(f32)
        sin_t = np.concatenate([-sinv, sinv], 0).astype(f32)
        jj = np.arange(128)[:, None]
        uu = np.arange(128)[None, :]
        maskk = np.zeros((128, NCORES * 128), f32)
        for r in range(NCORES):
            maskk[:, r * 128:(r + 1) * 128] = (jj <= uu - (1 if r > c else 0))
        in_maps.append({
            "x_loc": x_loc, "wgu": wgu, "wdn": wdn, "wqkv": wqkv, "wo": wo,
            "bias_cols": bias_cols, "bqkv_cols": bqkv_cols,
            "bqkv_swap": np.ascontiguousarray(bqkv_swap),
            "brow_d": brow_d, "brow_o": brow_o,
            "cos_q": (cos_t * qs).astype(f32), "sin_q": (sin_t * qs).astype(f32),
            "cos_k": cos_t, "sin_k": sin_t,
            "maskk": maskk.astype(bfnp),
        })
    return in_maps


class Chan:
    __slots__ = ("sem", "n", "name")

    def __init__(self, nc, name):
        self.sem = nc.alloc_semaphore(name)
        self.n = 0
        self.name = name


def FR(*marks):
    out = []
    for m in marks:
        if m is None:
            continue
        if isinstance(m, list):
            out.extend(x for x in m if x is not None and x[0] is not None and x[1] > 0)
        elif m[0] is not None and m[1] > 0:
            out.append(m)
    return out


class Emit:
    ENGINES = ("sync", "scalar", "tensor", "vector", "gpsimd")

    def __init__(self, nc):
        self.nc = nc
        self.q = {e: [] for e in self.ENGINES}
        self.wm = {}
        self.self_chan = {}
        self.debug_src = False

    def chan(self, name):
        return Chan(self.nc, name)

    def op(self, eng, fn, waits=(), inc=None, inc_by=1):
        selfc = self.self_chan.get(eng)
        waits = list(waits)
        if selfc is not None and selfc.n > 0:
            waits.append((selfc, selfc.n))
        waits = tuple((c, v) for (c, v) in waits if v > 0)
        if inc is not None:
            inc.n += inc_by
        if self.debug_src:
            import traceback
            fr = traceback.extract_stack(limit=3)[1]
            self.q[eng].append((fn, waits, inc, inc_by, f"{fr.name}:{fr.lineno}"))
        else:
            self.q[eng].append((fn, waits, inc, inc_by))

    def emit_engine(self, eng_name, engine):
        wm = self.wm
        for fn, waits, inc, inc_by, *_ in self.q[eng_name]:
            for (c, v) in waits:
                key = (eng_name, c.name)
                if wm.get(key, 0) < v:
                    engine.wait_ge(c.sem, v)
                    wm[key] = v
            inst = fn(engine)
            if inc is not None:
                inst.then_inc(inc.sem, inc_by)


class Build:
    def __init__(self, cfg):
        self.cfg = cfg
        nc = bass.Bass("TRN2", target_bir_lowering=False,
                       dynamic_dma_scratch_size=16384)
        self.nc = nc
        c = cfg
        DM, LT = c.DM, c.LT
        dt = nc.dram_tensor

        self.x_loc = dt("x_loc", [LT, DM], F32, kind="ExternalInput").ap()
        self.wgu = dt("wgu", [2 * c.NIB, 128, DM], F32, kind="ExternalInput").ap()
        self.wdn = dt("wdn", [c.NIB, 128, DM], F32, kind="ExternalInput").ap()
        self.wqkv = dt("wqkv", [c.NQKV, 128, DM], F32, kind="ExternalInput").ap()
        self.wo = dt("wo", [c.DMC, 128, DM], F32, kind="ExternalInput").ap()
        self.bias_cols = dt("bias_cols", [128, 2 * c.NIB], F32, kind="ExternalInput").ap()
        self.bqkv_cols = dt("bqkv_cols", [128, c.NQKV], F32, kind="ExternalInput").ap()
        self.bqkv_swap = dt("bqkv_swap", [128, c.NQKV], F32, kind="ExternalInput").ap()
        self.brow_d = dt("brow_d", [1, DM], F32, kind="ExternalInput").ap()
        self.brow_o = dt("brow_o", [1, DM], F32, kind="ExternalInput").ap()
        self.cos_q = dt("cos_q", [128, LT], F32, kind="ExternalInput").ap()
        self.sin_q = dt("sin_q", [128, LT], F32, kind="ExternalInput").ap()
        self.cos_k = dt("cos_k", [128, LT], F32, kind="ExternalInput").ap()
        self.sin_k = dt("sin_k", [128, LT], F32, kind="ExternalInput").ap()
        self.maskk = dt("maskk", [128, NCORES * 128], BF16, kind="ExternalInput").ap()
        self.out = dt("out_loc", [LT, DM], F32, kind="ExternalOutput").ap()
        nslot_dbg = max(c.IH0, c.NIB - c.IH0)
        self.dbg = dt("dbg", [128, nslot_dbg * c.W], BF16, kind="ExternalOutput").ap()

        self.wdn_bf = dt("wdn_bf", [c.NIB, 128, DM], BF16).ap()
        self.wo_bf = dt("wo_bf", [c.DMC, 128, DM], BF16).ap()
        self.wgu_bf = dt("wgu_bf", [2 * c.NIB, 128, DM], BF16).ap()
        self.wqkv_bf = dt("wqkv_bf", [c.NQKV, 128, DM], BF16).ap()
        self.x1 = dt("x1", [LT, DM], F32).ap()
        self.xmid = dt("xmid", [LT, DM], F32).ap()
        self.x2 = dt("x2", [LT, DM], F32).ap()
        self.kt_in = [dt(f"kt_in{b}", [c.NKV * 128, c.L], BF16).ap() for b in range(2)]
        self.v_in = [dt(f"v_in{b}", [c.L, c.NKV * 128], BF16).ap() for b in range(2)]
        self.kn_in = [dt(f"kn_in{b}", [1, 16], F32).ap() for b in range(2)]
        self.kt_g = [dt(f"kt_g{b}", [NCORES * c.NKV * 128, c.L], BF16).ap() for b in range(2)]
        self.v_g = [dt(f"v_g{b}", [NCORES * c.L, c.NKV * 128], BF16).ap() for b in range(2)]
        self.kn_g = [dt(f"kn_g{b}", [NCORES, 16], F32).ap() for b in range(2)]

        self.i_ident = nc.inline_tensor(np.eye(128, dtype=bfnp), name="i_ident").ap()
        self.i_ones_r = nc.inline_tensor(np.ones((1, 128), bfnp), name="i_ones_r").ap()
        self.i_ones_c = nc.inline_tensor(np.ones((128, 1), bfnp), name="i_ones_c").ap()
        self.i_ones_rf = nc.inline_tensor(np.ones((1, 128), np.float32), name="i_ones_rf").ap()

        self._off = (nc.sbuf_base + 63) // 64 * 64
        alloc = []

        def A(name, shape, dtype, at=None):
            nbytes = int(np.prod(shape[1:])) * (2 if dtype == BF16 else 4)
            if at is None:
                at = self._off
                self._off += (nbytes + 63) // 64 * 64
            t = nc.alloc_sbuf_tensor_at(name, list(shape), dtype, offset=at)
            alloc.append((name, at, nbytes))
            setattr(self, name[0] == "s" and name or name, t)
            return t

        self.s_biasc = A("s_biasc", [128, 2 * c.NIB], F32)
        self.s_bqkvc = A("s_bqkvc", [128, c.NQKV], F32)
        self.s_bqkvs = A("s_bqkvs", [128, c.NQKV], F32)
        self.s_maskk = A("s_maskk", [128, NCORES * 128], BF16)
        self.s_ident = A("s_ident", [128, 128], BF16)
        self.s_ones_r = A("s_ones_r", [1, 128], BF16)
        self.s_ones_c = A("s_ones_c", [128, 1], BF16)
        self.s_ones_rf = A("s_ones_rf", [1, 128], F32)
        self.s_brow = A("s_brow", [1, DM], BF16)
        self.s_stats = A("s_stats", [128, 32], F32)
        self.s_small = A("s_small", [1, 128], F32)
        self.s_negc0 = A("s_negc0", [128, 1], F32)
        self.s_drow = A("s_drow", [1, max(c.W, 16 * NCORES)], F32)
        self.s_dbc = A("s_dbc", [128, c.W], F32)
        self.s_xp = [A(f"s_xp{i}", [128, c.XP], F32) for i in range(6)]
        self.s_probs = [A(f"s_probs{i}", [128, 1024], BF16) for i in range(2)]
        assert c.XP * 2 <= 2048
        self.s_lntok = [A(f"s_lntok{i}", [128, c.XP], BF16,
                          at=[a for a in alloc if a[0] == f"s_probs{i}"][0][1])
                        for i in range(2)]
        self.s_silu = [A(f"s_silu{i}", [128, c.W], F32) for i in range(2)]
        self.s_rsc = [A(f"s_rsc{i}", [128, c.W], F32,
                        at=[a for a in alloc if a[0] == f"s_silu{i}"][0][1])
                      for i in range(2)]
        self.s_rsc.append(A("s_rsc2", [128, c.W], F32))
        self.s_kstg = A("s_kstg", [128, c.W], BF16)
        self.s_sq = A("s_sq", [128, c.W], BF16)
        self.s_vstg = A("s_vstg", [128, c.TS * c.NKV * 128], BF16)
        self.s_wd = [A(f"s_wd{i}", [128, DM], BF16) for i in range(2)]
        self.s_xlnt = A("s_xlnt", [128, c.DMC * c.W], BF16)
        xlnt_off = [a for a in alloc if a[0] == "s_xlnt"][0][1]
        self.s_att = A("s_att", [128, c.DMC * c.W], BF16, at=xlnt_off)

        nslot = max(c.IH0, c.NIB - c.IH0)
        prod_sz = ((nslot * c.W * 2 + 63) // 64) * 64
        ring_sz = 4 * DM * 2
        qt_sz = c.NH * LT * 2
        big0 = self._off
        self._off += max(prod_sz + ring_sz, qt_sz + 2 * LT * 4)
        self.s_prod = A("s_prod", [128, nslot * c.W], BF16, at=big0)
        self.s_ring = [A(f"s_ring{i}", [128, DM], BF16, at=big0 + prod_sz + i * DM * 2)
                       for i in range(4)]
        self.s_qt = A("s_qt", [128, c.NH * LT], BF16, at=big0)
        self.s_cosq = A("s_cosq", [128, LT], F32, at=big0 + qt_sz)
        self.s_sinq = A("s_sinq", [128, LT], F32, at=big0 + qt_sz + LT * 4)

        kv0 = self._off
        ksz = ((NCORES * c.L * 2 + 63) // 64) * 64
        self._off += 4 * ksz
        self.s_ksb = [A(f"s_ksb{i}", [128, NCORES * c.L], BF16, at=kv0 + i * ksz)
                      for i in range(2)]
        self.s_vsb = [A(f"s_vsb{i}", [128, NCORES * c.L], BF16, at=kv0 + (2 + i) * ksz)
                      for i in range(2)]
        assert 2 * ksz >= 2 * LT * 4
        self.s_cosk = A("s_cosk", [128, LT], F32, at=kv0)
        self.s_sink = A("s_sink", [128, LT], F32, at=kv0 + LT * 4)
        assert ksz >= 2048 * 4
        self.s_wstg = [A(f"s_wstg{i}", [128, 2048], F32, at=kv0 + (2 + i) * ksz)
                       for i in range(2)]

        cap = nc.SBUF_PARTITION_SIZE_BYTES
        assert self._off <= cap, (self._off, cap)
        self.sbuf_used = self._off
        self.alloc_table = alloc

        self.ps6 = nc.alloc_psum_tensor("ps6", [128, 3072], F32)
        self.psb = nc.alloc_psum_tensor("psb", [128, 2048], BF16)

        E = self.E = Emit(nc)
        self.c_const = E.chan("c_const")
        self.c_wfree = E.chan("c_wfree")
        self.c_mm = E.chan("c_mm")
        self.c_act = E.chan("c_act")
        self.c_dve = E.chan("c_dve")
        self.c_lnt = E.chan("c_lnt")
        self.c_cc = [[E.chan(f"c_cc{b}_{i}") for i in range(3)] for b in range(2)]
        self.c_gdma = E.chan("c_gdma")
        self.c_wcg = [E.chan(f"c_wcg{i}") for i in range(4)]
        self.c_ring = [E.chan(f"c_ring{i}") for i in range(4)]
        self.c_wdl = [E.chan(f"c_wdl{i}") for i in range(2)]
        self.c_xp = [E.chan(f"c_xp{i}") for i in range(6)]
        self.c_ksb = [E.chan(f"c_ksb{i}") for i in range(2)]
        self.c_vsb = [E.chan(f"c_vsb{i}") for i in range(2)]
        self.c_qv = [E.chan(f"c_qv{i}") for i in range(2)]
        self.c_kstg = E.chan("c_kstg")
        self.c_vstg = E.chan("c_vstg")
        self.c_wgub = E.chan("c_wgub")
        self.c_stg = [E.chan(f"c_stg{i}") for i in range(2)]
        self.c_kn = E.chan("c_kn")
        self.c_attld = E.chan("c_attld")
        E.self_chan = {"scalar": self.c_act, "vector": self.c_dve}
        self.c_kv = E.chan("c_kv")

        self.wcast_mark = {}
        self.wgu_wb = {}
        self.wqkv_wb = {}
        self.ffn_call = 0
        self.wstg_free = [None, None]
        self.wstg_par = 0
        self.stage_done = None
        self.xp_free = [None] * 6
        self.lntok_free = [None] * 2
        self.silu_free = [None] * 2
        self.rsc_free = [None] * 3
        self.probs_free = [None] * 2
        self.kv_free = [None] * 2
        self.ring_free = [None] * 4
        self.wd_free = [None] * 2
        self.bank_free = {}
        self.psb_free = None
        self.kstg_free = None
        self.vstg_free = None
        self.sq_free = None
        self.stats_free = None
        self.small_free = None
        self.drow_free = None
        self.dbc_free = None
        self.negc0_free = None
        self.brow_ready = None
        self.big_free = None
        self.xlnt_write_ok = None
        self.att_write_ok = None
        self.cosk_write_ok = None
        self.ln_ready = 0
        self.qt_done = {}
        self.xready = {}


def mark(ch):
    return (ch, ch.n)


def emit_consts(B):
    E = B.E
    loads = [
        (B.s_biasc[:, :], B.bias_cols), (B.s_bqkvc[:, :], B.bqkv_cols),
        (B.s_bqkvs[:, :], B.bqkv_swap),
        (B.s_maskk[:, :], B.maskk), (B.s_ident[:, :], B.i_ident),
        (B.s_ones_r[:, :], B.i_ones_r), (B.s_ones_c[:, :], B.i_ones_c),
        (B.s_ones_rf[:, :], B.i_ones_rf),
        (B.s_cosk[:, :], B.cos_k), (B.s_sink[:, :], B.sin_k),
    ]
    for d, s in loads:
        E.op("sync", (lambda d, s: lambda eng: eng.dma_start(out=d, in_=s))(d, s),
             inc=B.c_const, inc_by=16)
    B.const_n = B.c_const.n
    E.op("gpsimd", lambda eng: eng.dma_start(out=B.s_brow[:, :], in_=B.brow_d),
         inc=B.c_gdma, inc_by=16)
    B.brow_ready = (B.c_gdma, B.c_gdma.n)


def load_brow(B, src_row):
    E = B.E
    E.op("gpsimd",
         (lambda s: lambda eng: eng.dma_start(out=B.s_brow[:, :], in_=s))(src_row),
         waits=[(B.c_mm, B.c_mm.n)], inc=B.c_gdma, inc_by=16)
    B.brow_ready = (B.c_gdma, B.c_gdma.n)


def emit_weight_cast(B, kind_jobs):
    E, cfg = B.E, B.cfg
    GSZ = 8
    jobs = kind_jobs
    group_end = {}
    for gi in range(0, len(jobs), GSZ):
        grp = jobs[gi:gi + GSZ]
        sidx = (gi // GSZ) % 4
        ch = B.c_wcg[sidx]
        throttle = [(ch, ch.n)] if ch.n > 0 else []
        for n, (kind, i, dst, src) in enumerate(grp):
            E.op("gpsimd",
                 (lambda d, s: lambda eng: eng.dma_start(out=d, in_=s))(dst, src),
                 waits=(throttle if n == 0 else []), inc=ch, inc_by=16)
        for kind, i, dst, src in grp:
            B.wcast_mark[(kind, i)] = (ch, ch.n)


def emit_ln(B, src, half, src_ready):
    E, cfg = B.E, B.cfg
    st = B.s_stats
    inv_n = 1.0 / cfg.DM
    TS, NPC, XP = cfg.TS, cfg.NPC, cfg.XP
    last_sq_add = None

    for t in range(TS):
        row0 = half * cfg.W + t * 128
        for p in range(NPC):
            slot = 4 + ((t * NPC + p) % 2)
            buf = B.s_xp[slot]
            cx = B.c_xp[slot]
            E.op("sync",
                 (lambda bb, rr, pp: lambda eng: eng.dma_start(
                     out=bb[:, :], in_=src[rr:rr + 128, pp * XP:(pp + 1) * XP]))(buf, row0, p),
                 waits=FR(src_ready, B.xp_free[slot]), inc=cx, inc_by=16)
            ldn = cx.n
            E.op("vector",
                 (lambda bb, tt, pp: lambda eng: eng.reduce_sum(
                     st[:, (tt if pp == 0 else 16):(tt if pp == 0 else 16) + 1],
                     bb[:, :], axis=AX.X))(buf, t, p),
                 waits=[(cx, ldn)] + FR(B.stats_free), inc=B.c_dve)
            if p > 0:
                E.op("vector",
                     (lambda tt: lambda eng: eng.tensor_add(
                         st[:, tt:tt + 1], st[:, tt:tt + 1], st[:, 16:17]))(t),
                     inc=B.c_dve)
            sl = (t * NPC + p) % 2
            E.op("scalar",
                 (lambda bb, tt, pp, sl: lambda eng: eng.activation(
                     B.s_lntok[sl][:, :], bb[:, :], AF.Square,
                     accum_out=st[:, (4 + tt if pp == 0 else 17):
                                  (4 + tt if pp == 0 else 17) + 1]))(buf, t, p, sl),
                 waits=[(cx, ldn)] + FR(B.lntok_free[sl], last_sq_add,
                                        B.stats_free),
                 inc=B.c_act)
            sq_n = B.c_act.n
            if p > 0:
                E.op("vector",
                     (lambda tt: lambda eng: eng.tensor_add(
                         st[:, 4 + tt:5 + tt], st[:, 4 + tt:5 + tt], st[:, 17:18]))(t),
                     waits=[(B.c_act, sq_n)], inc=B.c_dve)
                last_sq_add = mark(B.c_dve)
            B.xp_free[slot] = [mark(B.c_act), mark(B.c_dve)]
        E.op("vector",
             (lambda tt: lambda eng: eng.tensor_scalar(
                 st[:, 8 + tt:9 + tt], st[:, tt:tt + 1], inv_n, None, ALU.mult))(t),
             waits=[(B.c_act, B.c_act.n)], inc=B.c_dve)
        E.op("vector",
             (lambda tt: lambda eng: eng.tensor_scalar(
                 st[:, 16:17], st[:, 8 + tt:9 + tt], st[:, 8 + tt:9 + tt],
                 cfg.eps, ALU.mult, ALU.subtract))(t),
             inc=B.c_dve)
        E.op("vector",
             (lambda tt: lambda eng: eng.scalar_tensor_tensor(
                 st[:, 12 + tt:13 + tt], st[:, 4 + tt:5 + tt], inv_n,
                 st[:, 16:17], ALU.mult, ALU.subtract))(t),
             inc=B.c_dve)
    veps_done = B.c_dve.n

    E.op("scalar",
         lambda eng: eng.activation(st[:, 18:18 + TS], st[:, 12:12 + TS], AF.Ln),
         waits=[(B.c_dve, veps_done)], inc=B.c_act)
    E.op("vector",
         lambda eng: eng.tensor_scalar(st[:, 18:18 + TS], st[:, 18:18 + TS],
                                       -0.5, None, ALU.mult),
         waits=[(B.c_act, B.c_act.n)], inc=B.c_dve)
    E.op("scalar",
         lambda eng: eng.activation(st[:, 18:18 + TS], st[:, 18:18 + TS], AF.Exp),
         waits=[(B.c_dve, B.c_dve.n)], inc=B.c_act)
    rstd_done = B.c_act.n
    E.op("vector",
         lambda eng: eng.tensor_mul(st[:, 24:24 + TS], st[:, 8:8 + TS],
                                    st[:, 18:18 + TS]),
         waits=[(B.c_act, rstd_done)], inc=B.c_dve)
    E.op("vector",
         lambda eng: eng.tensor_scalar(st[:, 24:24 + TS], st[:, 24:24 + TS],
                                       -1.0, None, ALU.mult),
         inc=B.c_dve)
    negmr_done = B.c_dve.n

    for t in range(TS):
        row0 = half * cfg.W + t * 128
        for p in range(NPC):
            slot = 4 + ((t * NPC + p) % 2)
            buf = B.s_xp[slot]
            cx = B.c_xp[slot]
            E.op("sync",
                 (lambda bb, rr, pp: lambda eng: eng.dma_start(
                     out=bb[:, :], in_=src[rr:rr + 128, pp * XP:(pp + 1) * XP]))(buf, row0, p),
                 waits=FR(B.xp_free[slot]), inc=cx, inc_by=16)
            ldn = cx.n
            sl = (t * NPC + p) % 2
            lnt = B.s_lntok[sl]
            E.op("vector",
                 (lambda bb, oo, tt: lambda eng: eng.tensor_scalar(
                     oo[:, :], bb[:, :], st[:, 18 + tt:19 + tt],
                     st[:, 24 + tt:25 + tt], ALU.mult, ALU.add))(buf, lnt, t),
                 waits=[(cx, ldn), (B.c_dve, negmr_done),
                        (B.c_act, rstd_done)] + FR(B.lntok_free[sl]),
                 inc=B.c_dve)
            ap_done = B.c_dve.n
            B.xp_free[slot] = mark(B.c_dve)
            nch = XP // 128
            for g in range(nch):
                E.op("tensor",
                     (lambda gg, ii: lambda eng: eng.transpose(
                         B.psb[:, gg * 128:(gg + 1) * 128], ii, B.s_ident[:, :]))(
                         g, lnt[:, g * 128:(g + 1) * 128]),
                     waits=[(B.c_dve, ap_done)] + FR(B.psb_free),
                     inc=(B.c_mm if g == nch - 1 else None))
            tr_done = B.c_mm.n
            B.lntok_free[sl] = mark(B.c_mm)

            def evac(eng, pp=p, nch=nch, tt=t):
                dst = B.s_xlnt[:, :].rearrange("p (c w) -> p c w", w=cfg.W)[
                    :, pp * nch:(pp + 1) * nch, tt * 128:tt * 128 + 128]
                srcap = B.psb[:, 0:nch * 128].rearrange("p (c u) -> p c u", u=128)
                return eng.tensor_copy(dst, srcap)
            E.op("vector", evac,
                 waits=[(B.c_mm, tr_done)] + FR(B.xlnt_write_ok), inc=B.c_lnt)
            B.psb_free = mark(B.c_lnt)
    B.ln_ready = B.c_lnt.n
    B.stats_free = mark(B.c_act)


def emit_ffn_half(B, half, src_res, dst, src_ready):
    E, cfg = B.E, B.cfg
    xl = B.s_xlnt
    Wd = cfg.W
    ln_need = B.ln_ready
    groups = [list(range(0, cfg.IH0)), list(range(cfg.IH0, cfg.NIB))]
    groups = [g for g in groups if g]
    occ = cfg.XP
    n_pass = cfg.DM // occ
    mid_mark = {}
    dst_marks = []

    for ih, blocks in enumerate(groups):
        last_ih = ih == len(groups) - 1
        mode = ("f32", "f32wb", "bf16", "bf16")[min(B.ffn_call, 3)]
        for slot, i in enumerate(blocks):
            mmk = [0, 0]
            for gu in range(2):
                ri = (2 * slot + gu) % 4
                rb = B.s_ring[ri]
                idx = 2 * i + gu
                if mode != "bf16":
                    rcm = [None, None]
                    for hc in range(2):
                        sl = B.wstg_par
                        B.wstg_par ^= 1
                        stb = B.s_wstg[sl]
                        cs = B.c_stg[sl]
                        E.op("sync",
                             (lambda stb, idx, hc: lambda eng: eng.dma_start(
                                 out=stb[:, :],
                                 in_=B.wgu[idx][:, hc * 2048:(hc + 1) * 2048]))(
                                 stb, idx, hc),
                             waits=FR(B.wstg_free[sl]), inc=cs, inc_by=16)
                        ldn = cs.n
                        cast_w = [(cs, ldn)] + FR(B.ring_free[ri], B.big_free)
                        if hc == 0:
                            E.op("vector",
                                 (lambda rb, stb, hc: lambda eng: eng.tensor_copy(
                                     rb[:, hc * 2048:(hc + 1) * 2048],
                                     stb[:, :]))(rb, stb, hc),
                                 waits=cast_w, inc=B.c_dve)
                            rcm[hc] = (B.c_dve, B.c_dve.n)
                        else:
                            E.op("scalar",
                                 (lambda rb, stb, hc: lambda eng: eng.activation(
                                     rb[:, hc * 2048:(hc + 1) * 2048],
                                     stb[:, :], AF.Copy))(rb, stb, hc),
                                 waits=cast_w, inc=B.c_act)
                            rcm[hc] = (B.c_act, B.c_act.n)
                        B.wstg_free[sl] = rcm[hc]
                    if mode == "f32wb":
                        E.op("sync",
                             (lambda rb, idx: lambda eng: eng.dma_start(
                                 out=B.wgu_bf[idx], in_=rb[:, :]))(rb, idx),
                             waits=FR(rcm[0], rcm[1]),
                             inc=B.c_wgub, inc_by=16)
                        B.wgu_wb[idx] = (B.c_wgub, B.c_wgub.n)
                        wb_mark = (B.c_wgub, B.c_wgub.n)
                    else:
                        wb_mark = None
                    wlo, whi = rcm[0], rcm[1]
                else:
                    E.op("sync",
                         (lambda rb, idx: lambda eng: eng.dma_start(
                             out=rb[:, :], in_=B.wgu_bf[idx]))(rb, idx),
                         waits=FR(B.ring_free[ri], B.big_free,
                                  B.wgu_wb[idx]),
                         inc=B.c_ring[ri], inc_by=16)
                    wlo = whi = (B.c_ring[ri], B.c_ring[ri].n)
                    wb_mark = None
                bank = gu
                for k in range(cfg.DMC):
                    E.op("tensor",
                         (lambda rb, kk, bank, first, last: lambda eng: eng.matmul(
                             B.ps6[:, bank * 512:bank * 512 + Wd],
                             rb[:, kk * 128:(kk + 1) * 128],
                             xl[:, kk * Wd:(kk + 1) * Wd],
                             start=first, stop=last))(rb, k, bank, k == 0,
                                                      k == cfg.DMC - 1),
                         waits=([wlo if k < 16 else whi, (B.c_lnt, ln_need)]
                                + (FR(B.bank_free.get(bank)) if k == 0 else [])),
                         inc=(B.c_mm if k == cfg.DMC - 1 else None))
                mmk[gu] = B.c_mm.n
                B.ring_free[ri] = FR(mark(B.c_mm), wb_mark)
            ssl = slot % 2
            E.op("scalar",
                 (lambda ssl, i: lambda eng: eng.activation(
                     B.s_silu[ssl][:, :], B.ps6[:, 0:Wd], AF.Silu,
                     bias=B.s_biasc[:, 2 * i:2 * i + 1]))(ssl, i),
                 waits=[(B.c_mm, mmk[0])] + FR(B.silu_free[ssl]), inc=B.c_act)
            B.bank_free[0] = mark(B.c_act)
            silu_done = B.c_act.n
            E.op("vector",
                 (lambda ssl, i, slot: lambda eng: eng.scalar_tensor_tensor(
                     B.s_prod[:, slot * Wd:(slot + 1) * Wd],
                     B.ps6[:, 512:512 + Wd],
                     B.s_biasc[:, 2 * i + 1:2 * i + 2], B.s_silu[ssl][:, :],
                     ALU.add, ALU.mult))(ssl, i, slot),
                 waits=[(B.c_mm, mmk[1]), (B.c_act, silu_done)]
                 + FR(getattr(B, "prod_free", None), B.big_free), inc=B.c_dve)
            B.bank_free[1] = mark(B.c_dve)
            B.silu_free[ssl] = mark(B.c_dve)
        prod_done = B.c_dve.n
        if cfg.stop == "prod" and not getattr(B, "_dumped", False):
            B._dumped = True
            E.op("sync", lambda eng: eng.dma_start(out=B.dbg, in_=B.s_prod[:, :]),
                 waits=[(B.c_dve, prod_done)], inc=B.c_xp[0], inc_by=16)
            return
        B.xlnt_read_done = mark(B.c_mm)

        ts_mm_done = {}
        BB = max(1, min(8, cfg.DM // occ))
        for p in range(n_pass):
            for b0 in range(0, len(blocks), BB):
                bgrp = blocks[b0:b0 + BB]
                nb = len(bgrp)
                buf = (b0 // BB) % 2
                wdb = B.s_wd[buf]
                i0 = bgrp[0]
                E.op("sync",
                     (lambda wdb, i0, nb, p: lambda eng: eng.dma_start(
                         out=wdb[:, 0:nb * occ].rearrange(
                             "p (b u) -> p b u", u=occ),
                         in_=B.wdn_bf[i0:i0 + nb, :, p * occ:(p + 1) * occ]
                         .rearrange("b p u -> p b u")))(wdb, i0, nb, p),
                     waits=FR(B.wcast_mark[("dn", bgrp[-1])]) + FR(B.wd_free[buf]),
                     inc=B.c_wdl[buf], inc_by=16)
                wldn = (B.c_wdl[buf], B.c_wdl[buf].n)
                for bn, i in enumerate(bgrp):
                    bi = b0 + bn
                    last_bi = bi == len(blocks) - 1
                    for ts in range(cfg.TS):
                        base = ts * (occ // 512)
                        for oc2 in range(occ // 512):
                            tail = (ts == cfg.TS - 1 and oc2 == occ // 512 - 1
                                    and bn == nb - 1) \
                                or (last_bi and oc2 == occ // 512 - 1)
                            E.op("tensor",
                                 (lambda wdb, slot, bn, ts, oc2, base, first, stp:
                                  lambda eng: eng.matmul(
                                      B.ps6[:, (base + oc2) * 512:(base + oc2 + 1) * 512],
                                      B.s_prod[:, slot * Wd + ts * 128:
                                               slot * Wd + ts * 128 + 128],
                                      wdb[:, (bn * (occ // 512) + oc2) * 512:
                                          (bn * (occ // 512) + oc2 + 1) * 512],
                                      start=first, stop=stp))(wdb, bi, bn, ts, oc2,
                                                              base, bi == 0,
                                                              last_bi and not last_ih),
                                 waits=([wldn, (B.c_dve, prod_done)]
                                        + (FR(B.bank_free.get(base + oc2)) if bi == 0 else [])),
                                 inc=(B.c_mm if tail else None))
                        if last_bi:
                            ts_mm_done[ts] = B.c_mm.n
                B.wd_free[buf] = mark(B.c_mm)
            for ts in range(cfg.TS):
                base = ts * (occ // 512)
                if last_ih:
                    for oc2 in range(occ // 512):
                        E.op("tensor",
                             (lambda p, oc2, base: lambda eng: eng.matmul(
                                 B.ps6[:, (base + oc2) * 512:(base + oc2 + 1) * 512],
                                 B.s_ones_r[:, :],
                                 B.s_brow[:, p * occ + oc2 * 512:
                                          p * occ + (oc2 + 1) * 512],
                                 start=False, stop=True))(p, oc2, base),
                             waits=FR(B.brow_ready),
                             inc=(B.c_mm if oc2 == occ // 512 - 1 else None))
                    mm_done = B.c_mm.n
                else:
                    mm_done = ts_mm_done[ts]
                xslot = ts % 4
                xb = B.s_xp[xslot]
                row0 = half * Wd + ts * 128
                sx = src_res if ih == 0 else B.xmid
                dx = dst if last_ih else B.xmid
                cx = B.c_xp[xslot]
                E.op("scalar",
                     (lambda xb, row0, p, sx: lambda eng: eng.dma_start(
                         out=xb[:, 0:occ], in_=sx[row0:row0 + 128,
                                                  p * occ:(p + 1) * occ]))(xb, row0, p, sx),
                     waits=FR(B.xp_free[xslot],
                              src_ready if ih == 0 else mid_mark.get((p, ts))),
                     inc=cx, inc_by=16)
                ldn = cx.n
                E.op("vector",
                     (lambda xb, base: lambda eng: eng.scalar_tensor_tensor(
                         xb[:, 0:occ], B.ps6[:, base * 512:base * 512 + occ],
                         0.5, xb[:, 0:occ], ALU.mult, ALU.add))(xb, base),
                     waits=[(B.c_mm, mm_done), (cx, ldn)], inc=B.c_dve)
                for oc2 in range(occ // 512):
                    B.bank_free[base + oc2] = mark(B.c_dve)
                stt = B.c_dve.n
                E.op("scalar",
                     (lambda xb, row0, p, dx: lambda eng: eng.dma_start(
                         out=dx[row0:row0 + 128, p * occ:(p + 1) * occ],
                         in_=xb[:, 0:occ]))(xb, row0, p, dx),
                     waits=[(B.c_dve, stt)], inc=cx, inc_by=16)
                B.xp_free[xslot] = mark(cx)
                dst_marks.append(mark(cx))
                if not last_ih:
                    mid_mark[(p, ts)] = mark(cx)
        B.prod_free = mark(B.c_mm)
    B.big_free = mark(B.c_mm)
    B.dst_done = dst_marks
    if B.ffn_call == 1:
        B.stage_done = FR(B.wstg_free[0], B.wstg_free[1])
    B.ffn_call += 1


def emit_qkv_half(B, half):
    E, cfg = B.E, B.cfg
    Wd = cfg.W
    xl = B.s_xlnt
    ln_need = B.ln_ready
    NH, NKV = cfg.NH, cfg.NKV
    if half == 0:
        E.op("sync", lambda eng: eng.dma_start(out=B.s_cosq[:, :], in_=B.cos_q),
             waits=FR(B.big_free), inc=B.c_const, inc_by=16)
        E.op("sync", lambda eng: eng.dma_start(out=B.s_sinq[:, :], in_=B.sin_q),
             waits=FR(B.big_free), inc=B.c_const, inc_by=16)
        B.ropeq_n = B.c_const.n
    E.op("vector", lambda eng: eng.memset(B.s_small[0:1, 64:80], 0.0),
         waits=FR(B.small_free), inc=B.c_dve)

    for j in range(cfg.NQKV):
        wdb = B.s_wd[j % 2]
        if half == 0:
            E.op("gpsimd",
                 (lambda wdb, j: lambda eng: eng.dma_start(
                     out=wdb[:, :], in_=B.wqkv[j]))(wdb, j),
                 waits=FR(B.wd_free[j % 2]),
                 inc=B.c_qv[j % 2], inc_by=16)
            wldn = (B.c_qv[j % 2], B.c_qv[j % 2].n)
            E.op("sync",
                 (lambda wdb, j: lambda eng: eng.dma_start(
                     out=B.wqkv_bf[j], in_=wdb[:, :]))(wdb, j),
                 waits=[wldn], inc=B.c_wgub, inc_by=16)
            B.wqkv_wb[j] = (B.c_wgub, B.c_wgub.n)
            wb_mark = (B.c_wgub, B.c_wgub.n)
        else:
            E.op("sync",
                 (lambda wdb, j: lambda eng: eng.dma_start(
                     out=wdb[:, :], in_=B.wqkv_bf[j]))(wdb, j),
                 waits=FR(B.wd_free[j % 2], B.wqkv_wb[j]),
                 inc=B.c_qv[j % 2], inc_by=16)
            wldn = (B.c_qv[j % 2], B.c_qv[j % 2].n)
            wb_mark = None
        bank = j % 3
        for k in range(cfg.DMC):
            E.op("tensor",
                 (lambda wdb, kk, bank, first, last: lambda eng: eng.matmul(
                     B.ps6[:, bank * 512:bank * 512 + Wd],
                     wdb[:, kk * 128:(kk + 1) * 128],
                     xl[:, kk * Wd:(kk + 1) * Wd],
                     start=first, stop=last))(wdb, k, bank, k == 0, k == cfg.DMC - 1),
                 waits=([wldn, (B.c_lnt, ln_need)]
                        + (FR(B.bank_free.get(bank)) if k == 0 else [])),
                 inc=(B.c_mm if k == cfg.DMC - 1 else None))
        mmn = B.c_mm.n
        B.wd_free[j % 2] = FR(mark(B.c_mm), wb_mark)
        ps = B.ps6[:, bank * 512:bank * 512 + Wd]
        bcol = B.s_bqkvc[:, j:j + 1]

        if j < NH + NKV:
            if j < NH:
                cosT, sinT = B.s_cosq, B.s_sinq
                tabw = [(B.c_const, B.ropeq_n)]
            else:
                cosT, sinT = B.s_cosk, B.s_sink
                tabw = [(B.c_const, B.const_n)] + FR(getattr(B, "costab_alive", None))
            cs = cosT[:, half * Wd:(half + 1) * Wd]
            sn = sinT[:, half * Wd:(half + 1) * Wd]
            bswp = B.s_bqkvs[:, j:j + 1]
            E.op("vector",
                 (lambda ps, bcol, cs: lambda eng: eng.scalar_tensor_tensor(
                     B.s_rsc[1][:, :], ps, bcol, cs, ALU.add, ALU.mult))(ps, bcol, cs),
                 waits=[(B.c_mm, mmn)] + FR(B.rsc_free[1]) + tabw, inc=B.c_dve)
            m1_done = B.c_dve.n
            E.op("vector",
                 (lambda ps, bswp, sn: lambda eng: (
                     eng.scalar_tensor_tensor(
                         B.s_rsc[2][0:64, :], ps[64:128, :], bswp[0:64, :],
                         sn[0:64, :], ALU.add, ALU.mult),
                     eng.scalar_tensor_tensor(
                         B.s_rsc[2][64:128, :], ps[0:64, :], bswp[64:128, :],
                         sn[64:128, :], ALU.add, ALU.mult))[-1])(ps, bswp, sn),
                 waits=[(B.c_mm, mmn)] + FR(B.rsc_free[2]) + tabw, inc=B.c_dve)
            m2_done = B.c_dve.n
            B.bank_free[bank] = mark(B.c_dve)
            if j < NH:
                dstap = B.s_qt[:, j * cfg.LT + half * Wd:j * cfg.LT + (half + 1) * Wd]
                extra = FR(B.big_free)
            else:
                dstap = B.s_kstg[:, :]
                extra = FR(B.kstg_free)
            E.op("vector",
                 (lambda dstap: lambda eng: eng.tensor_add(
                     dstap, B.s_rsc[1][:, :], B.s_rsc[2][:, :]))(dstap),
                 waits=[(B.c_dve, m2_done)] + extra, inc=B.c_dve)
            rot_done = B.c_dve.n
            B.rsc_free[1] = mark(B.c_dve)
            B.rsc_free[2] = mark(B.c_dve)
            E.op("scalar",
                 (lambda dstap: lambda eng: eng.activation(
                     B.s_sq[:, :], dstap, AF.Square))(dstap),
                 waits=[(B.c_dve, rot_done)] + FR(B.sq_free), inc=B.c_act)
            sq_done = B.c_act.n
            E.op("tensor",
                 lambda eng: eng.matmul(B.ps6[0:1, 1536:1536 + Wd],
                                        B.s_ones_c[:, :], B.s_sq[:, :],
                                        start=True, stop=True),
                 waits=[(B.c_act, sq_done)] + FR(B.bank_free.get(3)), inc=B.c_mm)
            nrm_mm = B.c_mm.n
            B.sq_free = mark(B.c_mm)
            col = half * NH + j if j < NH else 64 + (j - NH)
            E.op("vector",
                 (lambda col: lambda eng: eng.reduce_max(
                     B.s_small[0:1, col:col + 1], B.ps6[0:1, 1536:1536 + Wd],
                     axis=AX.X))(col),
                 waits=[(B.c_mm, nrm_mm)], inc=B.c_dve)
            B.bank_free[3] = mark(B.c_dve)
            if j >= NH:
                E.op("sync",
                     (lambda j, half: lambda eng: eng.dma_start(
                         out=B.kt_in[half][(j - NH) * 128:(j - NH + 1) * 128, :],
                         in_=B.s_kstg[:, :]))(j, half),
                     waits=[(B.c_dve, rot_done)], inc=B.c_kstg, inc_by=16)
                B.kstg_free = [mark(B.c_kstg), mark(B.c_act)]
        else:
            cidx = j - NH - NKV
            E.op("vector",
                 (lambda ps, bcol: lambda eng: eng.tensor_scalar(
                     B.s_kstg[:, :], ps, bcol, None, ALU.add))(ps, bcol),
                 waits=[(B.c_mm, mmn)] + FR(B.kstg_free), inc=B.c_dve)
            ev_done = B.c_dve.n
            B.bank_free[bank] = mark(B.c_dve)
            for ts in range(cfg.TS):
                E.op("tensor",
                     (lambda ts: lambda eng: eng.transpose(
                         B.psb[:, ts * 128:(ts + 1) * 128],
                         B.s_kstg[:, ts * 128:(ts + 1) * 128],
                         B.s_ident[:, :]))(ts),
                     waits=[(B.c_dve, ev_done)] + FR(B.psb_free),
                     inc=(B.c_mm if ts == cfg.TS - 1 else None))
            tr_done = B.c_mm.n
            B.kstg_free = mark(B.c_mm)

            def evacv(eng, cidx=cidx):
                dstv = B.s_vstg[:, :].rearrange(
                    "p (t c u) -> p t c u", c=cfg.NKV, u=128)[:, :, cidx, :]
                srcv = B.psb[:, 0:cfg.TS * 128].rearrange("p (t u) -> p t u", u=128)
                return eng.tensor_copy(dstv, srcv)
            E.op("vector", evacv,
                 waits=[(B.c_mm, tr_done)] + FR(B.vstg_free if cidx == 0 else None),
                 inc=B.c_dve)
            B.psb_free = mark(B.c_dve)
            if cidx == cfg.NKV - 1:
                vdone = B.c_dve.n
                for ts in range(cfg.TS):
                    E.op("sync",
                         (lambda ts, half: lambda eng: eng.dma_start(
                             out=B.v_in[half][ts * 128:(ts + 1) * 128, :],
                             in_=B.s_vstg[:, ts * cfg.NKV * 128:
                                          (ts + 1) * cfg.NKV * 128]))(ts, half),
                         waits=[(B.c_dve, vdone)], inc=B.c_vstg, inc_by=16)
                B.vstg_free = mark(B.c_vstg)
    E.op("sync",
         (lambda half: lambda eng: eng.dma_start(
             out=B.kn_in[half][:, :], in_=B.s_small[0:1, 64:80]))(half),
         waits=[(B.c_dve, B.c_dve.n)], inc=B.c_kn, inc_by=16)
    B.small_free = mark(B.c_kn)
    B.qkv_ctrb = [mark(B.c_kstg), mark(B.c_vstg), mark(B.c_kn)]
    B.qt_done[half] = mark(B.c_dve)
    B.costab_alive = None
    if half == 1:
        B.cosk_write_ok = mark(B.c_dve)
        B.xlnt_write_ok_att = mark(B.c_mm)


def emit_gather(B, half):
    marks = B.qkv_ctrb
    pairs = [(B.kt_in[half], B.kt_g[half]), (B.v_in[half], B.v_g[half]),
             (B.kn_in[half], B.kn_g[half])]
    for i, (src_t, dst_t) in enumerate(pairs):
        B.E.op("gpsimd",
               (lambda s, d: lambda eng: eng.collective_compute(
                   "AllGather", ALU.bypass,
                   replica_groups=[list(range(NCORES))],
                   ins=[s[:, :]], outs=[d[:, :]]))(src_t, dst_t),
               waits=FR(marks[i]), inc=B.c_cc[half][i])


def emit_attention_half(B, half):
    E, cfg = B.E, B.cfg
    L, MT, TS, Wd = cfg.L, cfg.MT, cfg.TS, cfg.W
    E.op("sync",
         (lambda half: lambda eng: eng.dma_start(
             out=B.s_drow[0:1, 0:16 * NCORES],
             in_=B.kn_g[half][:, :].rearrange("(o r) c -> o (r c)", o=1)))(half),
         waits=[(B.c_cc[half][2], 1)] + FR(B.drow_free), inc=B.c_attld, inc_by=16)
    knl = B.c_attld.n
    E.op("vector",
         (lambda half: lambda eng: (
             eng.reduce_max(B.s_small[0:1, 90:91], B.s_drow[0:1, 0:16 * NCORES],
                            axis=AX.X),
             eng.reduce_max(B.s_small[0:1, 91:92],
                            B.s_small[0:1, half * cfg.NH:half * cfg.NH + cfg.NH],
                            axis=AX.X))[-1])(half),
         waits=[(B.c_attld, knl)] + [B.qt_done[half]], inc=B.c_dve)
    mx = B.c_dve.n
    E.op("scalar",
         lambda eng: (eng.activation(B.s_small[0:1, 92:93], B.s_small[0:1, 90:91], AF.Ln),
                      eng.activation(B.s_small[0:1, 93:94], B.s_small[0:1, 91:92], AF.Ln))[-1],
         waits=[(B.c_dve, mx)], inc=B.c_act)
    lns = B.c_act.n
    E.op("vector",
         lambda eng: eng.tensor_add(B.s_small[0:1, 94:95], B.s_small[0:1, 92:93],
                                    B.s_small[0:1, 93:94]),
         waits=[(B.c_act, lns)], inc=B.c_dve)
    E.op("vector",
         lambda eng: eng.tensor_scalar(B.s_small[0:1, 94:95], B.s_small[0:1, 94:95],
                                       0.5, None, ALU.mult),
         inc=B.c_dve)
    addn = B.c_dve.n
    E.op("scalar",
         lambda eng: eng.activation(B.s_small[0:1, 95:96], B.s_small[0:1, 94:95],
                                    AF.Exp),
         waits=[(B.c_dve, addn)], inc=B.c_act)
    bn = B.c_act.n
    E.op("vector",
         lambda eng: eng.tensor_scalar(B.s_small[0:1, 96:97], B.s_small[0:1, 95:96],
                                       -40.0, 0.0, ALU.add, ALU.max),
         waits=[(B.c_act, bn)], inc=B.c_dve)
    E.op("vector",
         lambda eng: eng.tensor_scalar(B.s_small[0:1, 96:97], B.s_small[0:1, 96:97],
                                       -1.0, None, ALU.mult),
         inc=B.c_dve)
    negn = B.c_dve.n
    E.op("tensor",
         lambda eng: eng.matmul(B.ps6[:, 0:1], B.s_ones_rf[:, :],
                                B.s_small[0:1, 96:97], start=True, stop=True),
         waits=[(B.c_dve, negn)] + FR(B.bank_free.get(0)), inc=B.c_mm)
    bcmm = B.c_mm.n
    E.op("vector",
         lambda eng: eng.tensor_copy(B.s_negc0[:, :], B.ps6[:, 0:1]),
         waits=[(B.c_mm, bcmm)] + FR(B.negc0_free), inc=B.c_dve)
    negc0_ready = mark(B.c_dve)
    B.bank_free[0] = mark(B.c_dve)
    B.drow_free = mark(B.c_dve)

    p_free = getattr(B, "p_free", None)
    for g in range(cfg.NKV):
        buf = g % 2
        ksb, vsb = B.s_ksb[buf], B.s_vsb[buf]
        kwex = (FR(B.kv_free[buf], B.stage_done)
                + (FR(B.cosk_write_ok) if buf == 0 else []))
        for r in range(NCORES):
            E.op("sync",
                 (lambda half, g, r, ksb: lambda eng: eng.dma_start(
                     out=ksb[:, r * L:(r + 1) * L],
                     in_=B.kt_g[half][r * cfg.NKV * 128 + g * 128:
                                      r * cfg.NKV * 128 + (g + 1) * 128, :]))(half, g, r, ksb),
                 waits=[(B.c_cc[half][0], 1)] + kwex, inc=B.c_ksb[buf], inc_by=16)
        for r in range(NCORES):
            for ts in range(TS):
                E.op("sync",
                     (lambda half, g, r, ts, vsb: lambda eng: eng.dma_start(
                         out=vsb[:, (r * TS + ts) * 128:(r * TS + ts + 1) * 128],
                         in_=B.v_g[half][r * L + ts * 128:r * L + (ts + 1) * 128,
                                         g * 128:(g + 1) * 128]))(half, g, r, ts, vsb),
                     waits=[(B.c_cc[half][1], 1)] + kwex, inc=B.c_vsb[buf], inc_by=16)
        kvld = [(B.c_ksb[buf], B.c_ksb[buf].n), (B.c_vsb[buf], B.c_vsb[buf].n)]
        for h in range(g * cfg.GRP, (g + 1) * cfg.GRP):
            groups = [(s, rp) for s in range(MT) for rp in range(NCORES // 2)]

            def emit_scores(gi):
                s, rp = groups[gi]
                span = (MT - s) * 128
                sb0 = 2 * (gi % 2)
                pb = gi % 2
                for r2 in range(2):
                    r = 2 * rp + r2
                    E.op("tensor",
                         (lambda ksb, h, half, r, s, span, bank:
                          lambda eng: eng.matmul(
                              B.ps6[:, bank * 512:bank * 512 + span],
                              ksb[:, r * L + s * 128:r * L + (s + 1) * 128],
                              B.s_qt[:, h * cfg.LT + half * Wd + s * 128:
                                     h * cfg.LT + (half + 1) * Wd],
                              start=True, stop=True))(ksb, h, half, r, s, span,
                                                      sb0 + r2),
                         waits=kvld + [B.qt_done[half]]
                         + FR(B.bank_free.get(sb0 + r2)),
                         inc=B.c_mm)
                scmm = B.c_mm.n
                prb = B.s_probs[pb]
                E.op("scalar",
                     (lambda prb, sb0, span: lambda eng: eng.activation(
                         prb[:, 0:2 * span].rearrange("p (t u) -> p t u", t=2),
                         B.ps6[:, sb0 * 512:sb0 * 512 + 1024].rearrange(
                             "p (t u) -> p t u", t=2)[:, :, 0:span],
                         AF.Exp, bias=B.s_negc0[:, :]))(prb, sb0, span),
                     waits=[(B.c_mm, scmm), negc0_ready] + FR(B.probs_free[pb]),
                     inc=B.c_act)
                expn = B.c_act.n
                B.bank_free[sb0] = mark(B.c_act)
                B.bank_free[sb0 + 1] = mark(B.c_act)
                E.op("vector",
                     (lambda prb, rp, span: lambda eng: eng.tensor_mul(
                         prb[:, 0:2 * span].rearrange(
                             "p (t u) -> p t u", t=2)[:, :, 0:128],
                         prb[:, 0:2 * span].rearrange(
                             "p (t u) -> p t u", t=2)[:, :, 0:128],
                         B.s_maskk[:, 2 * rp * 128:(2 * rp + 2) * 128].rearrange(
                             "p (t u) -> p t u", t=2)))(prb, rp, span),
                     waits=[(B.c_act, expn)], inc=B.c_dve)
                return B.c_dve.n

            def emit_sumpv(gi, mskn, first):
                s, rp = groups[gi]
                span = (MT - s) * 128
                pb = gi % 2
                prb = B.s_probs[pb]
                last_g = gi == len(groups) - 1
                for r2 in range(2):
                    r = 2 * rp + r2
                    stp = last_g and r2 == 1
                    E.op("tensor",
                         (lambda prb, r2, s, span, first, stp:
                          lambda eng: eng.matmul(
                              B.ps6[0:1, 2048 + s * 128:2048 + Wd],
                              B.s_ones_c[:, :],
                              prb[:, r2 * span:(r2 + 1) * span],
                              start=first, stop=stp))(prb, r2, s, span,
                                                      first and r2 == 0, stp),
                         waits=[(B.c_dve, mskn)]
                         + (FR(B.p_free) if first and r2 == 0 else []),
                         inc=None)
                    E.op("tensor",
                         (lambda prb, vsb, r, r2, s, span, first, stp:
                          lambda eng: eng.matmul(
                              B.ps6[:, 2560 + s * 128:2560 + Wd],
                              vsb[:, (r * TS + s) * 128:(r * TS + s + 1) * 128],
                              prb[:, r2 * span:(r2 + 1) * span],
                              start=first, stop=stp))(prb, vsb, r, r2, s, span,
                                                      first and r2 == 0, stp),
                         waits=[(B.c_dve, mskn)]
                         + (FR(B.p_free) if first and r2 == 0 else []),
                         inc=(B.c_mm if r2 == 1 else None))
                    first = False
                B.probs_free[pb] = mark(B.c_mm)

            B.p_free = p_free
            mskn_of = {}
            for gi in range(len(groups)):
                mskn_of[gi] = emit_scores(gi)
                if gi >= 1:
                    emit_sumpv(gi - 1, mskn_of[gi - 1], first=(gi - 1 == 0))
            emit_sumpv(len(groups) - 1, mskn_of[len(groups) - 1],
                       first=(len(groups) == 1))
            p_free = B.p_free
            pvn = B.c_mm.n
            E.op("vector",
                 lambda eng: eng.reciprocal(B.s_drow[0:1, 0:Wd],
                                            B.ps6[0:1, 2048:2048 + Wd]),
                 waits=[(B.c_mm, pvn)] + FR(B.drow_free), inc=B.c_dve)
            recn = B.c_dve.n
            E.op("tensor",
                 lambda eng: eng.matmul(B.ps6[:, 1024:1024 + Wd], B.s_ones_rf[:, :],
                                        B.s_drow[0:1, 0:Wd], start=True, stop=True),
                 waits=[(B.c_dve, recn)] + FR(B.bank_free.get(2)),
                 inc=B.c_mm)
            dbcmm = B.c_mm.n
            E.op("vector",
                 lambda eng: eng.tensor_copy(B.s_dbc[:, :], B.ps6[:, 1024:1024 + Wd]),
                 waits=[(B.c_mm, dbcmm)] + FR(B.dbc_free), inc=B.c_dve)
            dbcn = B.c_dve.n
            B.bank_free[2] = mark(B.c_dve)
            B.drow_free = mark(B.c_dve)
            E.op("vector",
                 (lambda h: lambda eng: eng.tensor_mul(
                     B.s_att[:, h * Wd:(h + 1) * Wd],
                     B.ps6[:, 2560:2560 + Wd], B.s_dbc[:, :]))(h),
                 waits=[(B.c_dve, dbcn)] + FR(B.att_write_ok), inc=B.c_dve)
            B.dbc_free = mark(B.c_dve)
            B.p_free = p_free = mark(B.c_dve)
        B.kv_free[buf] = mark(B.c_mm)
    B.att_done = B.c_dve.n
    B.negc0_free = mark(B.c_act)


def emit_wo_half(B, half, src_res, dst, src_ready):
    E, cfg = B.E, B.cfg
    Wd = cfg.W
    occ = cfg.XP
    n_pass = cfg.DM // occ
    att_need = B.att_done
    dst_marks = []
    BB = max(1, min(8, cfg.DM // occ))
    for p in range(n_pass):
        for k0 in range(0, cfg.DMC, BB):
            nb = min(BB, cfg.DMC - k0)
            buf = (k0 // BB) % 2
            wdb = B.s_wd[buf]
            E.op("sync",
                 (lambda wdb, k0, nb, p: lambda eng: eng.dma_start(
                     out=wdb[:, 0:nb * occ].rearrange("p (b u) -> p b u", u=occ),
                     in_=B.wo_bf[k0:k0 + nb, :, p * occ:(p + 1) * occ]
                     .rearrange("b p u -> p b u")))(wdb, k0, nb, p),
                 waits=FR(B.wcast_mark[("wo", k0 + nb - 1)]) + FR(B.wd_free[buf]),
                 inc=B.c_wdl[buf], inc_by=16)
            wldn = (B.c_wdl[buf], B.c_wdl[buf].n)
            for bn in range(nb):
                k = k0 + bn
                for ts in range(cfg.TS):
                    base = ts * (occ // 512)
                    for oc2 in range(occ // 512):
                        tail = (ts == cfg.TS - 1 and oc2 == occ // 512 - 1
                                and bn == nb - 1)
                        E.op("tensor",
                             (lambda wdb, k, bn, ts, oc2, base, first:
                              lambda eng: eng.matmul(
                                  B.ps6[:, (base + oc2) * 512:(base + oc2 + 1) * 512],
                                  B.s_att[:, k * Wd + ts * 128:k * Wd + ts * 128 + 128],
                                  wdb[:, (bn * (occ // 512) + oc2) * 512:
                                      (bn * (occ // 512) + oc2 + 1) * 512],
                                  start=first, stop=False))(wdb, k, bn, ts, oc2,
                                                            base, k == 0),
                             waits=([wldn, (B.c_dve, att_need)]
                                    + (FR(B.bank_free.get(base + oc2)) if k == 0 else [])),
                             inc=(B.c_mm if tail else None))
            B.wd_free[buf] = mark(B.c_mm)
        for ts in range(cfg.TS):
            base = ts * (occ // 512)
            for oc2 in range(occ // 512):
                E.op("tensor",
                     (lambda p, oc2, base: lambda eng: eng.matmul(
                         B.ps6[:, (base + oc2) * 512:(base + oc2 + 1) * 512],
                         B.s_ones_r[:, :],
                         B.s_brow[:, p * occ + oc2 * 512:p * occ + (oc2 + 1) * 512],
                         start=False, stop=True))(p, oc2, base),
                     waits=FR(B.brow_ready),
                     inc=(B.c_mm if oc2 == occ // 512 - 1 else None))
            mm_done = B.c_mm.n
            xslot = ts % 4
            xb = B.s_xp[xslot]
            cx = B.c_xp[xslot]
            row0 = half * Wd + ts * 128
            E.op("scalar",
                 (lambda xb, row0, p: lambda eng: eng.dma_start(
                     out=xb[:, 0:occ], in_=src_res[row0:row0 + 128,
                                                   p * occ:(p + 1) * occ]))(xb, row0, p),
                 waits=FR(B.xp_free[xslot], src_ready), inc=cx, inc_by=16)
            ldn = cx.n
            E.op("vector",
                 (lambda xb, base: lambda eng: eng.scalar_tensor_tensor(
                     xb[:, 0:occ], B.ps6[:, base * 512:base * 512 + occ],
                     1.0, xb[:, 0:occ], ALU.mult, ALU.add))(xb, base),
                 waits=[(B.c_mm, mm_done), (cx, ldn)], inc=B.c_dve)
            for oc2 in range(occ // 512):
                B.bank_free[base + oc2] = mark(B.c_dve)
            stt = B.c_dve.n
            E.op("scalar",
                 (lambda xb, row0, p: lambda eng: eng.dma_start(
                     out=dst[row0:row0 + 128, p * occ:(p + 1) * occ],
                     in_=xb[:, 0:occ]))(xb, row0, p),
                 waits=[(B.c_dve, stt)], inc=cx, inc_by=16)
            B.xp_free[xslot] = mark(cx)
            dst_marks.append(mark(cx))
    B.att_write_ok = mark(B.c_mm)
    B.dst_done = dst_marks


def _debug_out(B, src):
    E, cfg = B.E, B.cfg
    occ = cfg.XP
    done = B.dst_done
    for t in range(2 * cfg.TS):
        for p in range(cfg.DM // occ):
            xslot = (t * (cfg.DM // occ) + p) % 2
            xb = B.s_xp[xslot]
            cx = B.c_xp[xslot]
            E.op("sync",
                 (lambda xb, t, p: lambda eng: eng.dma_start(
                     out=xb[:, 0:occ], in_=src[t * 128:(t + 1) * 128,
                                               p * occ:(p + 1) * occ]))(xb, t, p),
                 waits=FR(B.xp_free[xslot], done), inc=cx, inc_by=16)
            ldn = cx.n
            E.op("sync",
                 (lambda xb, t, p: lambda eng: eng.dma_start(
                     out=B.out[t * 128:(t + 1) * 128, p * occ:(p + 1) * occ],
                     in_=xb[:, 0:occ]))(xb, t, p),
                 waits=[(cx, ldn)], inc=cx, inc_by=16)
            B.xp_free[xslot] = mark(cx)


def emit_tail(B):
    B.E.op("sync", lambda eng: eng.nop(),
           waits=[(c, c.n) for c in B.c_xp if c.n > 0])


def emit_all(B):
    cfg = B.cfg
    dn_jobs = [("dn", i, B.wdn_bf[i], B.wdn[i]) for i in range(cfg.NIB)]
    wo_jobs = [("wo", k, B.wo_bf[k], B.wo[k]) for k in range(cfg.DMC)]
    if cfg.stop == "prod":
        emit_consts(B)
        for e in ("tensor", "vector", "scalar"):
            B.E.op(e, lambda eng: eng.nop(), waits=[(B.c_const, B.const_n)])
        emit_weight_cast(B, dn_jobs + wo_jobs)
        emit_ln(B, B.x_loc, 0, None)
        emit_ffn_half(B, 0, B.x_loc, B.x1, None)
        emit_tail(B)
        return
    emit_consts(B)
    for e in ("tensor", "vector", "scalar"):
        B.E.op(e, lambda eng: eng.nop(), waits=[(B.c_const, B.const_n)])
    emit_weight_cast(B, dn_jobs)
    B._wo_jobs = wo_jobs
    for b in range(2):
        emit_ln(B, B.x_loc, b, None)
        emit_ffn_half(B, b, B.x_loc, B.x1, None)
        B.xready[("x1", b)] = B.dst_done
        if b == 0 and B._wo_jobs is not None:
            emit_weight_cast(B, B._wo_jobs)
            B._wo_jobs = None
    if cfg.stop == "ffn1":
        _debug_out(B, B.x1)
        emit_tail(B)
        return
    for b in range(2):
        emit_ln(B, B.x1, b, B.xready[("x1", b)])
        emit_qkv_half(B, b)
        emit_gather(B, b)
    B.att_write_ok = B.xlnt_write_ok_att
    load_brow(B, B.brow_o)
    for b in range(2):
        emit_attention_half(B, b)
        emit_wo_half(B, b, B.x1, B.x2, B.xready[("x1", b)])
        B.xready[("x2", b)] = B.dst_done
    if cfg.stop == "attn":
        _debug_out(B, B.x2)
        emit_tail(B)
        return
    load_brow(B, B.brow_d)
    B.xlnt_write_ok = B.att_write_ok
    for b in range(2):
        emit_ln(B, B.x2, b, B.xready[("x2", b)])
        emit_ffn_half(B, b, B.x2, B.out, B.xready[("x2", b)])
    emit_tail(B)


def build_program(cfg):
    B = Build(cfg)
    emit_all(B)
    nc, E = B.nc, B.E
    with nc.Block() as block:
        @block.sync
        def _(eng):
            E.emit_engine("sync", eng)

        @block.scalar
        def _(eng):
            E.emit_engine("scalar", eng)

        @block.tensor
        def _(eng):
            E.emit_engine("tensor", eng)

        @block.vector
        def _(eng):
            E.emit_engine("vector", eng)

        @block.gpsimd
        def _(eng):
            E.emit_engine("gpsimd", eng)
    return B, nc


def kernel(**inputs):
    cfg = Cfg()
    in_maps = host_prepare(cfg, inputs)
    B, nc = build_program(cfg)
    res = run_bass_kernel_spmd(nc, in_maps, core_ids=list(range(NCORES)))
    out = np.empty((cfg.B, cfg.S, cfg.DM), np.float32)
    for c in range(NCORES):
        loc = np.asarray(res.results[c]["out_loc"], dtype=np.float32)
        out[:, c::NCORES, :] = loc.reshape(cfg.B, cfg.L, cfg.DM)
    return out


if __name__ == "__main__":
    cfg = Cfg()
    B, nc = build_program(cfg)
    print("built:", nc.n_instructions(), "instructions; sbuf",
          B.sbuf_used, "/", nc.SBUF_PARTITION_SIZE_BYTES)


# version 10
# speedup vs baseline: 1.0328x; 1.0328x over previous
import math
import numpy as np
import ml_dtypes

import concourse.bass as bass
import concourse.mybir as mybir
from concourse.bass_utils import run_bass_kernel_spmd

F32 = mybir.dt.float32
BF16 = mybir.dt.bfloat16
AF = mybir.ActivationFunctionType
ALU = mybir.AluOpType
AX = mybir.AxisListType
bfnp = ml_dtypes.bfloat16

NCORES = 8


class Cfg:
    def __init__(self, dm=4096, inter=11008, nh=32, nkv=8, seq=4096, batch=2,
                 xp=None):
        self.DM, self.IN, self.NH, self.NKV, self.HD = dm, inter, nh, nkv, 128
        assert nh * 128 == dm and batch == 2
        self.B, self.S = batch, seq
        self.L = seq // NCORES
        assert self.L % 128 == 0
        self.MT = self.L // 128
        self.LT = self.B * self.L
        self.W = self.L
        self.TS = self.W // 128
        self.DMC = dm // 128
        self.NIB = inter // 128
        self.NQKV = nh + 2 * nkv
        self.GRP = nh // nkv
        self.eps = 1e-5
        self.theta = 10000.0
        self.IH0 = (self.NIB + 1) // 2
        self.XP = xp if xp is not None else min(512, dm)
        self.NPC = dm // self.XP
        self.stop = "full"
        assert self.TS * (self.XP // 512) <= 4


def _img(wt):
    kin = wt.shape[0]
    return np.ascontiguousarray(
        wt.reshape(kin // 128, 128, 128).transpose(1, 0, 2).reshape(128, kin))


def host_prepare(cfg, inputs):
    f32 = np.float32
    x = np.asarray(inputs["L0_attn_x"], f32)
    w_qkv = np.asarray(inputs["L0_attn_w_qkv"], f32)
    b_qkv = np.asarray(inputs["L0_attn_b_qkv"], f32)
    w_o = np.asarray(inputs["L0_attn_w_o"], f32)
    b_o = np.asarray(inputs["L0_attn_b_o"], f32)
    w_g = np.asarray(inputs["L0_ffn_w_gate"], f32)
    b_g = np.asarray(inputs["L0_ffn_b_gate"], f32)
    w_u = np.asarray(inputs["L0_ffn_w_up"], f32)
    b_u = np.asarray(inputs["L0_ffn_b_up"], f32)
    w_d = np.asarray(inputs["L0_ffn_w_down"], f32)
    b_d = np.asarray(inputs["L0_ffn_b_down"], f32)
    DM, NH, NKV = cfg.DM, cfg.NH, cfg.NKV

    wgu = np.empty((2 * cfg.NIB, 128, DM), f32)
    for i in range(cfg.NIB):
        wgu[2 * i] = _img(np.ascontiguousarray(w_g[i * 128:(i + 1) * 128, :].T))
        wgu[2 * i + 1] = _img(np.ascontiguousarray(w_u[i * 128:(i + 1) * 128, :].T))
    wdn = np.ascontiguousarray(w_d.T).reshape(cfg.NIB, 128, DM)
    perm = np.concatenate([np.arange(0, 128, 2), np.arange(1, 128, 2)])
    wqkv = np.empty((cfg.NQKV, 128, DM), f32)
    bqkv_cols = np.empty((128, cfg.NQKV), f32)
    for j in range(cfg.NQKV):
        rows = np.arange(j * 128, (j + 1) * 128)
        if j < NH + NKV:
            rows = rows[perm]
        wqkv[j] = _img(np.ascontiguousarray(w_qkv[rows, :].T))
        bqkv_cols[:, j] = b_qkv[rows]
    bqkv_swap = np.concatenate([bqkv_cols[64:128, :], bqkv_cols[0:64, :]], 0)
    wo = np.ascontiguousarray(w_o.T).reshape(cfg.DMC, 128, DM)

    bias_cols = np.zeros((128, 2 * cfg.NIB), f32)
    for i in range(cfg.NIB):
        bias_cols[:, 2 * i] = b_g[i * 128:(i + 1) * 128]
        bias_cols[:, 2 * i + 1] = b_u[i * 128:(i + 1) * 128]
    brow_d = np.ascontiguousarray(b_d.reshape(1, DM))
    brow_o = np.ascontiguousarray(b_o.reshape(1, DM))

    in_maps = []
    inv_freq = 1.0 / cfg.theta ** (np.arange(0, 128, 2, dtype=np.float64) / 128)
    qs = 1.0 / math.sqrt(128.0)
    for c in range(NCORES):
        x_loc = np.ascontiguousarray(x[:, c::NCORES, :].reshape(cfg.LT, DM))
        pos = np.empty(cfg.LT, np.float64)
        for b in range(cfg.B):
            pos[b * cfg.L:(b + 1) * cfg.L] = (
                np.arange(cfg.L, dtype=np.float64) * NCORES + c)
        ang = inv_freq[:, None] * pos[None, :]
        cosv, sinv = np.cos(ang), np.sin(ang)
        cos_t = np.concatenate([cosv, cosv], 0).astype(f32)
        sin_t = np.concatenate([-sinv, sinv], 0).astype(f32)
        jj = np.arange(128)[:, None]
        uu = np.arange(128)[None, :]
        maskk = np.zeros((128, NCORES * 128), f32)
        for r in range(NCORES):
            maskk[:, r * 128:(r + 1) * 128] = (jj <= uu - (1 if r > c else 0))
        in_maps.append({
            "x_loc": x_loc, "wgu": wgu, "wdn": wdn, "wqkv": wqkv, "wo": wo,
            "bias_cols": bias_cols, "bqkv_cols": bqkv_cols,
            "bqkv_swap": np.ascontiguousarray(bqkv_swap),
            "brow_d": brow_d, "brow_o": brow_o,
            "cos_q": (cos_t * qs).astype(f32), "sin_q": (sin_t * qs).astype(f32),
            "cos_k": cos_t, "sin_k": sin_t,
            "maskk": maskk.astype(bfnp),
        })
    return in_maps


class Chan:
    __slots__ = ("sem", "n", "name")

    def __init__(self, nc, name):
        self.sem = nc.alloc_semaphore(name)
        self.n = 0
        self.name = name


def FR(*marks):
    out = []
    for m in marks:
        if m is None:
            continue
        if isinstance(m, list):
            out.extend(x for x in m if x is not None and x[0] is not None and x[1] > 0)
        elif m[0] is not None and m[1] > 0:
            out.append(m)
    return out


class Emit:
    ENGINES = ("sync", "scalar", "tensor", "vector", "gpsimd")

    def __init__(self, nc):
        self.nc = nc
        self.q = {e: [] for e in self.ENGINES}
        self.wm = {}
        self.self_chan = {}
        self.debug_src = False

    def chan(self, name):
        return Chan(self.nc, name)

    def op(self, eng, fn, waits=(), inc=None, inc_by=1):
        selfc = self.self_chan.get(eng)
        waits = list(waits)
        if selfc is not None and selfc.n > 0:
            waits.append((selfc, selfc.n))
        waits = tuple((c, v) for (c, v) in waits if v > 0)
        if inc is not None:
            inc.n += inc_by
        if self.debug_src:
            import traceback
            fr = traceback.extract_stack(limit=3)[1]
            self.q[eng].append((fn, waits, inc, inc_by, f"{fr.name}:{fr.lineno}"))
        else:
            self.q[eng].append((fn, waits, inc, inc_by))

    def emit_engine(self, eng_name, engine):
        wm = self.wm
        for fn, waits, inc, inc_by, *_ in self.q[eng_name]:
            for (c, v) in waits:
                key = (eng_name, c.name)
                if wm.get(key, 0) < v:
                    engine.wait_ge(c.sem, v)
                    wm[key] = v
            inst = fn(engine)
            if inc is not None:
                inst.then_inc(inc.sem, inc_by)


class Build:
    def __init__(self, cfg):
        self.cfg = cfg
        nc = bass.Bass("TRN2", target_bir_lowering=False,
                       dynamic_dma_scratch_size=16384)
        self.nc = nc
        c = cfg
        DM, LT = c.DM, c.LT
        dt = nc.dram_tensor

        self.x_loc = dt("x_loc", [LT, DM], F32, kind="ExternalInput").ap()
        self.wgu = dt("wgu", [2 * c.NIB, 128, DM], F32, kind="ExternalInput").ap()
        self.wdn = dt("wdn", [c.NIB, 128, DM], F32, kind="ExternalInput").ap()
        self.wqkv = dt("wqkv", [c.NQKV, 128, DM], F32, kind="ExternalInput").ap()
        self.wo = dt("wo", [c.DMC, 128, DM], F32, kind="ExternalInput").ap()
        self.bias_cols = dt("bias_cols", [128, 2 * c.NIB], F32, kind="ExternalInput").ap()
        self.bqkv_cols = dt("bqkv_cols", [128, c.NQKV], F32, kind="ExternalInput").ap()
        self.bqkv_swap = dt("bqkv_swap", [128, c.NQKV], F32, kind="ExternalInput").ap()
        self.brow_d = dt("brow_d", [1, DM], F32, kind="ExternalInput").ap()
        self.brow_o = dt("brow_o", [1, DM], F32, kind="ExternalInput").ap()
        self.cos_q = dt("cos_q", [128, LT], F32, kind="ExternalInput").ap()
        self.sin_q = dt("sin_q", [128, LT], F32, kind="ExternalInput").ap()
        self.cos_k = dt("cos_k", [128, LT], F32, kind="ExternalInput").ap()
        self.sin_k = dt("sin_k", [128, LT], F32, kind="ExternalInput").ap()
        self.maskk = dt("maskk", [128, NCORES * 128], BF16, kind="ExternalInput").ap()
        self.out = dt("out_loc", [LT, DM], F32, kind="ExternalOutput").ap()
        nslot_dbg = max(c.IH0, c.NIB - c.IH0)
        self.dbg = dt("dbg", [128, nslot_dbg * c.W], BF16, kind="ExternalOutput").ap()

        self.wdn_bf = dt("wdn_bf", [c.NIB, 128, DM], BF16).ap()
        self.wo_bf = dt("wo_bf", [c.DMC, 128, DM], BF16).ap()
        self.wgu_bf = dt("wgu_bf", [2 * c.NIB, 128, DM], BF16).ap()
        self.wqkv_bf = dt("wqkv_bf", [c.NQKV, 128, DM], BF16).ap()
        self.x1 = dt("x1", [LT, DM], F32).ap()
        self.xmid = dt("xmid", [LT, DM], F32).ap()
        self.x2 = dt("x2", [LT, DM], F32).ap()
        self.kt_in = [dt(f"kt_in{b}", [c.NKV * 128, c.L], BF16).ap() for b in range(2)]
        self.v_in = [dt(f"v_in{b}", [c.L, c.NKV * 128], BF16).ap() for b in range(2)]
        self.kn_in = [dt(f"kn_in{b}", [1, 16], F32).ap() for b in range(2)]
        self.kt_g = [dt(f"kt_g{b}", [NCORES * c.NKV * 128, c.L], BF16).ap() for b in range(2)]
        self.v_g = [dt(f"v_g{b}", [NCORES * c.L, c.NKV * 128], BF16).ap() for b in range(2)]
        self.kn_g = [dt(f"kn_g{b}", [NCORES, 16], F32).ap() for b in range(2)]

        self.i_ident = nc.inline_tensor(np.eye(128, dtype=bfnp), name="i_ident").ap()
        self.i_ones_r = nc.inline_tensor(np.ones((1, 128), bfnp), name="i_ones_r").ap()
        self.i_ones_c = nc.inline_tensor(np.ones((128, 1), bfnp), name="i_ones_c").ap()
        self.i_ones_rf = nc.inline_tensor(np.ones((1, 128), np.float32), name="i_ones_rf").ap()

        self._off = (nc.sbuf_base + 63) // 64 * 64
        alloc = []

        def A(name, shape, dtype, at=None):
            nbytes = int(np.prod(shape[1:])) * (2 if dtype == BF16 else 4)
            if at is None:
                at = self._off
                self._off += (nbytes + 63) // 64 * 64
            t = nc.alloc_sbuf_tensor_at(name, list(shape), dtype, offset=at)
            alloc.append((name, at, nbytes))
            setattr(self, name[0] == "s" and name or name, t)
            return t

        self.s_biasc = A("s_biasc", [128, 2 * c.NIB], F32)
        self.s_bqkvc = A("s_bqkvc", [128, c.NQKV], F32)
        self.s_bqkvs = A("s_bqkvs", [128, c.NQKV], F32)
        self.s_maskk = A("s_maskk", [128, NCORES * 128], BF16)
        self.s_ident = A("s_ident", [128, 128], BF16)
        self.s_ones_r = A("s_ones_r", [1, 128], BF16)
        self.s_ones_c = A("s_ones_c", [128, 1], BF16)
        self.s_ones_rf = A("s_ones_rf", [1, 128], F32)
        self.s_brow = A("s_brow", [1, DM], BF16)
        self.s_stats = A("s_stats", [128, 32], F32)
        self.s_small = A("s_small", [1, 128], F32)
        self.s_negc0 = A("s_negc0", [128, 1], F32)
        self.s_drow = A("s_drow", [1, max(c.W, 16 * NCORES)], F32)
        self.s_dbc = A("s_dbc", [128, c.W], F32)
        self.s_xp = [A(f"s_xp{i}", [128, c.XP], F32) for i in range(6)]
        self.s_probs = [A(f"s_probs{i}", [128, 1024], BF16) for i in range(2)]
        assert c.XP * 2 <= 2048
        self.s_lntok = [A(f"s_lntok{i}", [128, c.XP], BF16,
                          at=[a for a in alloc if a[0] == f"s_probs{i}"][0][1])
                        for i in range(2)]
        self.s_silu = [A(f"s_silu{i}", [128, c.W], F32) for i in range(2)]
        self.s_rsc = [A(f"s_rsc{i}", [128, c.W], F32,
                        at=[a for a in alloc if a[0] == f"s_silu{i}"][0][1])
                      for i in range(2)]
        self.s_rsc.append(A("s_rsc2", [128, c.W], F32))
        self.s_kstg = A("s_kstg", [128, c.W], BF16)
        self.s_sq = A("s_sq", [128, c.W], BF16)
        self.s_vstg = A("s_vstg", [128, c.TS * c.NKV * 128], BF16)
        self.s_wd = [A(f"s_wd{i}", [128, DM], BF16) for i in range(2)]
        self.s_xlnt = A("s_xlnt", [128, c.DMC * c.W], BF16)
        xlnt_off = [a for a in alloc if a[0] == "s_xlnt"][0][1]
        self.s_att = A("s_att", [128, c.DMC * c.W], BF16, at=xlnt_off)

        nslot = max(c.IH0, c.NIB - c.IH0)
        prod_sz = ((nslot * c.W * 2 + 63) // 64) * 64
        ring_sz = 4 * DM * 2
        qt_sz = c.NH * LT * 2
        big0 = self._off
        self._off += max(prod_sz + ring_sz, qt_sz + 2 * LT * 4)
        self.s_prod = A("s_prod", [128, nslot * c.W], BF16, at=big0)
        self.s_ring = [A(f"s_ring{i}", [128, DM], BF16, at=big0 + prod_sz + i * DM * 2)
                       for i in range(4)]
        self.s_qt = A("s_qt", [128, c.NH * LT], BF16, at=big0)
        self.s_cosq = A("s_cosq", [128, LT], F32, at=big0 + qt_sz)
        self.s_sinq = A("s_sinq", [128, LT], F32, at=big0 + qt_sz + LT * 4)

        kv0 = self._off
        ksz = ((NCORES * c.L * 2 + 63) // 64) * 64
        self._off += 4 * ksz
        self.s_ksb = [A(f"s_ksb{i}", [128, NCORES * c.L], BF16, at=kv0 + i * ksz)
                      for i in range(2)]
        self.s_vsb = [A(f"s_vsb{i}", [128, NCORES * c.L], BF16, at=kv0 + (2 + i) * ksz)
                      for i in range(2)]
        assert 2 * ksz >= 2 * LT * 4
        self.s_cosk = A("s_cosk", [128, LT], F32, at=kv0)
        self.s_sink = A("s_sink", [128, LT], F32, at=kv0 + LT * 4)
        assert ksz >= 2048 * 4
        self.s_wstg = [A(f"s_wstg{i}", [128, 2048], F32, at=kv0 + (2 + i) * ksz)
                       for i in range(2)]

        cap = nc.SBUF_PARTITION_SIZE_BYTES
        assert self._off <= cap, (self._off, cap)
        self.sbuf_used = self._off
        self.alloc_table = alloc

        self.ps6 = nc.alloc_psum_tensor("ps6", [128, 3072], F32)
        self.psb = nc.alloc_psum_tensor("psb", [128, 2048], BF16)

        E = self.E = Emit(nc)
        self.c_const = E.chan("c_const")
        self.c_wfree = E.chan("c_wfree")
        self.c_mm = E.chan("c_mm")
        self.c_act = E.chan("c_act")
        self.c_dve = E.chan("c_dve")
        self.c_lnt = E.chan("c_lnt")
        self.c_cc = [[E.chan(f"c_cc{b}_{i}") for i in range(3)] for b in range(2)]
        self.c_gdma = E.chan("c_gdma")
        self.c_wcg = [E.chan(f"c_wcg{i}") for i in range(4)]
        self.c_ring = [E.chan(f"c_ring{i}") for i in range(4)]
        self.c_wdl = [E.chan(f"c_wdl{i}") for i in range(2)]
        self.c_xp = [E.chan(f"c_xp{i}") for i in range(6)]
        self.c_ksb = [E.chan(f"c_ksb{i}") for i in range(2)]
        self.c_vsb = [E.chan(f"c_vsb{i}") for i in range(2)]
        self.c_qv = [E.chan(f"c_qv{i}") for i in range(2)]
        self.c_kstg = E.chan("c_kstg")
        self.c_vstg = E.chan("c_vstg")
        self.c_wgub = E.chan("c_wgub")
        self.c_stg = [E.chan(f"c_stg{i}") for i in range(2)]
        self.c_kn = E.chan("c_kn")
        self.c_attld = E.chan("c_attld")
        E.self_chan = {"scalar": self.c_act, "vector": self.c_dve}
        self.c_kv = E.chan("c_kv")

        self.wcast_mark = {}
        self.wgu_wb = {}
        self.wqkv_wb = {}
        self.ffn_call = 0
        self.wstg_free = [None, None]
        self.wstg_par = 0
        self.stage_done = None
        self.pending_casts = []
        self.cast_rot = 0
        self.xp_free = [None] * 6
        self.lntok_free = [None] * 2
        self.silu_free = [None] * 2
        self.rsc_free = [None] * 3
        self.probs_free = [None] * 2
        self.kv_free = [None] * 2
        self.ring_free = [None] * 4
        self.wd_free = [None] * 2
        self.bank_free = {}
        self.psb_free = None
        self.kstg_free = None
        self.vstg_free = None
        self.sq_free = None
        self.stats_free = None
        self.small_free = None
        self.drow_free = None
        self.dbc_free = None
        self.negc0_free = None
        self.brow_ready = None
        self.big_free = None
        self.xlnt_write_ok = None
        self.att_write_ok = None
        self.cosk_write_ok = None
        self.ln_ready = 0
        self.qt_done = {}
        self.xready = {}


def mark(ch):
    return (ch, ch.n)


def emit_consts(B):
    E = B.E
    loads = [
        (B.s_biasc[:, :], B.bias_cols), (B.s_bqkvc[:, :], B.bqkv_cols),
        (B.s_bqkvs[:, :], B.bqkv_swap),
        (B.s_maskk[:, :], B.maskk), (B.s_ident[:, :], B.i_ident),
        (B.s_ones_r[:, :], B.i_ones_r), (B.s_ones_c[:, :], B.i_ones_c),
        (B.s_ones_rf[:, :], B.i_ones_rf),
        (B.s_cosk[:, :], B.cos_k), (B.s_sink[:, :], B.sin_k),
    ]
    for d, s in loads:
        E.op("sync", (lambda d, s: lambda eng: eng.dma_start(out=d, in_=s))(d, s),
             inc=B.c_const, inc_by=16)
    B.const_n = B.c_const.n
    E.op("gpsimd", lambda eng: eng.dma_start(out=B.s_brow[:, :], in_=B.brow_d),
         inc=B.c_gdma, inc_by=16)
    B.brow_ready = (B.c_gdma, B.c_gdma.n)


def load_brow(B, src_row):
    E = B.E
    E.op("gpsimd",
         (lambda s: lambda eng: eng.dma_start(out=B.s_brow[:, :], in_=s))(src_row),
         waits=[(B.c_mm, B.c_mm.n)], inc=B.c_gdma, inc_by=16)
    B.brow_ready = (B.c_gdma, B.c_gdma.n)


def emit_weight_cast(B, kind_jobs, pace=None):
    E, cfg = B.E, B.cfg
    GSZ = 8
    jobs = kind_jobs
    for gi in range(0, len(jobs), GSZ):
        grp = jobs[gi:gi + GSZ]
        sidx = (gi // GSZ) % 4
        ch = B.c_wcg[sidx]
        throttle = [(ch, ch.n)] if ch.n > 0 else []
        if pace is not None:
            throttle = throttle + FR(pace(gi))
        for n, (kind, i, dst, src) in enumerate(grp):
            E.op("gpsimd",
                 (lambda d, s: lambda eng: eng.dma_start(out=d, in_=s))(dst, src),
                 waits=(throttle if n == 0 else []), inc=ch, inc_by=16)
        for kind, i, dst, src in grp:
            B.wcast_mark[(kind, i)] = (ch, ch.n)


def emit_ln(B, src, half, src_ready):
    E, cfg = B.E, B.cfg
    st = B.s_stats
    inv_n = 1.0 / cfg.DM
    TS, NPC, XP = cfg.TS, cfg.NPC, cfg.XP
    last_sq_add = None

    for t in range(TS):
        row0 = half * cfg.W + t * 128
        for p in range(NPC):
            slot = 4 + ((t * NPC + p) % 2)
            buf = B.s_xp[slot]
            cx = B.c_xp[slot]
            E.op("sync",
                 (lambda bb, rr, pp: lambda eng: eng.dma_start(
                     out=bb[:, :], in_=src[rr:rr + 128, pp * XP:(pp + 1) * XP]))(buf, row0, p),
                 waits=FR(src_ready, B.xp_free[slot]), inc=cx, inc_by=16)
            ldn = cx.n
            E.op("vector",
                 (lambda bb, tt, pp: lambda eng: eng.reduce_sum(
                     st[:, (tt if pp == 0 else 16):(tt if pp == 0 else 16) + 1],
                     bb[:, :], axis=AX.X))(buf, t, p),
                 waits=[(cx, ldn)] + FR(B.stats_free), inc=B.c_dve)
            if p > 0:
                E.op("vector",
                     (lambda tt: lambda eng: eng.tensor_add(
                         st[:, tt:tt + 1], st[:, tt:tt + 1], st[:, 16:17]))(t),
                     inc=B.c_dve)
            sl = (t * NPC + p) % 2
            E.op("scalar",
                 (lambda bb, tt, pp, sl: lambda eng: eng.activation(
                     B.s_lntok[sl][:, :], bb[:, :], AF.Square,
                     accum_out=st[:, (4 + tt if pp == 0 else 17):
                                  (4 + tt if pp == 0 else 17) + 1]))(buf, t, p, sl),
                 waits=[(cx, ldn)] + FR(B.lntok_free[sl], last_sq_add,
                                        B.stats_free),
                 inc=B.c_act)
            sq_n = B.c_act.n
            if p > 0:
                E.op("vector",
                     (lambda tt: lambda eng: eng.tensor_add(
                         st[:, 4 + tt:5 + tt], st[:, 4 + tt:5 + tt], st[:, 17:18]))(t),
                     waits=[(B.c_act, sq_n)], inc=B.c_dve)
                last_sq_add = mark(B.c_dve)
            B.xp_free[slot] = [mark(B.c_act), mark(B.c_dve)]
        E.op("vector",
             (lambda tt: lambda eng: eng.tensor_scalar(
                 st[:, 8 + tt:9 + tt], st[:, tt:tt + 1], inv_n, None, ALU.mult))(t),
             waits=[(B.c_act, B.c_act.n)], inc=B.c_dve)
        E.op("vector",
             (lambda tt: lambda eng: eng.tensor_scalar(
                 st[:, 16:17], st[:, 8 + tt:9 + tt], st[:, 8 + tt:9 + tt],
                 cfg.eps, ALU.mult, ALU.subtract))(t),
             inc=B.c_dve)
        E.op("vector",
             (lambda tt: lambda eng: eng.scalar_tensor_tensor(
                 st[:, 12 + tt:13 + tt], st[:, 4 + tt:5 + tt], inv_n,
                 st[:, 16:17], ALU.mult, ALU.subtract))(t),
             inc=B.c_dve)
    veps_done = B.c_dve.n

    E.op("scalar",
         lambda eng: eng.activation(st[:, 18:18 + TS], st[:, 12:12 + TS], AF.Ln),
         waits=[(B.c_dve, veps_done)], inc=B.c_act)
    E.op("vector",
         lambda eng: eng.tensor_scalar(st[:, 18:18 + TS], st[:, 18:18 + TS],
                                       -0.5, None, ALU.mult),
         waits=[(B.c_act, B.c_act.n)], inc=B.c_dve)
    E.op("scalar",
         lambda eng: eng.activation(st[:, 18:18 + TS], st[:, 18:18 + TS], AF.Exp),
         waits=[(B.c_dve, B.c_dve.n)], inc=B.c_act)
    rstd_done = B.c_act.n
    E.op("vector",
         lambda eng: eng.tensor_mul(st[:, 24:24 + TS], st[:, 8:8 + TS],
                                    st[:, 18:18 + TS]),
         waits=[(B.c_act, rstd_done)], inc=B.c_dve)
    E.op("vector",
         lambda eng: eng.tensor_scalar(st[:, 24:24 + TS], st[:, 24:24 + TS],
                                       -1.0, None, ALU.mult),
         inc=B.c_dve)
    negmr_done = B.c_dve.n

    for t in range(TS):
        row0 = half * cfg.W + t * 128
        for p in range(NPC):
            slot = 4 + ((t * NPC + p) % 2)
            buf = B.s_xp[slot]
            cx = B.c_xp[slot]
            E.op("sync",
                 (lambda bb, rr, pp: lambda eng: eng.dma_start(
                     out=bb[:, :], in_=src[rr:rr + 128, pp * XP:(pp + 1) * XP]))(buf, row0, p),
                 waits=FR(B.xp_free[slot]), inc=cx, inc_by=16)
            ldn = cx.n
            sl = (t * NPC + p) % 2
            lnt = B.s_lntok[sl]
            E.op("vector",
                 (lambda bb, oo, tt: lambda eng: eng.tensor_scalar(
                     oo[:, :], bb[:, :], st[:, 18 + tt:19 + tt],
                     st[:, 24 + tt:25 + tt], ALU.mult, ALU.add))(buf, lnt, t),
                 waits=[(cx, ldn), (B.c_dve, negmr_done),
                        (B.c_act, rstd_done)] + FR(B.lntok_free[sl]),
                 inc=B.c_dve)
            ap_done = B.c_dve.n
            B.xp_free[slot] = mark(B.c_dve)
            nch = XP // 128
            for g in range(nch):
                E.op("tensor",
                     (lambda gg, ii: lambda eng: eng.transpose(
                         B.psb[:, gg * 128:(gg + 1) * 128], ii, B.s_ident[:, :]))(
                         g, lnt[:, g * 128:(g + 1) * 128]),
                     waits=[(B.c_dve, ap_done)] + FR(B.psb_free),
                     inc=(B.c_mm if g == nch - 1 else None))
            tr_done = B.c_mm.n
            B.lntok_free[sl] = mark(B.c_mm)

            def evac(eng, pp=p, nch=nch, tt=t):
                dst = B.s_xlnt[:, :].rearrange("p (c w) -> p c w", w=cfg.W)[
                    :, pp * nch:(pp + 1) * nch, tt * 128:tt * 128 + 128]
                srcap = B.psb[:, 0:nch * 128].rearrange("p (c u) -> p c u", u=128)
                return eng.tensor_copy(dst, srcap)
            E.op("vector", evac,
                 waits=[(B.c_mm, tr_done)] + FR(B.xlnt_write_ok), inc=B.c_lnt)
            B.psb_free = mark(B.c_lnt)
    B.ln_ready = B.c_lnt.n
    B.stats_free = mark(B.c_act)


def emit_ffn_half(B, half, src_res, dst, src_ready):
    E, cfg = B.E, B.cfg
    xl = B.s_xlnt
    Wd = cfg.W
    ln_need = B.ln_ready
    groups = [list(range(0, cfg.IH0)), list(range(cfg.IH0, cfg.NIB))]
    groups = [g for g in groups if g]
    occ = cfg.XP
    n_pass = cfg.DM // occ
    mid_mark = {}
    dst_marks = []

    for ih, blocks in enumerate(groups):
        last_ih = ih == len(groups) - 1
        mode = ("f32", "f32wb", "bf16", "bf16")[min(B.ffn_call, 3)]
        for slot, i in enumerate(blocks):
            mmk = [0, 0]
            for gu in range(2):
                ri = (2 * slot + gu) % 4
                rb = B.s_ring[ri]
                idx = 2 * i + gu
                if mode != "bf16":
                    rcm = [None, None]
                    for hc in range(2):
                        sl = B.wstg_par
                        B.wstg_par ^= 1
                        stb = B.s_wstg[sl]
                        cs = B.c_stg[sl]
                        E.op("sync",
                             (lambda stb, idx, hc: lambda eng: eng.dma_start(
                                 out=stb[:, :],
                                 in_=B.wgu[idx][:, hc * 2048:(hc + 1) * 2048]))(
                                 stb, idx, hc),
                             waits=FR(B.wstg_free[sl]), inc=cs, inc_by=16)
                        ldn = cs.n
                        cast_w = [(cs, ldn)] + FR(B.ring_free[ri], B.big_free)
                        if hc == 0:
                            E.op("vector",
                                 (lambda rb, stb, hc: lambda eng: eng.tensor_copy(
                                     rb[:, hc * 2048:(hc + 1) * 2048],
                                     stb[:, :]))(rb, stb, hc),
                                 waits=cast_w, inc=B.c_dve)
                            rcm[hc] = (B.c_dve, B.c_dve.n)
                        else:
                            E.op("scalar",
                                 (lambda rb, stb, hc: lambda eng: eng.activation(
                                     rb[:, hc * 2048:(hc + 1) * 2048],
                                     stb[:, :], AF.Copy))(rb, stb, hc),
                                 waits=cast_w, inc=B.c_act)
                            rcm[hc] = (B.c_act, B.c_act.n)
                        B.wstg_free[sl] = rcm[hc]
                    if mode == "f32wb":
                        E.op("sync",
                             (lambda rb, idx: lambda eng: eng.dma_start(
                                 out=B.wgu_bf[idx], in_=rb[:, :]))(rb, idx),
                             waits=FR(rcm[0], rcm[1]),
                             inc=B.c_wgub, inc_by=16)
                        B.wgu_wb[idx] = (B.c_wgub, B.c_wgub.n)
                        wb_mark = (B.c_wgub, B.c_wgub.n)
                    else:
                        wb_mark = None
                    wlo, whi = rcm[0], rcm[1]
                else:
                    E.op("sync",
                         (lambda rb, idx: lambda eng: eng.dma_start(
                             out=rb[:, :], in_=B.wgu_bf[idx]))(rb, idx),
                         waits=FR(B.ring_free[ri], B.big_free,
                                  B.wgu_wb[idx]),
                         inc=B.c_ring[ri], inc_by=16)
                    wlo = whi = (B.c_ring[ri], B.c_ring[ri].n)
                    wb_mark = None
                bank = gu
                for k in range(cfg.DMC):
                    E.op("tensor",
                         (lambda rb, kk, bank, first, last: lambda eng: eng.matmul(
                             B.ps6[:, bank * 512:bank * 512 + Wd],
                             rb[:, kk * 128:(kk + 1) * 128],
                             xl[:, kk * Wd:(kk + 1) * Wd],
                             start=first, stop=last))(rb, k, bank, k == 0,
                                                      k == cfg.DMC - 1),
                         waits=([wlo if k < 16 else whi, (B.c_lnt, ln_need)]
                                + (FR(B.bank_free.get(bank)) if k == 0 else [])),
                         inc=(B.c_mm if k == cfg.DMC - 1 else None))
                mmk[gu] = B.c_mm.n
                B.ring_free[ri] = FR(mark(B.c_mm), wb_mark)
            if B.pending_casts:
                kind, ci, cdst, csrc = B.pending_casts.pop(0)
                ch = B.c_wcg[B.cast_rot % 4]
                B.cast_rot += 1
                thr = ([(ch, ch.n)] if ch.n > 0 else []) + [(B.c_mm, mmk[0])]
                E.op("gpsimd",
                     (lambda d, s: lambda eng: eng.dma_start(out=d, in_=s))(
                         cdst, csrc),
                     waits=thr, inc=ch, inc_by=16)
                B.wcast_mark[(kind, ci)] = (ch, ch.n)
            ssl = slot % 2
            E.op("scalar",
                 (lambda ssl, i: lambda eng: eng.activation(
                     B.s_silu[ssl][:, :], B.ps6[:, 0:Wd], AF.Silu,
                     bias=B.s_biasc[:, 2 * i:2 * i + 1]))(ssl, i),
                 waits=[(B.c_mm, mmk[0])] + FR(B.silu_free[ssl]), inc=B.c_act)
            B.bank_free[0] = mark(B.c_act)
            silu_done = B.c_act.n
            E.op("vector",
                 (lambda ssl, i, slot: lambda eng: eng.scalar_tensor_tensor(
                     B.s_prod[:, slot * Wd:(slot + 1) * Wd],
                     B.ps6[:, 512:512 + Wd],
                     B.s_biasc[:, 2 * i + 1:2 * i + 2], B.s_silu[ssl][:, :],
                     ALU.add, ALU.mult))(ssl, i, slot),
                 waits=[(B.c_mm, mmk[1]), (B.c_act, silu_done)]
                 + FR(getattr(B, "prod_free", None), B.big_free), inc=B.c_dve)
            B.bank_free[1] = mark(B.c_dve)
            B.silu_free[ssl] = mark(B.c_dve)
        prod_done = B.c_dve.n
        if cfg.stop == "prod" and not getattr(B, "_dumped", False):
            B._dumped = True
            E.op("sync", lambda eng: eng.dma_start(out=B.dbg, in_=B.s_prod[:, :]),
                 waits=[(B.c_dve, prod_done)], inc=B.c_xp[0], inc_by=16)
            return
        B.xlnt_read_done = mark(B.c_mm)

        ts_mm_done = {}
        BB = max(1, min(8, cfg.DM // occ))
        for p in range(n_pass):
            for b0 in range(0, len(blocks), BB):
                bgrp = blocks[b0:b0 + BB]
                nb = len(bgrp)
                buf = (b0 // BB) % 2
                wdb = B.s_wd[buf]
                i0 = bgrp[0]
                E.op("sync",
                     (lambda wdb, i0, nb, p: lambda eng: eng.dma_start(
                         out=wdb[:, 0:nb * occ].rearrange(
                             "p (b u) -> p b u", u=occ),
                         in_=B.wdn_bf[i0:i0 + nb, :, p * occ:(p + 1) * occ]
                         .rearrange("b p u -> p b u")))(wdb, i0, nb, p),
                     waits=FR(B.wcast_mark[("dn", bgrp[-1])]) + FR(B.wd_free[buf]),
                     inc=B.c_wdl[buf], inc_by=16)
                wldn = (B.c_wdl[buf], B.c_wdl[buf].n)
                for bn, i in enumerate(bgrp):
                    bi = b0 + bn
                    last_bi = bi == len(blocks) - 1
                    for ts in range(cfg.TS):
                        base = ts * (occ // 512)
                        for oc2 in range(occ // 512):
                            tail = (ts == cfg.TS - 1 and oc2 == occ // 512 - 1
                                    and bn == nb - 1) \
                                or (last_bi and oc2 == occ // 512 - 1)
                            E.op("tensor",
                                 (lambda wdb, slot, bn, ts, oc2, base, first, stp:
                                  lambda eng: eng.matmul(
                                      B.ps6[:, (base + oc2) * 512:(base + oc2 + 1) * 512],
                                      B.s_prod[:, slot * Wd + ts * 128:
                                               slot * Wd + ts * 128 + 128],
                                      wdb[:, (bn * (occ // 512) + oc2) * 512:
                                          (bn * (occ // 512) + oc2 + 1) * 512],
                                      start=first, stop=stp))(wdb, bi, bn, ts, oc2,
                                                              base, bi == 0,
                                                              last_bi and not last_ih),
                                 waits=([wldn, (B.c_dve, prod_done)]
                                        + (FR(B.bank_free.get(base + oc2)) if bi == 0 else [])),
                                 inc=(B.c_mm if tail else None))
                        if last_bi:
                            ts_mm_done[ts] = B.c_mm.n
                B.wd_free[buf] = mark(B.c_mm)
            for ts in range(cfg.TS):
                base = ts * (occ // 512)
                if last_ih:
                    for oc2 in range(occ // 512):
                        E.op("tensor",
                             (lambda p, oc2, base: lambda eng: eng.matmul(
                                 B.ps6[:, (base + oc2) * 512:(base + oc2 + 1) * 512],
                                 B.s_ones_r[:, :],
                                 B.s_brow[:, p * occ + oc2 * 512:
                                          p * occ + (oc2 + 1) * 512],
                                 start=False, stop=True))(p, oc2, base),
                             waits=FR(B.brow_ready),
                             inc=(B.c_mm if oc2 == occ // 512 - 1 else None))
                    mm_done = B.c_mm.n
                else:
                    mm_done = ts_mm_done[ts]
                xslot = ts % 4
                xb = B.s_xp[xslot]
                row0 = half * Wd + ts * 128
                sx = src_res if ih == 0 else B.xmid
                dx = dst if last_ih else B.xmid
                cx = B.c_xp[xslot]
                E.op("scalar",
                     (lambda xb, row0, p, sx: lambda eng: eng.dma_start(
                         out=xb[:, 0:occ], in_=sx[row0:row0 + 128,
                                                  p * occ:(p + 1) * occ]))(xb, row0, p, sx),
                     waits=FR(B.xp_free[xslot],
                              src_ready if ih == 0 else mid_mark.get((p, ts))),
                     inc=cx, inc_by=16)
                ldn = cx.n
                E.op("vector",
                     (lambda xb, base: lambda eng: eng.scalar_tensor_tensor(
                         xb[:, 0:occ], B.ps6[:, base * 512:base * 512 + occ],
                         0.5, xb[:, 0:occ], ALU.mult, ALU.add))(xb, base),
                     waits=[(B.c_mm, mm_done), (cx, ldn)], inc=B.c_dve)
                for oc2 in range(occ // 512):
                    B.bank_free[base + oc2] = mark(B.c_dve)
                stt = B.c_dve.n
                E.op("scalar",
                     (lambda xb, row0, p, dx: lambda eng: eng.dma_start(
                         out=dx[row0:row0 + 128, p * occ:(p + 1) * occ],
                         in_=xb[:, 0:occ]))(xb, row0, p, dx),
                     waits=[(B.c_dve, stt)], inc=cx, inc_by=16)
                B.xp_free[xslot] = mark(cx)
                dst_marks.append(mark(cx))
                if not last_ih:
                    mid_mark[(p, ts)] = mark(cx)
        B.prod_free = mark(B.c_mm)
    B.big_free = mark(B.c_mm)
    B.dst_done = dst_marks
    if B.ffn_call == 1:
        B.stage_done = FR(B.wstg_free[0], B.wstg_free[1])
    B.ffn_call += 1


def emit_qkv_half(B, half):
    E, cfg = B.E, B.cfg
    Wd = cfg.W
    xl = B.s_xlnt
    ln_need = B.ln_ready
    NH, NKV = cfg.NH, cfg.NKV
    if half == 0:
        E.op("sync", lambda eng: eng.dma_start(out=B.s_cosq[:, :], in_=B.cos_q),
             waits=FR(B.big_free), inc=B.c_const, inc_by=16)
        E.op("sync", lambda eng: eng.dma_start(out=B.s_sinq[:, :], in_=B.sin_q),
             waits=FR(B.big_free), inc=B.c_const, inc_by=16)
        B.ropeq_n = B.c_const.n
    E.op("vector", lambda eng: eng.memset(B.s_small[0:1, 64:80], 0.0),
         waits=FR(B.small_free), inc=B.c_dve)

    for j in range(cfg.NQKV):
        wdb = B.s_wd[j % 2]
        if half == 0:
            E.op("gpsimd",
                 (lambda wdb, j: lambda eng: eng.dma_start(
                     out=wdb[:, :], in_=B.wqkv[j]))(wdb, j),
                 waits=FR(B.wd_free[j % 2]),
                 inc=B.c_qv[j % 2], inc_by=16)
            wldn = (B.c_qv[j % 2], B.c_qv[j % 2].n)
            E.op("sync",
                 (lambda wdb, j: lambda eng: eng.dma_start(
                     out=B.wqkv_bf[j], in_=wdb[:, :]))(wdb, j),
                 waits=[wldn], inc=B.c_wgub, inc_by=16)
            B.wqkv_wb[j] = (B.c_wgub, B.c_wgub.n)
            wb_mark = (B.c_wgub, B.c_wgub.n)
        else:
            E.op("sync",
                 (lambda wdb, j: lambda eng: eng.dma_start(
                     out=wdb[:, :], in_=B.wqkv_bf[j]))(wdb, j),
                 waits=FR(B.wd_free[j % 2], B.wqkv_wb[j]),
                 inc=B.c_qv[j % 2], inc_by=16)
            wldn = (B.c_qv[j % 2], B.c_qv[j % 2].n)
            wb_mark = None
        bank = j % 3
        for k in range(cfg.DMC):
            E.op("tensor",
                 (lambda wdb, kk, bank, first, last: lambda eng: eng.matmul(
                     B.ps6[:, bank * 512:bank * 512 + Wd],
                     wdb[:, kk * 128:(kk + 1) * 128],
                     xl[:, kk * Wd:(kk + 1) * Wd],
                     start=first, stop=last))(wdb, k, bank, k == 0, k == cfg.DMC - 1),
                 waits=([wldn, (B.c_lnt, ln_need)]
                        + (FR(B.bank_free.get(bank)) if k == 0 else [])),
                 inc=(B.c_mm if k == cfg.DMC - 1 else None))
        mmn = B.c_mm.n
        B.wd_free[j % 2] = FR(mark(B.c_mm), wb_mark)
        ps = B.ps6[:, bank * 512:bank * 512 + Wd]
        bcol = B.s_bqkvc[:, j:j + 1]

        if j < NH + NKV:
            if j < NH:
                cosT, sinT = B.s_cosq, B.s_sinq
                tabw = [(B.c_const, B.ropeq_n)]
            else:
                cosT, sinT = B.s_cosk, B.s_sink
                tabw = [(B.c_const, B.const_n)] + FR(getattr(B, "costab_alive", None))
            cs = cosT[:, half * Wd:(half + 1) * Wd]
            sn = sinT[:, half * Wd:(half + 1) * Wd]
            bswp = B.s_bqkvs[:, j:j + 1]
            E.op("vector",
                 (lambda ps, bcol, cs: lambda eng: eng.scalar_tensor_tensor(
                     B.s_rsc[1][:, :], ps, bcol, cs, ALU.add, ALU.mult))(ps, bcol, cs),
                 waits=[(B.c_mm, mmn)] + FR(B.rsc_free[1]) + tabw, inc=B.c_dve)
            m1_done = B.c_dve.n
            E.op("vector",
                 (lambda ps, bswp, sn: lambda eng: (
                     eng.scalar_tensor_tensor(
                         B.s_rsc[2][0:64, :], ps[64:128, :], bswp[0:64, :],
                         sn[0:64, :], ALU.add, ALU.mult),
                     eng.scalar_tensor_tensor(
                         B.s_rsc[2][64:128, :], ps[0:64, :], bswp[64:128, :],
                         sn[64:128, :], ALU.add, ALU.mult))[-1])(ps, bswp, sn),
                 waits=[(B.c_mm, mmn)] + FR(B.rsc_free[2]) + tabw, inc=B.c_dve)
            m2_done = B.c_dve.n
            B.bank_free[bank] = mark(B.c_dve)
            if j < NH:
                dstap = B.s_qt[:, j * cfg.LT + half * Wd:j * cfg.LT + (half + 1) * Wd]
                extra = FR(B.big_free)
            else:
                dstap = B.s_kstg[:, :]
                extra = FR(B.kstg_free)
            E.op("vector",
                 (lambda dstap: lambda eng: eng.tensor_add(
                     dstap, B.s_rsc[1][:, :], B.s_rsc[2][:, :]))(dstap),
                 waits=[(B.c_dve, m2_done)] + extra, inc=B.c_dve)
            rot_done = B.c_dve.n
            B.rsc_free[1] = mark(B.c_dve)
            B.rsc_free[2] = mark(B.c_dve)
            E.op("scalar",
                 (lambda dstap: lambda eng: eng.activation(
                     B.s_sq[:, :], dstap, AF.Square))(dstap),
                 waits=[(B.c_dve, rot_done)] + FR(B.sq_free), inc=B.c_act)
            sq_done = B.c_act.n
            E.op("tensor",
                 lambda eng: eng.matmul(B.ps6[0:1, 1536:1536 + Wd],
                                        B.s_ones_c[:, :], B.s_sq[:, :],
                                        start=True, stop=True),
                 waits=[(B.c_act, sq_done)] + FR(B.bank_free.get(3)), inc=B.c_mm)
            nrm_mm = B.c_mm.n
            B.sq_free = mark(B.c_mm)
            col = half * NH + j if j < NH else 64 + (j - NH)
            E.op("vector",
                 (lambda col: lambda eng: eng.reduce_max(
                     B.s_small[0:1, col:col + 1], B.ps6[0:1, 1536:1536 + Wd],
                     axis=AX.X))(col),
                 waits=[(B.c_mm, nrm_mm)], inc=B.c_dve)
            B.bank_free[3] = mark(B.c_dve)
            if j >= NH:
                E.op("sync",
                     (lambda j, half: lambda eng: eng.dma_start(
                         out=B.kt_in[half][(j - NH) * 128:(j - NH + 1) * 128, :],
                         in_=B.s_kstg[:, :]))(j, half),
                     waits=[(B.c_dve, rot_done)], inc=B.c_kstg, inc_by=16)
                B.kstg_free = [mark(B.c_kstg), mark(B.c_act)]
        else:
            cidx = j - NH - NKV
            E.op("vector",
                 (lambda ps, bcol: lambda eng: eng.tensor_scalar(
                     B.s_kstg[:, :], ps, bcol, None, ALU.add))(ps, bcol),
                 waits=[(B.c_mm, mmn)] + FR(B.kstg_free), inc=B.c_dve)
            ev_done = B.c_dve.n
            B.bank_free[bank] = mark(B.c_dve)
            for ts in range(cfg.TS):
                E.op("tensor",
                     (lambda ts: lambda eng: eng.transpose(
                         B.psb[:, ts * 128:(ts + 1) * 128],
                         B.s_kstg[:, ts * 128:(ts + 1) * 128],
                         B.s_ident[:, :]))(ts),
                     waits=[(B.c_dve, ev_done)] + FR(B.psb_free),
                     inc=(B.c_mm if ts == cfg.TS - 1 else None))
            tr_done = B.c_mm.n
            B.kstg_free = mark(B.c_mm)

            def evacv(eng, cidx=cidx):
                dstv = B.s_vstg[:, :].rearrange(
                    "p (t c u) -> p t c u", c=cfg.NKV, u=128)[:, :, cidx, :]
                srcv = B.psb[:, 0:cfg.TS * 128].rearrange("p (t u) -> p t u", u=128)
                return eng.tensor_copy(dstv, srcv)
            E.op("vector", evacv,
                 waits=[(B.c_mm, tr_done)] + FR(B.vstg_free if cidx == 0 else None),
                 inc=B.c_dve)
            B.psb_free = mark(B.c_dve)
            if cidx == cfg.NKV - 1:
                vdone = B.c_dve.n
                for ts in range(cfg.TS):
                    E.op("sync",
                         (lambda ts, half: lambda eng: eng.dma_start(
                             out=B.v_in[half][ts * 128:(ts + 1) * 128, :],
                             in_=B.s_vstg[:, ts * cfg.NKV * 128:
                                          (ts + 1) * cfg.NKV * 128]))(ts, half),
                         waits=[(B.c_dve, vdone)], inc=B.c_vstg, inc_by=16)
                B.vstg_free = mark(B.c_vstg)
    E.op("sync",
         (lambda half: lambda eng: eng.dma_start(
             out=B.kn_in[half][:, :], in_=B.s_small[0:1, 64:80]))(half),
         waits=[(B.c_dve, B.c_dve.n)], inc=B.c_kn, inc_by=16)
    B.small_free = mark(B.c_kn)
    B.qkv_ctrb = [mark(B.c_kstg), mark(B.c_vstg), mark(B.c_kn)]
    B.qt_done[half] = mark(B.c_dve)
    B.costab_alive = None
    if half == 1:
        B.cosk_write_ok = mark(B.c_dve)
        B.xlnt_write_ok_att = mark(B.c_mm)


def emit_gather(B, half):
    marks = B.qkv_ctrb
    pairs = [(B.kt_in[half], B.kt_g[half]), (B.v_in[half], B.v_g[half]),
             (B.kn_in[half], B.kn_g[half])]
    for i, (src_t, dst_t) in enumerate(pairs):
        B.E.op("gpsimd",
               (lambda s, d: lambda eng: eng.collective_compute(
                   "AllGather", ALU.bypass,
                   replica_groups=[list(range(NCORES))],
                   ins=[s[:, :]], outs=[d[:, :]]))(src_t, dst_t),
               waits=FR(marks[i]), inc=B.c_cc[half][i])


def emit_attention_half(B, half):
    E, cfg = B.E, B.cfg
    L, MT, TS, Wd = cfg.L, cfg.MT, cfg.TS, cfg.W
    E.op("sync",
         (lambda half: lambda eng: eng.dma_start(
             out=B.s_drow[0:1, 0:16 * NCORES],
             in_=B.kn_g[half][:, :].rearrange("(o r) c -> o (r c)", o=1)))(half),
         waits=[(B.c_cc[half][2], 1)] + FR(B.drow_free), inc=B.c_attld, inc_by=16)
    knl = B.c_attld.n
    E.op("vector",
         (lambda half: lambda eng: (
             eng.reduce_max(B.s_small[0:1, 90:91], B.s_drow[0:1, 0:16 * NCORES],
                            axis=AX.X),
             eng.reduce_max(B.s_small[0:1, 91:92],
                            B.s_small[0:1, half * cfg.NH:half * cfg.NH + cfg.NH],
                            axis=AX.X))[-1])(half),
         waits=[(B.c_attld, knl)] + [B.qt_done[half]], inc=B.c_dve)
    mx = B.c_dve.n
    E.op("scalar",
         lambda eng: (eng.activation(B.s_small[0:1, 92:93], B.s_small[0:1, 90:91], AF.Ln),
                      eng.activation(B.s_small[0:1, 93:94], B.s_small[0:1, 91:92], AF.Ln))[-1],
         waits=[(B.c_dve, mx)], inc=B.c_act)
    lns = B.c_act.n
    E.op("vector",
         lambda eng: eng.tensor_add(B.s_small[0:1, 94:95], B.s_small[0:1, 92:93],
                                    B.s_small[0:1, 93:94]),
         waits=[(B.c_act, lns)], inc=B.c_dve)
    E.op("vector",
         lambda eng: eng.tensor_scalar(B.s_small[0:1, 94:95], B.s_small[0:1, 94:95],
                                       0.5, None, ALU.mult),
         inc=B.c_dve)
    addn = B.c_dve.n
    E.op("scalar",
         lambda eng: eng.activation(B.s_small[0:1, 95:96], B.s_small[0:1, 94:95],
                                    AF.Exp),
         waits=[(B.c_dve, addn)], inc=B.c_act)
    bn = B.c_act.n
    E.op("vector",
         lambda eng: eng.tensor_scalar(B.s_small[0:1, 96:97], B.s_small[0:1, 95:96],
                                       -40.0, 0.0, ALU.add, ALU.max),
         waits=[(B.c_act, bn)], inc=B.c_dve)
    E.op("vector",
         lambda eng: eng.tensor_scalar(B.s_small[0:1, 96:97], B.s_small[0:1, 96:97],
                                       -1.0, None, ALU.mult),
         inc=B.c_dve)
    negn = B.c_dve.n
    E.op("tensor",
         lambda eng: eng.matmul(B.ps6[:, 0:1], B.s_ones_rf[:, :],
                                B.s_small[0:1, 96:97], start=True, stop=True),
         waits=[(B.c_dve, negn)] + FR(B.bank_free.get(0)), inc=B.c_mm)
    bcmm = B.c_mm.n
    E.op("vector",
         lambda eng: eng.tensor_copy(B.s_negc0[:, :], B.ps6[:, 0:1]),
         waits=[(B.c_mm, bcmm)] + FR(B.negc0_free), inc=B.c_dve)
    negc0_ready = mark(B.c_dve)
    B.bank_free[0] = mark(B.c_dve)
    B.drow_free = mark(B.c_dve)

    p_free = getattr(B, "p_free", None)
    for g in range(cfg.NKV):
        buf = g % 2
        ksb, vsb = B.s_ksb[buf], B.s_vsb[buf]
        kwex = (FR(B.kv_free[buf], B.stage_done)
                + (FR(B.cosk_write_ok) if buf == 0 else []))
        for r in range(NCORES):
            E.op("sync",
                 (lambda half, g, r, ksb: lambda eng: eng.dma_start(
                     out=ksb[:, r * L:(r + 1) * L],
                     in_=B.kt_g[half][r * cfg.NKV * 128 + g * 128:
                                      r * cfg.NKV * 128 + (g + 1) * 128, :]))(half, g, r, ksb),
                 waits=[(B.c_cc[half][0], 1)] + kwex, inc=B.c_ksb[buf], inc_by=16)
        for r in range(NCORES):
            for ts in range(TS):
                E.op("sync",
                     (lambda half, g, r, ts, vsb: lambda eng: eng.dma_start(
                         out=vsb[:, (r * TS + ts) * 128:(r * TS + ts + 1) * 128],
                         in_=B.v_g[half][r * L + ts * 128:r * L + (ts + 1) * 128,
                                         g * 128:(g + 1) * 128]))(half, g, r, ts, vsb),
                     waits=[(B.c_cc[half][1], 1)] + kwex, inc=B.c_vsb[buf], inc_by=16)
        kvld = [(B.c_ksb[buf], B.c_ksb[buf].n), (B.c_vsb[buf], B.c_vsb[buf].n)]
        for h in range(g * cfg.GRP, (g + 1) * cfg.GRP):
            groups = [(s, rp) for s in range(MT) for rp in range(NCORES // 2)]

            def emit_scores(gi):
                s, rp = groups[gi]
                span = (MT - s) * 128
                sb0 = 2 * (gi % 2)
                pb = gi % 2
                for r2 in range(2):
                    r = 2 * rp + r2
                    E.op("tensor",
                         (lambda ksb, h, half, r, s, span, bank:
                          lambda eng: eng.matmul(
                              B.ps6[:, bank * 512:bank * 512 + span],
                              ksb[:, r * L + s * 128:r * L + (s + 1) * 128],
                              B.s_qt[:, h * cfg.LT + half * Wd + s * 128:
                                     h * cfg.LT + (half + 1) * Wd],
                              start=True, stop=True))(ksb, h, half, r, s, span,
                                                      sb0 + r2),
                         waits=kvld + [B.qt_done[half]]
                         + FR(B.bank_free.get(sb0 + r2)),
                         inc=B.c_mm)
                scmm = B.c_mm.n
                prb = B.s_probs[pb]
                E.op("scalar",
                     (lambda prb, sb0, span: lambda eng: eng.activation(
                         prb[:, 0:2 * span].rearrange("p (t u) -> p t u", t=2),
                         B.ps6[:, sb0 * 512:sb0 * 512 + 1024].rearrange(
                             "p (t u) -> p t u", t=2)[:, :, 0:span],
                         AF.Exp, bias=B.s_negc0[:, :]))(prb, sb0, span),
                     waits=[(B.c_mm, scmm), negc0_ready] + FR(B.probs_free[pb]),
                     inc=B.c_act)
                expn = B.c_act.n
                B.bank_free[sb0] = mark(B.c_act)
                B.bank_free[sb0 + 1] = mark(B.c_act)
                E.op("vector",
                     (lambda prb, rp, span: lambda eng: eng.tensor_mul(
                         prb[:, 0:2 * span].rearrange(
                             "p (t u) -> p t u", t=2)[:, :, 0:128],
                         prb[:, 0:2 * span].rearrange(
                             "p (t u) -> p t u", t=2)[:, :, 0:128],
                         B.s_maskk[:, 2 * rp * 128:(2 * rp + 2) * 128].rearrange(
                             "p (t u) -> p t u", t=2)))(prb, rp, span),
                     waits=[(B.c_act, expn)], inc=B.c_dve)
                return B.c_dve.n

            def emit_sumpv(gi, mskn, first):
                s, rp = groups[gi]
                span = (MT - s) * 128
                pb = gi % 2
                prb = B.s_probs[pb]
                last_g = gi == len(groups) - 1
                for r2 in range(2):
                    r = 2 * rp + r2
                    stp = last_g and r2 == 1
                    E.op("tensor",
                         (lambda prb, r2, s, span, first, stp:
                          lambda eng: eng.matmul(
                              B.ps6[0:1, 2048 + s * 128:2048 + Wd],
                              B.s_ones_c[:, :],
                              prb[:, r2 * span:(r2 + 1) * span],
                              start=first, stop=stp))(prb, r2, s, span,
                                                      first and r2 == 0, stp),
                         waits=[(B.c_dve, mskn)]
                         + (FR(B.p_free) if first and r2 == 0 else []),
                         inc=None)
                    E.op("tensor",
                         (lambda prb, vsb, r, r2, s, span, first, stp:
                          lambda eng: eng.matmul(
                              B.ps6[:, 2560 + s * 128:2560 + Wd],
                              vsb[:, (r * TS + s) * 128:(r * TS + s + 1) * 128],
                              prb[:, r2 * span:(r2 + 1) * span],
                              start=first, stop=stp))(prb, vsb, r, r2, s, span,
                                                      first and r2 == 0, stp),
                         waits=[(B.c_dve, mskn)]
                         + (FR(B.p_free) if first and r2 == 0 else []),
                         inc=(B.c_mm if r2 == 1 else None))
                    first = False
                B.probs_free[pb] = mark(B.c_mm)

            B.p_free = p_free
            mskn_of = {}
            for gi in range(len(groups)):
                mskn_of[gi] = emit_scores(gi)
                if gi >= 1:
                    emit_sumpv(gi - 1, mskn_of[gi - 1], first=(gi - 1 == 0))
            emit_sumpv(len(groups) - 1, mskn_of[len(groups) - 1],
                       first=(len(groups) == 1))
            p_free = B.p_free
            pvn = B.c_mm.n
            E.op("vector",
                 lambda eng: eng.reciprocal(B.s_drow[0:1, 0:Wd],
                                            B.ps6[0:1, 2048:2048 + Wd]),
                 waits=[(B.c_mm, pvn)] + FR(B.drow_free), inc=B.c_dve)
            recn = B.c_dve.n
            E.op("tensor",
                 lambda eng: eng.matmul(B.ps6[:, 1024:1024 + Wd], B.s_ones_rf[:, :],
                                        B.s_drow[0:1, 0:Wd], start=True, stop=True),
                 waits=[(B.c_dve, recn)] + FR(B.bank_free.get(2)),
                 inc=B.c_mm)
            dbcmm = B.c_mm.n
            E.op("vector",
                 lambda eng: eng.tensor_copy(B.s_dbc[:, :], B.ps6[:, 1024:1024 + Wd]),
                 waits=[(B.c_mm, dbcmm)] + FR(B.dbc_free), inc=B.c_dve)
            dbcn = B.c_dve.n
            B.bank_free[2] = mark(B.c_dve)
            B.drow_free = mark(B.c_dve)
            E.op("vector",
                 (lambda h: lambda eng: eng.tensor_mul(
                     B.s_att[:, h * Wd:(h + 1) * Wd],
                     B.ps6[:, 2560:2560 + Wd], B.s_dbc[:, :]))(h),
                 waits=[(B.c_dve, dbcn)] + FR(B.att_write_ok), inc=B.c_dve)
            B.dbc_free = mark(B.c_dve)
            B.p_free = p_free = mark(B.c_dve)
        B.kv_free[buf] = mark(B.c_mm)
    B.att_done = B.c_dve.n
    B.negc0_free = mark(B.c_act)


def emit_wo_half(B, half, src_res, dst, src_ready):
    E, cfg = B.E, B.cfg
    Wd = cfg.W
    occ = cfg.XP
    n_pass = cfg.DM // occ
    att_need = B.att_done
    dst_marks = []
    BB = max(1, min(8, cfg.DM // occ))
    for p in range(n_pass):
        for k0 in range(0, cfg.DMC, BB):
            nb = min(BB, cfg.DMC - k0)
            buf = (k0 // BB) % 2
            wdb = B.s_wd[buf]
            E.op("sync",
                 (lambda wdb, k0, nb, p: lambda eng: eng.dma_start(
                     out=wdb[:, 0:nb * occ].rearrange("p (b u) -> p b u", u=occ),
                     in_=B.wo_bf[k0:k0 + nb, :, p * occ:(p + 1) * occ]
                     .rearrange("b p u -> p b u")))(wdb, k0, nb, p),
                 waits=FR(B.wcast_mark[("wo", k0 + nb - 1)]) + FR(B.wd_free[buf]),
                 inc=B.c_wdl[buf], inc_by=16)
            wldn = (B.c_wdl[buf], B.c_wdl[buf].n)
            for bn in range(nb):
                k = k0 + bn
                for ts in range(cfg.TS):
                    base = ts * (occ // 512)
                    for oc2 in range(occ // 512):
                        tail = (ts == cfg.TS - 1 and oc2 == occ // 512 - 1
                                and bn == nb - 1)
                        E.op("tensor",
                             (lambda wdb, k, bn, ts, oc2, base, first:
                              lambda eng: eng.matmul(
                                  B.ps6[:, (base + oc2) * 512:(base + oc2 + 1) * 512],
                                  B.s_att[:, k * Wd + ts * 128:k * Wd + ts * 128 + 128],
                                  wdb[:, (bn * (occ // 512) + oc2) * 512:
                                      (bn * (occ // 512) + oc2 + 1) * 512],
                                  start=first, stop=False))(wdb, k, bn, ts, oc2,
                                                            base, k == 0),
                             waits=([wldn, (B.c_dve, att_need)]
                                    + (FR(B.bank_free.get(base + oc2)) if k == 0 else [])),
                             inc=(B.c_mm if tail else None))
            B.wd_free[buf] = mark(B.c_mm)
        for ts in range(cfg.TS):
            base = ts * (occ // 512)
            for oc2 in range(occ // 512):
                E.op("tensor",
                     (lambda p, oc2, base: lambda eng: eng.matmul(
                         B.ps6[:, (base + oc2) * 512:(base + oc2 + 1) * 512],
                         B.s_ones_r[:, :],
                         B.s_brow[:, p * occ + oc2 * 512:p * occ + (oc2 + 1) * 512],
                         start=False, stop=True))(p, oc2, base),
                     waits=FR(B.brow_ready),
                     inc=(B.c_mm if oc2 == occ // 512 - 1 else None))
            mm_done = B.c_mm.n
            xslot = ts % 4
            xb = B.s_xp[xslot]
            cx = B.c_xp[xslot]
            row0 = half * Wd + ts * 128
            E.op("scalar",
                 (lambda xb, row0, p: lambda eng: eng.dma_start(
                     out=xb[:, 0:occ], in_=src_res[row0:row0 + 128,
                                                   p * occ:(p + 1) * occ]))(xb, row0, p),
                 waits=FR(B.xp_free[xslot], src_ready), inc=cx, inc_by=16)
            ldn = cx.n
            E.op("vector",
                 (lambda xb, base: lambda eng: eng.scalar_tensor_tensor(
                     xb[:, 0:occ], B.ps6[:, base * 512:base * 512 + occ],
                     1.0, xb[:, 0:occ], ALU.mult, ALU.add))(xb, base),
                 waits=[(B.c_mm, mm_done), (cx, ldn)], inc=B.c_dve)
            for oc2 in range(occ // 512):
                B.bank_free[base + oc2] = mark(B.c_dve)
            stt = B.c_dve.n
            E.op("scalar",
                 (lambda xb, row0, p: lambda eng: eng.dma_start(
                     out=dst[row0:row0 + 128, p * occ:(p + 1) * occ],
                     in_=xb[:, 0:occ]))(xb, row0, p),
                 waits=[(B.c_dve, stt)], inc=cx, inc_by=16)
            B.xp_free[xslot] = mark(cx)
            dst_marks.append(mark(cx))
    B.att_write_ok = mark(B.c_mm)
    B.dst_done = dst_marks


def _debug_out(B, src):
    E, cfg = B.E, B.cfg
    occ = cfg.XP
    done = B.dst_done
    for t in range(2 * cfg.TS):
        for p in range(cfg.DM // occ):
            xslot = (t * (cfg.DM // occ) + p) % 2
            xb = B.s_xp[xslot]
            cx = B.c_xp[xslot]
            E.op("sync",
                 (lambda xb, t, p: lambda eng: eng.dma_start(
                     out=xb[:, 0:occ], in_=src[t * 128:(t + 1) * 128,
                                               p * occ:(p + 1) * occ]))(xb, t, p),
                 waits=FR(B.xp_free[xslot], done), inc=cx, inc_by=16)
            ldn = cx.n
            E.op("sync",
                 (lambda xb, t, p: lambda eng: eng.dma_start(
                     out=B.out[t * 128:(t + 1) * 128, p * occ:(p + 1) * occ],
                     in_=xb[:, 0:occ]))(xb, t, p),
                 waits=[(cx, ldn)], inc=cx, inc_by=16)
            B.xp_free[xslot] = mark(cx)


def emit_tail(B):
    B.E.op("sync", lambda eng: eng.nop(),
           waits=[(c, c.n) for c in B.c_xp if c.n > 0])


def emit_all(B):
    cfg = B.cfg
    dn_jobs = [("dn", i, B.wdn_bf[i], B.wdn[i]) for i in range(cfg.NIB)]
    wo_jobs = [("wo", k, B.wo_bf[k], B.wo[k]) for k in range(cfg.DMC)]
    if cfg.stop == "prod":
        emit_consts(B)
        for e in ("tensor", "vector", "scalar"):
            B.E.op(e, lambda eng: eng.nop(), waits=[(B.c_const, B.const_n)])
        emit_weight_cast(B, dn_jobs + wo_jobs)
        emit_ln(B, B.x_loc, 0, None)
        emit_ffn_half(B, 0, B.x_loc, B.x1, None)
        emit_tail(B)
        return
    emit_consts(B)
    for e in ("tensor", "vector", "scalar"):
        B.E.op(e, lambda eng: eng.nop(), waits=[(B.c_const, B.const_n)])
    B.pending_casts.extend(dn_jobs)
    for b in range(2):
        emit_ln(B, B.x_loc, b, None)
        if b == 1:
            B.pending_casts.extend(wo_jobs)
        emit_ffn_half(B, b, B.x_loc, B.x1, None)
        B.xready[("x1", b)] = B.dst_done
    if cfg.stop == "ffn1":
        _debug_out(B, B.x1)
        emit_tail(B)
        return
    for b in range(2):
        emit_ln(B, B.x1, b, B.xready[("x1", b)])
        emit_qkv_half(B, b)
        emit_gather(B, b)
    B.att_write_ok = B.xlnt_write_ok_att
    load_brow(B, B.brow_o)
    for b in range(2):
        emit_attention_half(B, b)
        emit_wo_half(B, b, B.x1, B.x2, B.xready[("x1", b)])
        B.xready[("x2", b)] = B.dst_done
    if cfg.stop == "attn":
        _debug_out(B, B.x2)
        emit_tail(B)
        return
    load_brow(B, B.brow_d)
    B.xlnt_write_ok = B.att_write_ok
    for b in range(2):
        emit_ln(B, B.x2, b, B.xready[("x2", b)])
        emit_ffn_half(B, b, B.x2, B.out, B.xready[("x2", b)])
    emit_tail(B)


def build_program(cfg):
    B = Build(cfg)
    emit_all(B)
    nc, E = B.nc, B.E
    with nc.Block() as block:
        @block.sync
        def _(eng):
            E.emit_engine("sync", eng)

        @block.scalar
        def _(eng):
            E.emit_engine("scalar", eng)

        @block.tensor
        def _(eng):
            E.emit_engine("tensor", eng)

        @block.vector
        def _(eng):
            E.emit_engine("vector", eng)

        @block.gpsimd
        def _(eng):
            E.emit_engine("gpsimd", eng)
    return B, nc


def kernel(**inputs):
    cfg = Cfg()
    in_maps = host_prepare(cfg, inputs)
    B, nc = build_program(cfg)
    res = run_bass_kernel_spmd(nc, in_maps, core_ids=list(range(NCORES)))
    out = np.empty((cfg.B, cfg.S, cfg.DM), np.float32)
    for c in range(NCORES):
        loc = np.asarray(res.results[c]["out_loc"], dtype=np.float32)
        out[:, c::NCORES, :] = loc.reshape(cfg.B, cfg.L, cfg.DM)
    return out


if __name__ == "__main__":
    cfg = Cfg()
    B, nc = build_program(cfg)
    print("built:", nc.n_instructions(), "instructions; sbuf",
          B.sbuf_used, "/", nc.SBUF_PARTITION_SIZE_BYTES)


# version 11
# speedup vs baseline: 1.0454x; 1.0122x over previous
import math
import numpy as np
import ml_dtypes

import concourse.bass as bass
import concourse.mybir as mybir
from concourse.bass_utils import run_bass_kernel_spmd

F32 = mybir.dt.float32
BF16 = mybir.dt.bfloat16
AF = mybir.ActivationFunctionType
ALU = mybir.AluOpType
AX = mybir.AxisListType
bfnp = ml_dtypes.bfloat16

NCORES = 8


class Cfg:
    def __init__(self, dm=4096, inter=11008, nh=32, nkv=8, seq=4096, batch=2,
                 xp=None):
        self.DM, self.IN, self.NH, self.NKV, self.HD = dm, inter, nh, nkv, 128
        assert nh * 128 == dm and batch == 2
        self.B, self.S = batch, seq
        self.L = seq // NCORES
        assert self.L % 128 == 0
        self.MT = self.L // 128
        self.LT = self.B * self.L
        self.W = self.L
        self.TS = self.W // 128
        self.DMC = dm // 128
        self.NIB = inter // 128
        self.NQKV = nh + 2 * nkv
        self.GRP = nh // nkv
        self.eps = 1e-5
        self.theta = 10000.0
        self.IH0 = (self.NIB + 1) // 2
        self.XP = xp if xp is not None else min(512, dm)
        self.NPC = dm // self.XP
        self.stop = "full"
        assert self.TS * (self.XP // 512) <= 4


def _img(wt):
    kin = wt.shape[0]
    return np.ascontiguousarray(
        wt.reshape(kin // 128, 128, 128).transpose(1, 0, 2).reshape(128, kin))


def host_prepare(cfg, inputs):
    f32 = np.float32
    x = np.asarray(inputs["L0_attn_x"], f32)
    w_qkv = np.asarray(inputs["L0_attn_w_qkv"], f32)
    b_qkv = np.asarray(inputs["L0_attn_b_qkv"], f32)
    w_o = np.asarray(inputs["L0_attn_w_o"], f32)
    b_o = np.asarray(inputs["L0_attn_b_o"], f32)
    w_g = np.asarray(inputs["L0_ffn_w_gate"], f32)
    b_g = np.asarray(inputs["L0_ffn_b_gate"], f32)
    w_u = np.asarray(inputs["L0_ffn_w_up"], f32)
    b_u = np.asarray(inputs["L0_ffn_b_up"], f32)
    w_d = np.asarray(inputs["L0_ffn_w_down"], f32)
    b_d = np.asarray(inputs["L0_ffn_b_down"], f32)
    DM, NH, NKV = cfg.DM, cfg.NH, cfg.NKV

    wgu = np.empty((2 * cfg.NIB, 128, DM), f32)
    for i in range(cfg.NIB):
        wgu[2 * i] = _img(np.ascontiguousarray(w_g[i * 128:(i + 1) * 128, :].T))
        wgu[2 * i + 1] = _img(np.ascontiguousarray(w_u[i * 128:(i + 1) * 128, :].T))
    wdn = np.ascontiguousarray(w_d.T).reshape(cfg.NIB, 128, DM)
    perm = np.concatenate([np.arange(0, 128, 2), np.arange(1, 128, 2)])
    wqkv = np.empty((cfg.NQKV, 128, DM), f32)
    bqkv_cols = np.empty((128, cfg.NQKV), f32)
    for j in range(cfg.NQKV):
        rows = np.arange(j * 128, (j + 1) * 128)
        if j < NH + NKV:
            rows = rows[perm]
        wqkv[j] = _img(np.ascontiguousarray(w_qkv[rows, :].T))
        bqkv_cols[:, j] = b_qkv[rows]
    bqkv_swap = np.concatenate([bqkv_cols[64:128, :], bqkv_cols[0:64, :]], 0)
    wo = np.ascontiguousarray(w_o.T).reshape(cfg.DMC, 128, DM)

    bias_cols = np.zeros((128, 2 * cfg.NIB), f32)
    for i in range(cfg.NIB):
        bias_cols[:, 2 * i] = b_g[i * 128:(i + 1) * 128]
        bias_cols[:, 2 * i + 1] = b_u[i * 128:(i + 1) * 128]
    brow_d = np.ascontiguousarray(b_d.reshape(1, DM))
    brow_o = np.ascontiguousarray(b_o.reshape(1, DM))

    in_maps = []
    inv_freq = 1.0 / cfg.theta ** (np.arange(0, 128, 2, dtype=np.float64) / 128)
    qs = 1.0 / math.sqrt(128.0)
    for c in range(NCORES):
        x_loc = np.ascontiguousarray(x[:, c::NCORES, :].reshape(cfg.LT, DM))
        pos = np.empty(cfg.LT, np.float64)
        for b in range(cfg.B):
            pos[b * cfg.L:(b + 1) * cfg.L] = (
                np.arange(cfg.L, dtype=np.float64) * NCORES + c)
        ang = inv_freq[:, None] * pos[None, :]
        cosv, sinv = np.cos(ang), np.sin(ang)
        cos_t = np.concatenate([cosv, cosv], 0).astype(f32)
        sin_t = np.concatenate([-sinv, sinv], 0).astype(f32)
        jj = np.arange(128)[:, None]
        uu = np.arange(128)[None, :]
        maskk = np.zeros((128, NCORES * 128), f32)
        for r in range(NCORES):
            maskk[:, r * 128:(r + 1) * 128] = (jj <= uu - (1 if r > c else 0))
        in_maps.append({
            "x_loc": x_loc, "wgu": wgu, "wdn": wdn, "wqkv": wqkv, "wo": wo,
            "bias_cols": bias_cols, "bqkv_cols": bqkv_cols,
            "bqkv_swap": np.ascontiguousarray(bqkv_swap),
            "brow_d": brow_d, "brow_o": brow_o,
            "cos_q": (cos_t * qs).astype(f32), "sin_q": (sin_t * qs).astype(f32),
            "cos_k": cos_t, "sin_k": sin_t,
            "maskk": maskk.astype(bfnp),
        })
    return in_maps


class Chan:
    __slots__ = ("sem", "n", "name")

    def __init__(self, nc, name):
        self.sem = nc.alloc_semaphore(name)
        self.n = 0
        self.name = name


def FR(*marks):
    out = []
    for m in marks:
        if m is None:
            continue
        if isinstance(m, list):
            out.extend(x for x in m if x is not None and x[0] is not None and x[1] > 0)
        elif m[0] is not None and m[1] > 0:
            out.append(m)
    return out


class Emit:
    ENGINES = ("sync", "scalar", "tensor", "vector", "gpsimd")

    def __init__(self, nc):
        self.nc = nc
        self.q = {e: [] for e in self.ENGINES}
        self.wm = {}
        self.self_chan = {}
        self.debug_src = False

    def chan(self, name):
        return Chan(self.nc, name)

    def op(self, eng, fn, waits=(), inc=None, inc_by=1):
        selfc = self.self_chan.get(eng)
        waits = list(waits)
        if selfc is not None and selfc.n > 0:
            waits.append((selfc, selfc.n))
        waits = tuple((c, v) for (c, v) in waits if v > 0)
        if inc is not None:
            inc.n += inc_by
        if self.debug_src:
            import traceback
            fr = traceback.extract_stack(limit=3)[1]
            self.q[eng].append((fn, waits, inc, inc_by, f"{fr.name}:{fr.lineno}"))
        else:
            self.q[eng].append((fn, waits, inc, inc_by))

    def emit_engine(self, eng_name, engine):
        wm = self.wm
        for fn, waits, inc, inc_by, *_ in self.q[eng_name]:
            for (c, v) in waits:
                key = (eng_name, c.name)
                if wm.get(key, 0) < v:
                    engine.wait_ge(c.sem, v)
                    wm[key] = v
            inst = fn(engine)
            if inc is not None:
                inst.then_inc(inc.sem, inc_by)


class Build:
    def __init__(self, cfg):
        self.cfg = cfg
        nc = bass.Bass("TRN2", target_bir_lowering=False,
                       dynamic_dma_scratch_size=16384)
        self.nc = nc
        c = cfg
        DM, LT = c.DM, c.LT
        dt = nc.dram_tensor

        self.x_loc = dt("x_loc", [LT, DM], F32, kind="ExternalInput").ap()
        self.wgu = dt("wgu", [2 * c.NIB, 128, DM], F32, kind="ExternalInput").ap()
        self.wdn = dt("wdn", [c.NIB, 128, DM], F32, kind="ExternalInput").ap()
        self.wqkv = dt("wqkv", [c.NQKV, 128, DM], F32, kind="ExternalInput").ap()
        self.wo = dt("wo", [c.DMC, 128, DM], F32, kind="ExternalInput").ap()
        self.bias_cols = dt("bias_cols", [128, 2 * c.NIB], F32, kind="ExternalInput").ap()
        self.bqkv_cols = dt("bqkv_cols", [128, c.NQKV], F32, kind="ExternalInput").ap()
        self.bqkv_swap = dt("bqkv_swap", [128, c.NQKV], F32, kind="ExternalInput").ap()
        self.brow_d = dt("brow_d", [1, DM], F32, kind="ExternalInput").ap()
        self.brow_o = dt("brow_o", [1, DM], F32, kind="ExternalInput").ap()
        self.cos_q = dt("cos_q", [128, LT], F32, kind="ExternalInput").ap()
        self.sin_q = dt("sin_q", [128, LT], F32, kind="ExternalInput").ap()
        self.cos_k = dt("cos_k", [128, LT], F32, kind="ExternalInput").ap()
        self.sin_k = dt("sin_k", [128, LT], F32, kind="ExternalInput").ap()
        self.maskk = dt("maskk", [128, NCORES * 128], BF16, kind="ExternalInput").ap()
        self.out = dt("out_loc", [LT, DM], F32, kind="ExternalOutput").ap()
        nslot_dbg = max(c.IH0, c.NIB - c.IH0)
        self.dbg = dt("dbg", [128, nslot_dbg * c.W], BF16, kind="ExternalOutput").ap()

        self.wdn_bf = dt("wdn_bf", [c.NIB, 128, DM], BF16).ap()
        self.wo_bf = dt("wo_bf", [c.DMC, 128, DM], BF16).ap()
        self.wgu_bf = dt("wgu_bf", [2 * c.NIB, 128, DM], BF16).ap()
        self.wqkv_bf = dt("wqkv_bf", [c.NQKV, 128, DM], BF16).ap()
        self.x1 = dt("x1", [LT, DM], F32).ap()
        self.xmid = dt("xmid", [LT, DM], F32).ap()
        self.x2 = dt("x2", [LT, DM], F32).ap()
        self.kt_in = [dt(f"kt_in{b}", [c.NKV * 128, c.L], BF16).ap() for b in range(2)]
        self.v_in = [dt(f"v_in{b}", [c.L, c.NKV * 128], BF16).ap() for b in range(2)]
        self.kn_in = [dt(f"kn_in{b}", [1, 16], F32).ap() for b in range(2)]
        self.kt_g = [dt(f"kt_g{b}", [NCORES * c.NKV * 128, c.L], BF16).ap() for b in range(2)]
        self.v_g = [dt(f"v_g{b}", [NCORES * c.L, c.NKV * 128], BF16).ap() for b in range(2)]
        self.kn_g = [dt(f"kn_g{b}", [NCORES, 16], F32).ap() for b in range(2)]

        self.i_ident = nc.inline_tensor(np.eye(128, dtype=bfnp), name="i_ident").ap()
        self.i_ones_r = nc.inline_tensor(np.ones((1, 128), bfnp), name="i_ones_r").ap()
        self.i_ones_c = nc.inline_tensor(np.ones((128, 1), bfnp), name="i_ones_c").ap()
        self.i_ones_rf = nc.inline_tensor(np.ones((1, 128), np.float32), name="i_ones_rf").ap()

        self._off = (nc.sbuf_base + 63) // 64 * 64
        alloc = []

        def A(name, shape, dtype, at=None):
            nbytes = int(np.prod(shape[1:])) * (2 if dtype == BF16 else 4)
            if at is None:
                at = self._off
                self._off += (nbytes + 63) // 64 * 64
            t = nc.alloc_sbuf_tensor_at(name, list(shape), dtype, offset=at)
            alloc.append((name, at, nbytes))
            setattr(self, name[0] == "s" and name or name, t)
            return t

        self.s_biasc = A("s_biasc", [128, 2 * c.NIB], F32)
        self.s_bqkvc = A("s_bqkvc", [128, c.NQKV], F32)
        self.s_bqkvs = A("s_bqkvs", [128, c.NQKV], F32)
        self.s_maskk = A("s_maskk", [128, NCORES * 128], BF16)
        self.s_ident = A("s_ident", [128, 128], BF16)
        self.s_ones_r = A("s_ones_r", [1, 128], BF16)
        self.s_ones_c = A("s_ones_c", [128, 1], BF16)
        self.s_ones_rf = A("s_ones_rf", [1, 128], F32)
        self.s_brow = A("s_brow", [1, DM], BF16)
        self.s_stats = A("s_stats", [128, 32], F32)
        self.s_small = A("s_small", [1, 128], F32)
        self.s_negc0 = A("s_negc0", [128, 1], F32)
        self.s_drow = A("s_drow", [1, max(c.W, 16 * NCORES)], F32)
        self.s_dbc = A("s_dbc", [128, c.W], F32)
        self.s_xp = [A(f"s_xp{i}", [128, c.XP], F32) for i in range(6)]
        self.s_probs = [A(f"s_probs{i}", [128, 1024], BF16) for i in range(3)]
        assert c.XP * 2 <= 2048
        self.s_lntok = [A(f"s_lntok{i}", [128, c.XP], BF16,
                          at=[a for a in alloc if a[0] == f"s_probs{i}"][0][1])
                        for i in range(2)]
        self.s_silu = [A(f"s_silu{i}", [128, c.W], F32) for i in range(2)]
        self.s_rsc = [A(f"s_rsc{i}", [128, c.W], F32,
                        at=[a for a in alloc if a[0] == f"s_silu{i}"][0][1])
                      for i in range(2)]
        self.s_rsc.append(A("s_rsc2", [128, c.W], F32))
        self.s_kstg = A("s_kstg", [128, c.W], BF16)
        self.s_sq = A("s_sq", [128, c.W], BF16)
        self.s_vstg = A("s_vstg", [128, c.TS * c.NKV * 128], BF16)
        self.s_wd = [A(f"s_wd{i}", [128, DM], BF16) for i in range(2)]
        self.s_xlnt = A("s_xlnt", [128, c.DMC * c.W], BF16)
        xlnt_off = [a for a in alloc if a[0] == "s_xlnt"][0][1]
        self.s_att = A("s_att", [128, c.DMC * c.W], BF16, at=xlnt_off)

        nslot = max(c.IH0, c.NIB - c.IH0)
        prod_sz = ((nslot * c.W * 2 + 63) // 64) * 64
        ring_sz = 4 * DM * 2
        qt_sz = c.NH * LT * 2
        big0 = self._off
        self._off += max(prod_sz + ring_sz, qt_sz + 2 * LT * 4)
        self.s_prod = A("s_prod", [128, nslot * c.W], BF16, at=big0)
        self.s_ring = [A(f"s_ring{i}", [128, DM], BF16, at=big0 + prod_sz + i * DM * 2)
                       for i in range(4)]
        self.s_qt = A("s_qt", [128, c.NH * LT], BF16, at=big0)
        self.s_cosq = A("s_cosq", [128, LT], F32, at=big0 + qt_sz)
        self.s_sinq = A("s_sinq", [128, LT], F32, at=big0 + qt_sz + LT * 4)

        kv0 = self._off
        ksz = ((NCORES * c.L * 2 + 63) // 64) * 64
        self._off += 4 * ksz
        self.s_ksb = [A(f"s_ksb{i}", [128, NCORES * c.L], BF16, at=kv0 + i * ksz)
                      for i in range(2)]
        self.s_vsb = [A(f"s_vsb{i}", [128, NCORES * c.L], BF16, at=kv0 + (2 + i) * ksz)
                      for i in range(2)]
        assert 2 * ksz >= 2 * LT * 4
        self.s_cosk = A("s_cosk", [128, LT], F32, at=kv0)
        self.s_sink = A("s_sink", [128, LT], F32, at=kv0 + LT * 4)
        assert ksz >= 2048 * 4
        self.s_wstg = [A(f"s_wstg{i}", [128, 2048], F32, at=kv0 + (2 + i) * ksz)
                       for i in range(2)]

        cap = nc.SBUF_PARTITION_SIZE_BYTES
        assert self._off <= cap, (self._off, cap)
        self.sbuf_used = self._off
        self.alloc_table = alloc

        self.ps6 = nc.alloc_psum_tensor("ps6", [128, 3072], F32)
        self.psb = nc.alloc_psum_tensor("psb", [128, 2048], BF16)
        self.ps67 = self.psb.bitcast(F32)

        E = self.E = Emit(nc)
        self.c_const = E.chan("c_const")
        self.c_wfree = E.chan("c_wfree")
        self.c_mm = E.chan("c_mm")
        self.c_act = E.chan("c_act")
        self.c_dve = E.chan("c_dve")
        self.c_lnt = E.chan("c_lnt")
        self.c_cc = [[E.chan(f"c_cc{b}_{i}") for i in range(3)] for b in range(2)]
        self.c_gdma = E.chan("c_gdma")
        self.c_wcg = [E.chan(f"c_wcg{i}") for i in range(4)]
        self.c_ring = [E.chan(f"c_ring{i}") for i in range(4)]
        self.c_wdl = [E.chan(f"c_wdl{i}") for i in range(2)]
        self.c_xp = [E.chan(f"c_xp{i}") for i in range(6)]
        self.c_ksb = [E.chan(f"c_ksb{i}") for i in range(2)]
        self.c_vsb = [E.chan(f"c_vsb{i}") for i in range(2)]
        self.c_qv = [E.chan(f"c_qv{i}") for i in range(2)]
        self.c_kstg = E.chan("c_kstg")
        self.c_vstg = E.chan("c_vstg")
        self.c_wgub = E.chan("c_wgub")
        self.c_stg = [E.chan(f"c_stg{i}") for i in range(2)]
        self.c_kn = E.chan("c_kn")
        self.c_attld = E.chan("c_attld")
        E.self_chan = {"scalar": self.c_act, "vector": self.c_dve}
        self.c_kv = E.chan("c_kv")

        self.wcast_mark = {}
        self.wgu_wb = {}
        self.wqkv_wb = {}
        self.ffn_call = 0
        self.wstg_free = [None, None]
        self.wstg_par = 0
        self.stage_done = None
        self.pending_casts = []
        self.cast_rot = 0
        self.xp_free = [None] * 6
        self.lntok_free = [None] * 2
        self.silu_free = [None] * 2
        self.rsc_free = [None] * 3
        self.probs_free = [None] * 3
        self.ksb_free = [None] * 2
        self.vsb_free = [None] * 2
        self.pv_free = {0: None, 1: None}
        self.head_par = 0
        self.norm_q = []
        self.kv_pre = set()
        self.ring_free = [None] * 4
        self.wd_free = [None] * 2
        self.bank_free = {}
        self.psb_free = None
        self.kstg_free = None
        self.vstg_free = None
        self.sq_free = None
        self.stats_free = None
        self.small_free = None
        self.drow_free = None
        self.dbc_free = None
        self.negc0_free = None
        self.brow_ready = None
        self.big_free = None
        self.xlnt_write_ok = None
        self.att_write_ok = None
        self.cosk_write_ok = None
        self.ln_ready = 0
        self.qt_done = {}
        self.xready = {}


def mark(ch):
    return (ch, ch.n)


def emit_consts(B):
    E = B.E
    loads = [
        (B.s_biasc[:, :], B.bias_cols), (B.s_bqkvc[:, :], B.bqkv_cols),
        (B.s_bqkvs[:, :], B.bqkv_swap),
        (B.s_maskk[:, :], B.maskk), (B.s_ident[:, :], B.i_ident),
        (B.s_ones_r[:, :], B.i_ones_r), (B.s_ones_c[:, :], B.i_ones_c),
        (B.s_ones_rf[:, :], B.i_ones_rf),
        (B.s_cosk[:, :], B.cos_k), (B.s_sink[:, :], B.sin_k),
    ]
    for d, s in loads:
        E.op("sync", (lambda d, s: lambda eng: eng.dma_start(out=d, in_=s))(d, s),
             inc=B.c_const, inc_by=16)
    B.const_n = B.c_const.n
    E.op("gpsimd", lambda eng: eng.dma_start(out=B.s_brow[:, :], in_=B.brow_d),
         inc=B.c_gdma, inc_by=16)
    B.brow_ready = (B.c_gdma, B.c_gdma.n)


def load_brow(B, src_row):
    E = B.E
    E.op("gpsimd",
         (lambda s: lambda eng: eng.dma_start(out=B.s_brow[:, :], in_=s))(src_row),
         waits=[(B.c_mm, B.c_mm.n)], inc=B.c_gdma, inc_by=16)
    B.brow_ready = (B.c_gdma, B.c_gdma.n)


def emit_weight_cast(B, kind_jobs, pace=None):
    E, cfg = B.E, B.cfg
    GSZ = 8
    jobs = kind_jobs
    for gi in range(0, len(jobs), GSZ):
        grp = jobs[gi:gi + GSZ]
        sidx = (gi // GSZ) % 4
        ch = B.c_wcg[sidx]
        throttle = [(ch, ch.n)] if ch.n > 0 else []
        if pace is not None:
            throttle = throttle + FR(pace(gi))
        for n, (kind, i, dst, src) in enumerate(grp):
            E.op("gpsimd",
                 (lambda d, s: lambda eng: eng.dma_start(out=d, in_=s))(dst, src),
                 waits=(throttle if n == 0 else []), inc=ch, inc_by=16)
        for kind, i, dst, src in grp:
            B.wcast_mark[(kind, i)] = (ch, ch.n)


def emit_ln(B, src, half, src_ready):
    E, cfg = B.E, B.cfg
    st = B.s_stats
    inv_n = 1.0 / cfg.DM
    TS, NPC, XP = cfg.TS, cfg.NPC, cfg.XP
    last_sq_add = None

    for t in range(TS):
        row0 = half * cfg.W + t * 128
        for p in range(NPC):
            slot = 4 + ((t * NPC + p) % 2)
            buf = B.s_xp[slot]
            cx = B.c_xp[slot]
            E.op("sync",
                 (lambda bb, rr, pp: lambda eng: eng.dma_start(
                     out=bb[:, :], in_=src[rr:rr + 128, pp * XP:(pp + 1) * XP]))(buf, row0, p),
                 waits=FR(src_ready, B.xp_free[slot]), inc=cx, inc_by=16)
            ldn = cx.n
            E.op("vector",
                 (lambda bb, tt, pp: lambda eng: eng.reduce_sum(
                     st[:, (tt if pp == 0 else 16):(tt if pp == 0 else 16) + 1],
                     bb[:, :], axis=AX.X))(buf, t, p),
                 waits=[(cx, ldn)] + FR(B.stats_free), inc=B.c_dve)
            if p > 0:
                E.op("vector",
                     (lambda tt: lambda eng: eng.tensor_add(
                         st[:, tt:tt + 1], st[:, tt:tt + 1], st[:, 16:17]))(t),
                     inc=B.c_dve)
            sl = (t * NPC + p) % 2
            E.op("scalar",
                 (lambda bb, tt, pp, sl: lambda eng: eng.activation(
                     B.s_lntok[sl][:, :], bb[:, :], AF.Square,
                     accum_out=st[:, (4 + tt if pp == 0 else 17):
                                  (4 + tt if pp == 0 else 17) + 1]))(buf, t, p, sl),
                 waits=[(cx, ldn)] + FR(B.lntok_free[sl], last_sq_add,
                                        B.stats_free),
                 inc=B.c_act)
            sq_n = B.c_act.n
            if p > 0:
                E.op("vector",
                     (lambda tt: lambda eng: eng.tensor_add(
                         st[:, 4 + tt:5 + tt], st[:, 4 + tt:5 + tt], st[:, 17:18]))(t),
                     waits=[(B.c_act, sq_n)], inc=B.c_dve)
                last_sq_add = mark(B.c_dve)
            B.xp_free[slot] = [mark(B.c_act), mark(B.c_dve)]
        E.op("vector",
             (lambda tt: lambda eng: eng.tensor_scalar(
                 st[:, 8 + tt:9 + tt], st[:, tt:tt + 1], inv_n, None, ALU.mult))(t),
             waits=[(B.c_act, B.c_act.n)], inc=B.c_dve)
        E.op("vector",
             (lambda tt: lambda eng: eng.tensor_scalar(
                 st[:, 16:17], st[:, 8 + tt:9 + tt], st[:, 8 + tt:9 + tt],
                 cfg.eps, ALU.mult, ALU.subtract))(t),
             inc=B.c_dve)
        E.op("vector",
             (lambda tt: lambda eng: eng.scalar_tensor_tensor(
                 st[:, 12 + tt:13 + tt], st[:, 4 + tt:5 + tt], inv_n,
                 st[:, 16:17], ALU.mult, ALU.subtract))(t),
             inc=B.c_dve)
    veps_done = B.c_dve.n

    E.op("scalar",
         lambda eng: eng.activation(st[:, 18:18 + TS], st[:, 12:12 + TS], AF.Ln),
         waits=[(B.c_dve, veps_done)], inc=B.c_act)
    E.op("vector",
         lambda eng: eng.tensor_scalar(st[:, 18:18 + TS], st[:, 18:18 + TS],
                                       -0.5, None, ALU.mult),
         waits=[(B.c_act, B.c_act.n)], inc=B.c_dve)
    E.op("scalar",
         lambda eng: eng.activation(st[:, 18:18 + TS], st[:, 18:18 + TS], AF.Exp),
         waits=[(B.c_dve, B.c_dve.n)], inc=B.c_act)
    rstd_done = B.c_act.n
    E.op("vector",
         lambda eng: eng.tensor_mul(st[:, 24:24 + TS], st[:, 8:8 + TS],
                                    st[:, 18:18 + TS]),
         waits=[(B.c_act, rstd_done)], inc=B.c_dve)
    E.op("vector",
         lambda eng: eng.tensor_scalar(st[:, 24:24 + TS], st[:, 24:24 + TS],
                                       -1.0, None, ALU.mult),
         inc=B.c_dve)
    negmr_done = B.c_dve.n

    for t in range(TS):
        row0 = half * cfg.W + t * 128
        for p in range(NPC):
            slot = 4 + ((t * NPC + p) % 2)
            buf = B.s_xp[slot]
            cx = B.c_xp[slot]
            E.op("sync",
                 (lambda bb, rr, pp: lambda eng: eng.dma_start(
                     out=bb[:, :], in_=src[rr:rr + 128, pp * XP:(pp + 1) * XP]))(buf, row0, p),
                 waits=FR(B.xp_free[slot]), inc=cx, inc_by=16)
            ldn = cx.n
            sl = (t * NPC + p) % 2
            lnt = B.s_lntok[sl]
            E.op("vector",
                 (lambda bb, oo, tt: lambda eng: eng.tensor_scalar(
                     oo[:, :], bb[:, :], st[:, 18 + tt:19 + tt],
                     st[:, 24 + tt:25 + tt], ALU.mult, ALU.add))(buf, lnt, t),
                 waits=[(cx, ldn), (B.c_dve, negmr_done),
                        (B.c_act, rstd_done)] + FR(B.lntok_free[sl]),
                 inc=B.c_dve)
            ap_done = B.c_dve.n
            B.xp_free[slot] = mark(B.c_dve)
            nch = XP // 128
            for g in range(nch):
                E.op("tensor",
                     (lambda gg, ii: lambda eng: eng.transpose(
                         B.psb[:, gg * 128:(gg + 1) * 128], ii, B.s_ident[:, :]))(
                         g, lnt[:, g * 128:(g + 1) * 128]),
                     waits=[(B.c_dve, ap_done)] + FR(B.psb_free),
                     inc=(B.c_mm if g == nch - 1 else None))
            tr_done = B.c_mm.n
            B.lntok_free[sl] = mark(B.c_mm)

            def evac(eng, pp=p, nch=nch, tt=t):
                dst = B.s_xlnt[:, :].rearrange("p (c w) -> p c w", w=cfg.W)[
                    :, pp * nch:(pp + 1) * nch, tt * 128:tt * 128 + 128]
                srcap = B.psb[:, 0:nch * 128].rearrange("p (c u) -> p c u", u=128)
                return eng.tensor_copy(dst, srcap)
            E.op("vector", evac,
                 waits=[(B.c_mm, tr_done)] + FR(B.xlnt_write_ok), inc=B.c_lnt)
            B.psb_free = mark(B.c_lnt)
    B.ln_ready = B.c_lnt.n
    B.stats_free = mark(B.c_act)


def emit_ffn_half(B, half, src_res, dst, src_ready):
    E, cfg = B.E, B.cfg
    xl = B.s_xlnt
    Wd = cfg.W
    ln_need = B.ln_ready
    groups = [list(range(0, cfg.IH0)), list(range(cfg.IH0, cfg.NIB))]
    groups = [g for g in groups if g]
    occ = cfg.XP
    n_pass = cfg.DM // occ
    mid_mark = {}
    dst_marks = []

    for ih, blocks in enumerate(groups):
        last_ih = ih == len(groups) - 1
        mode = ("f32", "f32wb", "bf16", "bf16")[min(B.ffn_call, 3)]
        for slot, i in enumerate(blocks):
            mmk = [0, 0]
            for gu in range(2):
                ri = (2 * slot + gu) % 4
                rb = B.s_ring[ri]
                idx = 2 * i + gu
                if mode != "bf16":
                    rcm = [None, None]
                    for hc in range(2):
                        sl = B.wstg_par
                        B.wstg_par ^= 1
                        stb = B.s_wstg[sl]
                        cs = B.c_stg[sl]
                        E.op("sync",
                             (lambda stb, idx, hc: lambda eng: eng.dma_start(
                                 out=stb[:, :],
                                 in_=B.wgu[idx][:, hc * 2048:(hc + 1) * 2048]))(
                                 stb, idx, hc),
                             waits=FR(B.wstg_free[sl]), inc=cs, inc_by=16)
                        ldn = cs.n
                        cast_w = [(cs, ldn)] + FR(B.ring_free[ri], B.big_free)
                        if hc == 0:
                            E.op("vector",
                                 (lambda rb, stb, hc: lambda eng: eng.tensor_copy(
                                     rb[:, hc * 2048:(hc + 1) * 2048],
                                     stb[:, :]))(rb, stb, hc),
                                 waits=cast_w, inc=B.c_dve)
                            rcm[hc] = (B.c_dve, B.c_dve.n)
                        else:
                            E.op("scalar",
                                 (lambda rb, stb, hc: lambda eng: eng.activation(
                                     rb[:, hc * 2048:(hc + 1) * 2048],
                                     stb[:, :], AF.Copy))(rb, stb, hc),
                                 waits=cast_w, inc=B.c_act)
                            rcm[hc] = (B.c_act, B.c_act.n)
                        B.wstg_free[sl] = rcm[hc]
                    if mode == "f32wb":
                        E.op("sync",
                             (lambda rb, idx: lambda eng: eng.dma_start(
                                 out=B.wgu_bf[idx], in_=rb[:, :]))(rb, idx),
                             waits=FR(rcm[0], rcm[1]),
                             inc=B.c_wgub, inc_by=16)
                        B.wgu_wb[idx] = (B.c_wgub, B.c_wgub.n)
                        wb_mark = (B.c_wgub, B.c_wgub.n)
                    else:
                        wb_mark = None
                    wlo, whi = rcm[0], rcm[1]
                else:
                    E.op("sync",
                         (lambda rb, idx: lambda eng: eng.dma_start(
                             out=rb[:, :], in_=B.wgu_bf[idx]))(rb, idx),
                         waits=FR(B.ring_free[ri], B.big_free,
                                  B.wgu_wb[idx]),
                         inc=B.c_ring[ri], inc_by=16)
                    wlo = whi = (B.c_ring[ri], B.c_ring[ri].n)
                    wb_mark = None
                bank = gu
                for k in range(cfg.DMC):
                    E.op("tensor",
                         (lambda rb, kk, bank, first, last: lambda eng: eng.matmul(
                             B.ps6[:, bank * 512:bank * 512 + Wd],
                             rb[:, kk * 128:(kk + 1) * 128],
                             xl[:, kk * Wd:(kk + 1) * Wd],
                             start=first, stop=last))(rb, k, bank, k == 0,
                                                      k == cfg.DMC - 1),
                         waits=([wlo if k < 16 else whi, (B.c_lnt, ln_need)]
                                + (FR(B.bank_free.get(bank)) if k == 0 else [])),
                         inc=(B.c_mm if k == cfg.DMC - 1 else None))
                mmk[gu] = B.c_mm.n
                B.ring_free[ri] = FR(mark(B.c_mm), wb_mark)
            if B.pending_casts:
                kind, ci, cdst, csrc = B.pending_casts.pop(0)
                ch = B.c_wcg[B.cast_rot % 4]
                B.cast_rot += 1
                thr = ([(ch, ch.n)] if ch.n > 0 else []) + [(B.c_mm, mmk[0])]
                E.op("gpsimd",
                     (lambda d, s: lambda eng: eng.dma_start(out=d, in_=s))(
                         cdst, csrc),
                     waits=thr, inc=ch, inc_by=16)
                B.wcast_mark[(kind, ci)] = (ch, ch.n)
            ssl = slot % 2
            E.op("scalar",
                 (lambda ssl, i: lambda eng: eng.activation(
                     B.s_silu[ssl][:, :], B.ps6[:, 0:Wd], AF.Silu,
                     bias=B.s_biasc[:, 2 * i:2 * i + 1]))(ssl, i),
                 waits=[(B.c_mm, mmk[0])] + FR(B.silu_free[ssl]), inc=B.c_act)
            B.bank_free[0] = mark(B.c_act)
            silu_done = B.c_act.n
            E.op("vector",
                 (lambda ssl, i, slot: lambda eng: eng.scalar_tensor_tensor(
                     B.s_prod[:, slot * Wd:(slot + 1) * Wd],
                     B.ps6[:, 512:512 + Wd],
                     B.s_biasc[:, 2 * i + 1:2 * i + 2], B.s_silu[ssl][:, :],
                     ALU.add, ALU.mult))(ssl, i, slot),
                 waits=[(B.c_mm, mmk[1]), (B.c_act, silu_done)]
                 + FR(getattr(B, "prod_free", None), B.big_free), inc=B.c_dve)
            B.bank_free[1] = mark(B.c_dve)
            B.silu_free[ssl] = mark(B.c_dve)
        prod_done = B.c_dve.n
        if cfg.stop == "prod" and not getattr(B, "_dumped", False):
            B._dumped = True
            E.op("sync", lambda eng: eng.dma_start(out=B.dbg, in_=B.s_prod[:, :]),
                 waits=[(B.c_dve, prod_done)], inc=B.c_xp[0], inc_by=16)
            return
        B.xlnt_read_done = mark(B.c_mm)

        ts_mm_done = {}
        BB = max(1, min(8, cfg.DM // occ))
        for p in range(n_pass):
            for b0 in range(0, len(blocks), BB):
                bgrp = blocks[b0:b0 + BB]
                nb = len(bgrp)
                buf = (b0 // BB) % 2
                wdb = B.s_wd[buf]
                i0 = bgrp[0]
                E.op("sync",
                     (lambda wdb, i0, nb, p: lambda eng: eng.dma_start(
                         out=wdb[:, 0:nb * occ].rearrange(
                             "p (b u) -> p b u", u=occ),
                         in_=B.wdn_bf[i0:i0 + nb, :, p * occ:(p + 1) * occ]
                         .rearrange("b p u -> p b u")))(wdb, i0, nb, p),
                     waits=FR(B.wcast_mark[("dn", bgrp[-1])]) + FR(B.wd_free[buf]),
                     inc=B.c_wdl[buf], inc_by=16)
                wldn = (B.c_wdl[buf], B.c_wdl[buf].n)
                for bn, i in enumerate(bgrp):
                    bi = b0 + bn
                    last_bi = bi == len(blocks) - 1
                    for ts in range(cfg.TS):
                        base = ts * (occ // 512)
                        for oc2 in range(occ // 512):
                            tail = (ts == cfg.TS - 1 and oc2 == occ // 512 - 1
                                    and bn == nb - 1) \
                                or (last_bi and oc2 == occ // 512 - 1)
                            E.op("tensor",
                                 (lambda wdb, slot, bn, ts, oc2, base, first, stp:
                                  lambda eng: eng.matmul(
                                      B.ps6[:, (base + oc2) * 512:(base + oc2 + 1) * 512],
                                      B.s_prod[:, slot * Wd + ts * 128:
                                               slot * Wd + ts * 128 + 128],
                                      wdb[:, (bn * (occ // 512) + oc2) * 512:
                                          (bn * (occ // 512) + oc2 + 1) * 512],
                                      start=first, stop=stp))(wdb, bi, bn, ts, oc2,
                                                              base, bi == 0,
                                                              last_bi and not last_ih),
                                 waits=([wldn, (B.c_dve, prod_done)]
                                        + (FR(B.bank_free.get(base + oc2)) if bi == 0 else [])),
                                 inc=(B.c_mm if tail else None))
                        if last_bi:
                            ts_mm_done[ts] = B.c_mm.n
                B.wd_free[buf] = mark(B.c_mm)
            for ts in range(cfg.TS):
                base = ts * (occ // 512)
                if last_ih:
                    for oc2 in range(occ // 512):
                        E.op("tensor",
                             (lambda p, oc2, base: lambda eng: eng.matmul(
                                 B.ps6[:, (base + oc2) * 512:(base + oc2 + 1) * 512],
                                 B.s_ones_r[:, :],
                                 B.s_brow[:, p * occ + oc2 * 512:
                                          p * occ + (oc2 + 1) * 512],
                                 start=False, stop=True))(p, oc2, base),
                             waits=FR(B.brow_ready),
                             inc=(B.c_mm if oc2 == occ // 512 - 1 else None))
                    mm_done = B.c_mm.n
                else:
                    mm_done = ts_mm_done[ts]
                xslot = ts % 4
                xb = B.s_xp[xslot]
                row0 = half * Wd + ts * 128
                sx = src_res if ih == 0 else B.xmid
                dx = dst if last_ih else B.xmid
                cx = B.c_xp[xslot]
                E.op("scalar",
                     (lambda xb, row0, p, sx: lambda eng: eng.dma_start(
                         out=xb[:, 0:occ], in_=sx[row0:row0 + 128,
                                                  p * occ:(p + 1) * occ]))(xb, row0, p, sx),
                     waits=FR(B.xp_free[xslot],
                              src_ready if ih == 0 else mid_mark.get((p, ts))),
                     inc=cx, inc_by=16)
                ldn = cx.n
                E.op("vector",
                     (lambda xb, base: lambda eng: eng.scalar_tensor_tensor(
                         xb[:, 0:occ], B.ps6[:, base * 512:base * 512 + occ],
                         0.5, xb[:, 0:occ], ALU.mult, ALU.add))(xb, base),
                     waits=[(B.c_mm, mm_done), (cx, ldn)], inc=B.c_dve)
                for oc2 in range(occ // 512):
                    B.bank_free[base + oc2] = mark(B.c_dve)
                stt = B.c_dve.n
                E.op("scalar",
                     (lambda xb, row0, p, dx: lambda eng: eng.dma_start(
                         out=dx[row0:row0 + 128, p * occ:(p + 1) * occ],
                         in_=xb[:, 0:occ]))(xb, row0, p, dx),
                     waits=[(B.c_dve, stt)], inc=cx, inc_by=16)
                B.xp_free[xslot] = mark(cx)
                dst_marks.append(mark(cx))
                if not last_ih:
                    mid_mark[(p, ts)] = mark(cx)
        B.prod_free = mark(B.c_mm)
    B.big_free = mark(B.c_mm)
    B.dst_done = dst_marks
    if B.ffn_call == 1:
        B.stage_done = FR(B.wstg_free[0], B.wstg_free[1])
    B.ffn_call += 1


def emit_qkv_half(B, half):
    E, cfg = B.E, B.cfg
    Wd = cfg.W
    xl = B.s_xlnt
    ln_need = B.ln_ready
    NH, NKV = cfg.NH, cfg.NKV
    if half == 0:
        E.op("sync", lambda eng: eng.dma_start(out=B.s_cosq[:, :], in_=B.cos_q),
             waits=FR(B.big_free), inc=B.c_const, inc_by=16)
        E.op("sync", lambda eng: eng.dma_start(out=B.s_sinq[:, :], in_=B.sin_q),
             waits=FR(B.big_free), inc=B.c_const, inc_by=16)
        B.ropeq_n = B.c_const.n
    E.op("vector", lambda eng: eng.memset(B.s_small[0:1, 64:80], 0.0),
         waits=FR(B.small_free), inc=B.c_dve)

    for j in range(cfg.NQKV):
        wdb = B.s_wd[j % 2]
        if half == 0:
            E.op("gpsimd",
                 (lambda wdb, j: lambda eng: eng.dma_start(
                     out=wdb[:, :], in_=B.wqkv[j]))(wdb, j),
                 waits=FR(B.wd_free[j % 2]),
                 inc=B.c_qv[j % 2], inc_by=16)
            wldn = (B.c_qv[j % 2], B.c_qv[j % 2].n)
            E.op("sync",
                 (lambda wdb, j: lambda eng: eng.dma_start(
                     out=B.wqkv_bf[j], in_=wdb[:, :]))(wdb, j),
                 waits=[wldn], inc=B.c_wgub, inc_by=16)
            B.wqkv_wb[j] = (B.c_wgub, B.c_wgub.n)
            wb_mark = (B.c_wgub, B.c_wgub.n)
        else:
            E.op("sync",
                 (lambda wdb, j: lambda eng: eng.dma_start(
                     out=wdb[:, :], in_=B.wqkv_bf[j]))(wdb, j),
                 waits=FR(B.wd_free[j % 2], B.wqkv_wb[j]),
                 inc=B.c_qv[j % 2], inc_by=16)
            wldn = (B.c_qv[j % 2], B.c_qv[j % 2].n)
            wb_mark = None
        bank = j % 3
        for k in range(cfg.DMC):
            E.op("tensor",
                 (lambda wdb, kk, bank, first, last: lambda eng: eng.matmul(
                     B.ps6[:, bank * 512:bank * 512 + Wd],
                     wdb[:, kk * 128:(kk + 1) * 128],
                     xl[:, kk * Wd:(kk + 1) * Wd],
                     start=first, stop=last))(wdb, k, bank, k == 0, k == cfg.DMC - 1),
                 waits=([wldn, (B.c_lnt, ln_need)]
                        + (FR(B.bank_free.get(bank)) if k == 0 else [])),
                 inc=(B.c_mm if k == cfg.DMC - 1 else None))
        mmn = B.c_mm.n
        B.wd_free[j % 2] = FR(mark(B.c_mm), wb_mark)
        ps = B.ps6[:, bank * 512:bank * 512 + Wd]
        bcol = B.s_bqkvc[:, j:j + 1]

        if j < NH + NKV:
            if j < NH:
                cosT, sinT = B.s_cosq, B.s_sinq
                tabw = [(B.c_const, B.ropeq_n)]
            else:
                cosT, sinT = B.s_cosk, B.s_sink
                tabw = [(B.c_const, B.const_n)] + FR(getattr(B, "costab_alive", None))
            cs = cosT[:, half * Wd:(half + 1) * Wd]
            sn = sinT[:, half * Wd:(half + 1) * Wd]
            bswp = B.s_bqkvs[:, j:j + 1]
            E.op("vector",
                 (lambda ps, bcol, cs: lambda eng: eng.scalar_tensor_tensor(
                     B.s_rsc[1][:, :], ps, bcol, cs, ALU.add, ALU.mult))(ps, bcol, cs),
                 waits=[(B.c_mm, mmn)] + FR(B.rsc_free[1]) + tabw, inc=B.c_dve)
            m1_done = B.c_dve.n
            E.op("vector",
                 (lambda ps, bswp, sn: lambda eng: (
                     eng.scalar_tensor_tensor(
                         B.s_rsc[2][0:64, :], ps[64:128, :], bswp[0:64, :],
                         sn[0:64, :], ALU.add, ALU.mult),
                     eng.scalar_tensor_tensor(
                         B.s_rsc[2][64:128, :], ps[0:64, :], bswp[64:128, :],
                         sn[64:128, :], ALU.add, ALU.mult))[-1])(ps, bswp, sn),
                 waits=[(B.c_mm, mmn)] + FR(B.rsc_free[2]) + tabw, inc=B.c_dve)
            m2_done = B.c_dve.n
            B.bank_free[bank] = mark(B.c_dve)
            if j < NH:
                dstap = B.s_qt[:, j * cfg.LT + half * Wd:j * cfg.LT + (half + 1) * Wd]
                extra = FR(B.big_free)
            else:
                dstap = B.s_kstg[:, :]
                extra = FR(B.kstg_free)
            E.op("vector",
                 (lambda dstap: lambda eng: eng.tensor_add(
                     dstap, B.s_rsc[1][:, :], B.s_rsc[2][:, :]))(dstap),
                 waits=[(B.c_dve, m2_done)] + extra, inc=B.c_dve)
            rot_done = B.c_dve.n
            B.rsc_free[1] = mark(B.c_dve)
            B.rsc_free[2] = mark(B.c_dve)
            E.op("scalar",
                 (lambda dstap: lambda eng: eng.activation(
                     B.s_sq[:, :], dstap, AF.Square))(dstap),
                 waits=[(B.c_dve, rot_done)] + FR(B.sq_free), inc=B.c_act)
            sq_done = B.c_act.n
            E.op("tensor",
                 lambda eng: eng.matmul(B.ps6[0:1, 1536:1536 + Wd],
                                        B.s_ones_c[:, :], B.s_sq[:, :],
                                        start=True, stop=True),
                 waits=[(B.c_act, sq_done)] + FR(B.bank_free.get(3)), inc=B.c_mm)
            nrm_mm = B.c_mm.n
            B.sq_free = mark(B.c_mm)
            col = half * NH + j if j < NH else 64 + (j - NH)
            E.op("vector",
                 (lambda col: lambda eng: eng.reduce_max(
                     B.s_small[0:1, col:col + 1], B.ps6[0:1, 1536:1536 + Wd],
                     axis=AX.X))(col),
                 waits=[(B.c_mm, nrm_mm)], inc=B.c_dve)
            B.bank_free[3] = mark(B.c_dve)
            if j >= NH:
                E.op("sync",
                     (lambda j, half: lambda eng: eng.dma_start(
                         out=B.kt_in[half][(j - NH) * 128:(j - NH + 1) * 128, :],
                         in_=B.s_kstg[:, :]))(j, half),
                     waits=[(B.c_dve, rot_done)], inc=B.c_kstg, inc_by=16)
                B.kstg_free = [mark(B.c_kstg), mark(B.c_act)]
        else:
            cidx = j - NH - NKV
            E.op("vector",
                 (lambda ps, bcol: lambda eng: eng.tensor_scalar(
                     B.s_kstg[:, :], ps, bcol, None, ALU.add))(ps, bcol),
                 waits=[(B.c_mm, mmn)] + FR(B.kstg_free), inc=B.c_dve)
            ev_done = B.c_dve.n
            B.bank_free[bank] = mark(B.c_dve)
            for ts in range(cfg.TS):
                E.op("tensor",
                     (lambda ts: lambda eng: eng.transpose(
                         B.psb[:, ts * 128:(ts + 1) * 128],
                         B.s_kstg[:, ts * 128:(ts + 1) * 128],
                         B.s_ident[:, :]))(ts),
                     waits=[(B.c_dve, ev_done)] + FR(B.psb_free),
                     inc=(B.c_mm if ts == cfg.TS - 1 else None))
            tr_done = B.c_mm.n
            B.kstg_free = mark(B.c_mm)

            def evacv(eng, cidx=cidx):
                dstv = B.s_vstg[:, :].rearrange(
                    "p (t c u) -> p t c u", c=cfg.NKV, u=128)[:, :, cidx, :]
                srcv = B.psb[:, 0:cfg.TS * 128].rearrange("p (t u) -> p t u", u=128)
                return eng.tensor_copy(dstv, srcv)
            E.op("vector", evacv,
                 waits=[(B.c_mm, tr_done)] + FR(B.vstg_free if cidx == 0 else None),
                 inc=B.c_dve)
            B.psb_free = mark(B.c_dve)
            if cidx == cfg.NKV - 1:
                vdone = B.c_dve.n
                for ts in range(cfg.TS):
                    E.op("sync",
                         (lambda ts, half: lambda eng: eng.dma_start(
                             out=B.v_in[half][ts * 128:(ts + 1) * 128, :],
                             in_=B.s_vstg[:, ts * cfg.NKV * 128:
                                          (ts + 1) * cfg.NKV * 128]))(ts, half),
                         waits=[(B.c_dve, vdone)], inc=B.c_vstg, inc_by=16)
                B.vstg_free = mark(B.c_vstg)
    E.op("sync",
         (lambda half: lambda eng: eng.dma_start(
             out=B.kn_in[half][:, :], in_=B.s_small[0:1, 64:80]))(half),
         waits=[(B.c_dve, B.c_dve.n)], inc=B.c_kn, inc_by=16)
    B.small_free = mark(B.c_kn)
    B.qkv_ctrb = [mark(B.c_kstg), mark(B.c_vstg), mark(B.c_kn)]
    B.qt_done[half] = mark(B.c_dve)
    B.costab_alive = None
    if half == 1:
        B.cosk_write_ok = mark(B.c_dve)
        B.xlnt_write_ok_att = mark(B.c_mm)


def emit_gather(B, half):
    marks = B.qkv_ctrb
    pairs = [(B.kt_in[half], B.kt_g[half]), (B.v_in[half], B.v_g[half]),
             (B.kn_in[half], B.kn_g[half])]
    for i, (src_t, dst_t) in enumerate(pairs):
        B.E.op("gpsimd",
               (lambda s, d: lambda eng: eng.collective_compute(
                   "AllGather", ALU.bypass,
                   replica_groups=[list(range(NCORES))],
                   ins=[s[:, :]], outs=[d[:, :]]))(src_t, dst_t),
               waits=FR(marks[i]), inc=B.c_cc[half][i])


def emit_kv_prefetch(B, half, g, want_k=True, want_v=True):
    E, cfg = B.E, B.cfg
    L, TS = cfg.L, cfg.TS
    buf = g % 2
    ksb, vsb = B.s_ksb[buf], B.s_vsb[buf]
    if want_k:
        kex = (FR(B.ksb_free[buf])
               + (FR(B.cosk_write_ok) if buf == 0 else []))
        for r in range(NCORES):
            E.op("gpsimd",
                 (lambda half, g, r, ksb: lambda eng: eng.dma_start(
                     out=ksb[:, r * L:(r + 1) * L],
                     in_=B.kt_g[half][r * cfg.NKV * 128 + g * 128:
                                      r * cfg.NKV * 128 + (g + 1) * 128, :]))(
                     half, g, r, ksb),
                 waits=[(B.c_cc[half][0], 1)] + kex,
                 inc=B.c_ksb[buf], inc_by=16)
        B.kv_pre.add((half, g, 'k'))
    if want_v:
        vex = FR(B.vsb_free[buf], B.stage_done)
        for r in range(NCORES):
            for ts in range(TS):
                E.op("gpsimd",
                     (lambda half, g, r, ts, vsb: lambda eng: eng.dma_start(
                         out=vsb[:, (r * TS + ts) * 128:(r * TS + ts + 1) * 128],
                         in_=B.v_g[half][r * L + ts * 128:r * L + (ts + 1) * 128,
                                         g * 128:(g + 1) * 128]))(half, g, r, ts, vsb),
                     waits=[(B.c_cc[half][1], 1)] + vex,
                     inc=B.c_vsb[buf], inc_by=16)
        B.kv_pre.add((half, g, 'v'))


def emit_attention_half(B, half):
    E, cfg = B.E, B.cfg
    L, MT, TS, Wd = cfg.L, cfg.MT, cfg.TS, cfg.W
    E.op("sync",
         (lambda half: lambda eng: eng.dma_start(
             out=B.s_drow[0:1, 0:16 * NCORES],
             in_=B.kn_g[half][:, :].rearrange("(o r) c -> o (r c)", o=1)))(half),
         waits=[(B.c_cc[half][2], 1)] + FR(B.drow_free), inc=B.c_attld, inc_by=16)
    knl = B.c_attld.n
    E.op("vector",
         (lambda half: lambda eng: (
             eng.reduce_max(B.s_small[0:1, 90:91], B.s_drow[0:1, 0:16 * NCORES],
                            axis=AX.X),
             eng.reduce_max(B.s_small[0:1, 91:92],
                            B.s_small[0:1, half * cfg.NH:half * cfg.NH + cfg.NH],
                            axis=AX.X))[-1])(half),
         waits=[(B.c_attld, knl)] + [B.qt_done[half]], inc=B.c_dve)
    mx = B.c_dve.n
    E.op("scalar",
         lambda eng: (eng.activation(B.s_small[0:1, 92:93], B.s_small[0:1, 90:91], AF.Ln),
                      eng.activation(B.s_small[0:1, 93:94], B.s_small[0:1, 91:92], AF.Ln))[-1],
         waits=[(B.c_dve, mx)], inc=B.c_act)
    lns = B.c_act.n
    E.op("vector",
         lambda eng: eng.tensor_add(B.s_small[0:1, 94:95], B.s_small[0:1, 92:93],
                                    B.s_small[0:1, 93:94]),
         waits=[(B.c_act, lns)], inc=B.c_dve)
    E.op("vector",
         lambda eng: eng.tensor_scalar(B.s_small[0:1, 94:95], B.s_small[0:1, 94:95],
                                       0.5, None, ALU.mult),
         inc=B.c_dve)
    addn = B.c_dve.n
    E.op("scalar",
         lambda eng: eng.activation(B.s_small[0:1, 95:96], B.s_small[0:1, 94:95],
                                    AF.Exp),
         waits=[(B.c_dve, addn)], inc=B.c_act)
    bn = B.c_act.n
    E.op("vector",
         lambda eng: eng.tensor_scalar(B.s_small[0:1, 96:97], B.s_small[0:1, 95:96],
                                       -40.0, 0.0, ALU.add, ALU.max),
         waits=[(B.c_act, bn)], inc=B.c_dve)
    E.op("vector",
         lambda eng: eng.tensor_scalar(B.s_small[0:1, 96:97], B.s_small[0:1, 96:97],
                                       -1.0, None, ALU.mult),
         inc=B.c_dve)
    negn = B.c_dve.n
    E.op("tensor",
         lambda eng: eng.matmul(B.ps6[:, 0:1], B.s_ones_rf[:, :],
                                B.s_small[0:1, 96:97], start=True, stop=True),
         waits=[(B.c_dve, negn)] + FR(B.bank_free.get(0)), inc=B.c_mm)
    bcmm = B.c_mm.n
    E.op("vector",
         lambda eng: eng.tensor_copy(B.s_negc0[:, :], B.ps6[:, 0:1]),
         waits=[(B.c_mm, bcmm)] + FR(B.negc0_free), inc=B.c_dve)
    negc0_ready = mark(B.c_dve)
    B.bank_free[0] = mark(B.c_dve)
    B.drow_free = mark(B.c_dve)

    groups_sr = [(s, rp) for s in range(MT) for rp in range(NCORES // 2)]
    NG = len(groups_sr)
    if B.pv_free[1] is None:
        B.pv_free[1] = FR(B.psb_free)

    def sum_row(q, lo):
        return (B.ps6[0:1, 2048 + lo:2048 + Wd] if q == 0
                else B.ps67[0:1, lo:Wd])

    def sum_row_full(q):
        return B.ps6[0:1, 2048:2048 + Wd] if q == 0 else B.ps67[0:1, 0:Wd]

    def sum_full(q):
        return B.ps6[:, 2048:2048 + Wd] if q == 0 else B.ps67[:, 0:Wd]

    def pv_acc(q, lo):
        return (B.ps6[:, 2560 + lo:2560 + Wd] if q == 0
                else B.ps67[:, 512 + lo:1024])

    for g in range(cfg.NKV):
        buf = g % 2
        ksb, vsb = B.s_ksb[buf], B.s_vsb[buf]
        if (half, g, 'k') not in B.kv_pre:
            kex = (FR(B.ksb_free[buf])
                   + (FR(B.cosk_write_ok) if buf == 0 else []))
            for r in range(NCORES):
                E.op("sync",
                     (lambda half, g, r, ksb: lambda eng: eng.dma_start(
                         out=ksb[:, r * L:(r + 1) * L],
                         in_=B.kt_g[half][r * cfg.NKV * 128 + g * 128:
                                          r * cfg.NKV * 128 + (g + 1) * 128, :]))(
                         half, g, r, ksb),
                     waits=[(B.c_cc[half][0], 1)] + kex,
                     inc=B.c_ksb[buf], inc_by=16)
        if (half, g, 'v') not in B.kv_pre:
            vex = FR(B.vsb_free[buf], B.stage_done)
            for r in range(NCORES):
                for ts in range(TS):
                    E.op("sync",
                         (lambda half, g, r, ts, vsb: lambda eng: eng.dma_start(
                             out=vsb[:, (r * TS + ts) * 128:(r * TS + ts + 1) * 128],
                             in_=B.v_g[half][r * L + ts * 128:r * L + (ts + 1) * 128,
                                             g * 128:(g + 1) * 128]))(half, g, r, ts, vsb),
                         waits=[(B.c_cc[half][1], 1)] + vex,
                         inc=B.c_vsb[buf], inc_by=16)
        kld = (B.c_ksb[buf], B.c_ksb[buf].n)
        vld = (B.c_vsb[buf], B.c_vsb[buf].n)
        for h in range(g * cfg.GRP, (g + 1) * cfg.GRP):
            q = B.head_par & 1
            B.head_par += 1
            mskn = {}

            def emit_scores(gi):
                s, rp = groups_sr[gi]
                span = (MT - s) * 128
                sb0 = 2 * (gi % 2)
                pb = gi % 3
                prb = B.s_probs[pb]
                scm = [0, 0]
                for r2 in range(2):
                    r = 2 * rp + r2
                    E.op("tensor",
                         (lambda ksb, h, half, r, s, span, bank:
                          lambda eng: eng.matmul(
                              B.ps6[:, bank * 512:bank * 512 + span],
                              ksb[:, r * L + s * 128:r * L + (s + 1) * 128],
                              B.s_qt[:, h * cfg.LT + half * Wd + s * 128:
                                     h * cfg.LT + (half + 1) * Wd],
                              start=True, stop=True))(ksb, h, half, r, s, span,
                                                      sb0 + r2),
                         waits=[kld, B.qt_done[half]]
                         + FR(B.bank_free.get(sb0 + r2)),
                         inc=B.c_mm)
                    scm[r2] = B.c_mm.n
                B._sc_mm_last = scm[1]
                for r2 in range(2):
                    E.op("scalar",
                         (lambda prb, sb0, r2, span: lambda eng: eng.activation(
                             prb[:, r2 * span:(r2 + 1) * span],
                             B.ps6[:, (sb0 + r2) * 512:(sb0 + r2) * 512 + span],
                             AF.Exp, bias=B.s_negc0[:, :]))(prb, sb0, r2, span),
                         waits=[(B.c_mm, scm[r2]), negc0_ready]
                         + (FR(B.probs_free[pb]) if r2 == 0 else []),
                         inc=B.c_act)
                    expn = B.c_act.n
                    B.bank_free[sb0 + r2] = mark(B.c_act)
                    E.op("vector",
                         (lambda prb, rp, r2, span: lambda eng: eng.tensor_mul(
                             prb[:, r2 * span:r2 * span + 128],
                             prb[:, r2 * span:r2 * span + 128],
                             B.s_maskk[:, (2 * rp + r2) * 128:
                                       (2 * rp + r2 + 1) * 128]))(prb, rp, r2, span),
                         waits=[(B.c_act, expn)], inc=B.c_dve)
                    mskn[(gi, r2)] = B.c_dve.n

            def emit_sumpv(gi, first):
                s, rp = groups_sr[gi]
                span = (MT - s) * 128
                pb = gi % 3
                prb = B.s_probs[pb]
                last_g = gi == NG - 1
                for r2 in range(2):
                    r = 2 * rp + r2
                    stp = last_g and r2 == 1
                    st = first and r2 == 0
                    w = [(B.c_dve, mskn[(gi, r2)]), vld]
                    if st:
                        w = w + FR(B.pv_free[q])
                    E.op("tensor",
                         (lambda prb, q, r2, s, span, st, stp:
                          lambda eng: eng.matmul(
                              sum_row(q, s * 128), B.s_ones_c[:, :],
                              prb[:, r2 * span:(r2 + 1) * span],
                              start=st, stop=stp))(prb, q, r2, s, span, st, stp),
                         waits=w, inc=None)
                    E.op("tensor",
                         (lambda prb, vsb, q, r, r2, s, span, st, stp:
                          lambda eng: eng.matmul(
                              pv_acc(q, s * 128),
                              vsb[:, (r * TS + s) * 128:(r * TS + s + 1) * 128],
                              prb[:, r2 * span:(r2 + 1) * span],
                              start=st, stop=stp))(prb, vsb, q, r, r2, s, span,
                                                   st, stp),
                         waits=w, inc=(B.c_mm if r2 == 1 else None))
                B.probs_free[pb] = mark(B.c_mm)

            for gi in range(NG):
                emit_scores(gi)
                if gi == 1 and B.norm_q:
                    B.norm_q.pop(0)()
                if gi >= 1:
                    emit_sumpv(gi - 1, first=(gi - 1 == 0))
            emit_sumpv(NG - 1, first=(NG == 1))
            B._pv_mm = B.c_mm.n
            pvn = B.c_mm.n

            def norm(h=h, q=q, pvn=pvn):
                E.op("vector",
                     (lambda q: lambda eng: eng.reciprocal(
                         B.s_drow[0:1, 0:Wd], sum_row_full(q)))(q),
                     waits=[(B.c_mm, pvn)] + FR(B.drow_free), inc=B.c_dve)
                recn = B.c_dve.n
                E.op("tensor",
                     (lambda q: lambda eng: eng.matmul(
                         sum_full(q), B.s_ones_rf[:, :], B.s_drow[0:1, 0:Wd],
                         start=True, stop=True))(q),
                     waits=[(B.c_dve, recn)], inc=B.c_mm)
                dbn = B.c_mm.n
                E.op("vector",
                     (lambda q: lambda eng: eng.tensor_copy(
                         B.s_dbc[:, :], sum_full(q)))(q),
                     waits=[(B.c_mm, dbn)] + FR(B.dbc_free), inc=B.c_dve)
                dcn = B.c_dve.n
                B.drow_free = mark(B.c_dve)
                E.op("vector",
                     (lambda h, q: lambda eng: eng.tensor_mul(
                         B.s_att[:, h * Wd:(h + 1) * Wd],
                         pv_acc(q, 0), B.s_dbc[:, :]))(h, q),
                     waits=[(B.c_dve, dcn)] + FR(B.att_write_ok), inc=B.c_dve)
                B.dbc_free = mark(B.c_dve)
                B.pv_free[q] = mark(B.c_dve)
            B.norm_q.append(norm)
        B.ksb_free[buf] = (B.c_mm, B._sc_mm_last)
        B.vsb_free[buf] = (B.c_mm, B._pv_mm)
    while B.norm_q:
        B.norm_q.pop(0)()
    B.att_done = B.c_dve.n
    B.negc0_free = mark(B.c_act)
    B.psb_free = FR(B.pv_free[1])


def emit_wo_half(B, half, src_res, dst, src_ready):
    E, cfg = B.E, B.cfg
    Wd = cfg.W
    occ = cfg.XP
    n_pass = cfg.DM // occ
    att_need = B.att_done
    dst_marks = []
    BB = max(1, min(8, cfg.DM // occ))
    for p in range(n_pass):
        for k0 in range(0, cfg.DMC, BB):
            nb = min(BB, cfg.DMC - k0)
            buf = (k0 // BB) % 2
            wdb = B.s_wd[buf]
            E.op("sync",
                 (lambda wdb, k0, nb, p: lambda eng: eng.dma_start(
                     out=wdb[:, 0:nb * occ].rearrange("p (b u) -> p b u", u=occ),
                     in_=B.wo_bf[k0:k0 + nb, :, p * occ:(p + 1) * occ]
                     .rearrange("b p u -> p b u")))(wdb, k0, nb, p),
                 waits=FR(B.wcast_mark[("wo", k0 + nb - 1)]) + FR(B.wd_free[buf]),
                 inc=B.c_wdl[buf], inc_by=16)
            wldn = (B.c_wdl[buf], B.c_wdl[buf].n)
            for bn in range(nb):
                k = k0 + bn
                for ts in range(cfg.TS):
                    base = ts * (occ // 512)
                    for oc2 in range(occ // 512):
                        tail = (ts == cfg.TS - 1 and oc2 == occ // 512 - 1
                                and bn == nb - 1)
                        E.op("tensor",
                             (lambda wdb, k, bn, ts, oc2, base, first:
                              lambda eng: eng.matmul(
                                  B.ps6[:, (base + oc2) * 512:(base + oc2 + 1) * 512],
                                  B.s_att[:, k * Wd + ts * 128:k * Wd + ts * 128 + 128],
                                  wdb[:, (bn * (occ // 512) + oc2) * 512:
                                      (bn * (occ // 512) + oc2 + 1) * 512],
                                  start=first, stop=False))(wdb, k, bn, ts, oc2,
                                                            base, k == 0),
                             waits=([wldn, (B.c_dve, att_need)]
                                    + (FR(B.bank_free.get(base + oc2)) if k == 0 else [])),
                             inc=(B.c_mm if tail else None))
            B.wd_free[buf] = mark(B.c_mm)
        for ts in range(cfg.TS):
            base = ts * (occ // 512)
            for oc2 in range(occ // 512):
                E.op("tensor",
                     (lambda p, oc2, base: lambda eng: eng.matmul(
                         B.ps6[:, (base + oc2) * 512:(base + oc2 + 1) * 512],
                         B.s_ones_r[:, :],
                         B.s_brow[:, p * occ + oc2 * 512:p * occ + (oc2 + 1) * 512],
                         start=False, stop=True))(p, oc2, base),
                     waits=FR(B.brow_ready),
                     inc=(B.c_mm if oc2 == occ // 512 - 1 else None))
            mm_done = B.c_mm.n
            xslot = ts % 4
            xb = B.s_xp[xslot]
            cx = B.c_xp[xslot]
            row0 = half * Wd + ts * 128
            E.op("scalar",
                 (lambda xb, row0, p: lambda eng: eng.dma_start(
                     out=xb[:, 0:occ], in_=src_res[row0:row0 + 128,
                                                   p * occ:(p + 1) * occ]))(xb, row0, p),
                 waits=FR(B.xp_free[xslot], src_ready), inc=cx, inc_by=16)
            ldn = cx.n
            E.op("vector",
                 (lambda xb, base: lambda eng: eng.scalar_tensor_tensor(
                     xb[:, 0:occ], B.ps6[:, base * 512:base * 512 + occ],
                     1.0, xb[:, 0:occ], ALU.mult, ALU.add))(xb, base),
                 waits=[(B.c_mm, mm_done), (cx, ldn)], inc=B.c_dve)
            for oc2 in range(occ // 512):
                B.bank_free[base + oc2] = mark(B.c_dve)
            stt = B.c_dve.n
            E.op("scalar",
                 (lambda xb, row0, p: lambda eng: eng.dma_start(
                     out=dst[row0:row0 + 128, p * occ:(p + 1) * occ],
                     in_=xb[:, 0:occ]))(xb, row0, p),
                 waits=[(B.c_dve, stt)], inc=cx, inc_by=16)
            B.xp_free[xslot] = mark(cx)
            dst_marks.append(mark(cx))
    B.att_write_ok = mark(B.c_mm)
    B.dst_done = dst_marks


def _debug_out(B, src):
    E, cfg = B.E, B.cfg
    occ = cfg.XP
    done = B.dst_done
    for t in range(2 * cfg.TS):
        for p in range(cfg.DM // occ):
            xslot = (t * (cfg.DM // occ) + p) % 2
            xb = B.s_xp[xslot]
            cx = B.c_xp[xslot]
            E.op("sync",
                 (lambda xb, t, p: lambda eng: eng.dma_start(
                     out=xb[:, 0:occ], in_=src[t * 128:(t + 1) * 128,
                                               p * occ:(p + 1) * occ]))(xb, t, p),
                 waits=FR(B.xp_free[xslot], done), inc=cx, inc_by=16)
            ldn = cx.n
            E.op("sync",
                 (lambda xb, t, p: lambda eng: eng.dma_start(
                     out=B.out[t * 128:(t + 1) * 128, p * occ:(p + 1) * occ],
                     in_=xb[:, 0:occ]))(xb, t, p),
                 waits=[(cx, ldn)], inc=cx, inc_by=16)
            B.xp_free[xslot] = mark(cx)


def emit_tail(B):
    B.E.op("sync", lambda eng: eng.nop(),
           waits=[(c, c.n) for c in B.c_xp if c.n > 0])


def emit_all(B):
    cfg = B.cfg
    dn_jobs = [("dn", i, B.wdn_bf[i], B.wdn[i]) for i in range(cfg.NIB)]
    wo_jobs = [("wo", k, B.wo_bf[k], B.wo[k]) for k in range(cfg.DMC)]
    if cfg.stop == "prod":
        emit_consts(B)
        for e in ("tensor", "vector", "scalar"):
            B.E.op(e, lambda eng: eng.nop(), waits=[(B.c_const, B.const_n)])
        emit_weight_cast(B, dn_jobs + wo_jobs)
        emit_ln(B, B.x_loc, 0, None)
        emit_ffn_half(B, 0, B.x_loc, B.x1, None)
        emit_tail(B)
        return
    emit_consts(B)
    for e in ("tensor", "vector", "scalar"):
        B.E.op(e, lambda eng: eng.nop(), waits=[(B.c_const, B.const_n)])
    B.pending_casts.extend(dn_jobs)
    for b in range(2):
        emit_ln(B, B.x_loc, b, None)
        if b == 1:
            B.pending_casts.extend(wo_jobs)
        emit_ffn_half(B, b, B.x_loc, B.x1, None)
        B.xready[("x1", b)] = B.dst_done
    if cfg.stop == "ffn1":
        _debug_out(B, B.x1)
        emit_tail(B)
        return
    for b in range(2):
        emit_ln(B, B.x1, b, B.xready[("x1", b)])
        emit_qkv_half(B, b)
        emit_gather(B, b)
        if b == 0:
            emit_kv_prefetch(B, 0, 0, want_k=False)
            emit_kv_prefetch(B, 0, 1)
    B.att_write_ok = B.xlnt_write_ok_att
    load_brow(B, B.brow_o)
    for b in range(2):
        emit_attention_half(B, b)
        emit_wo_half(B, b, B.x1, B.x2, B.xready[("x1", b)])
        B.xready[("x2", b)] = B.dst_done
    if cfg.stop == "attn":
        _debug_out(B, B.x2)
        emit_tail(B)
        return
    load_brow(B, B.brow_d)
    B.xlnt_write_ok = B.att_write_ok
    for b in range(2):
        emit_ln(B, B.x2, b, B.xready[("x2", b)])
        emit_ffn_half(B, b, B.x2, B.out, B.xready[("x2", b)])
    emit_tail(B)


def build_program(cfg):
    B = Build(cfg)
    emit_all(B)
    nc, E = B.nc, B.E
    with nc.Block() as block:
        @block.sync
        def _(eng):
            E.emit_engine("sync", eng)

        @block.scalar
        def _(eng):
            E.emit_engine("scalar", eng)

        @block.tensor
        def _(eng):
            E.emit_engine("tensor", eng)

        @block.vector
        def _(eng):
            E.emit_engine("vector", eng)

        @block.gpsimd
        def _(eng):
            E.emit_engine("gpsimd", eng)
    return B, nc


def kernel(**inputs):
    cfg = Cfg()
    in_maps = host_prepare(cfg, inputs)
    B, nc = build_program(cfg)
    res = run_bass_kernel_spmd(nc, in_maps, core_ids=list(range(NCORES)))
    out = np.empty((cfg.B, cfg.S, cfg.DM), np.float32)
    for c in range(NCORES):
        loc = np.asarray(res.results[c]["out_loc"], dtype=np.float32)
        out[:, c::NCORES, :] = loc.reshape(cfg.B, cfg.L, cfg.DM)
    return out


if __name__ == "__main__":
    cfg = Cfg()
    B, nc = build_program(cfg)
    print("built:", nc.n_instructions(), "instructions; sbuf",
          B.sbuf_used, "/", nc.SBUF_PARTITION_SIZE_BYTES)


# version 12
# speedup vs baseline: 1.0645x; 1.0182x over previous
import math
import numpy as np
import ml_dtypes

import concourse.bass as bass
import concourse.mybir as mybir
from concourse.bass_utils import run_bass_kernel_spmd

F32 = mybir.dt.float32
BF16 = mybir.dt.bfloat16
AF = mybir.ActivationFunctionType
ALU = mybir.AluOpType
AX = mybir.AxisListType
bfnp = ml_dtypes.bfloat16

NCORES = 8


class Cfg:
    def __init__(self, dm=4096, inter=11008, nh=32, nkv=8, seq=4096, batch=2,
                 xp=None):
        self.DM, self.IN, self.NH, self.NKV, self.HD = dm, inter, nh, nkv, 128
        assert nh * 128 == dm and batch == 2
        self.B, self.S = batch, seq
        self.L = seq // NCORES
        assert self.L % 128 == 0
        self.MT = self.L // 128
        self.LT = self.B * self.L
        self.W = self.L
        self.TS = self.W // 128
        self.DMC = dm // 128
        self.NIB = inter // 128
        self.NQKV = nh + 2 * nkv
        self.GRP = nh // nkv
        self.eps = 1e-5
        self.theta = 10000.0
        self.IH0 = (self.NIB + 1) // 2
        self.XP = xp if xp is not None else min(512, dm)
        self.NPC = dm // self.XP
        self.stop = "full"
        assert self.TS * (self.XP // 512) <= 4


def _img(wt):
    kin = wt.shape[0]
    return np.ascontiguousarray(
        wt.reshape(kin // 128, 128, 128).transpose(1, 0, 2).reshape(128, kin))


def host_prepare(cfg, inputs):
    f32 = np.float32
    x = np.asarray(inputs["L0_attn_x"], f32)
    w_qkv = np.asarray(inputs["L0_attn_w_qkv"], f32)
    b_qkv = np.asarray(inputs["L0_attn_b_qkv"], f32)
    w_o = np.asarray(inputs["L0_attn_w_o"], f32)
    b_o = np.asarray(inputs["L0_attn_b_o"], f32)
    w_g = np.asarray(inputs["L0_ffn_w_gate"], f32)
    b_g = np.asarray(inputs["L0_ffn_b_gate"], f32)
    w_u = np.asarray(inputs["L0_ffn_w_up"], f32)
    b_u = np.asarray(inputs["L0_ffn_b_up"], f32)
    w_d = np.asarray(inputs["L0_ffn_w_down"], f32)
    b_d = np.asarray(inputs["L0_ffn_b_down"], f32)
    DM, NH, NKV = cfg.DM, cfg.NH, cfg.NKV

    wgu = np.empty((2 * cfg.NIB, 128, DM), f32)
    for i in range(cfg.NIB):
        wgu[2 * i] = _img(np.ascontiguousarray(w_g[i * 128:(i + 1) * 128, :].T))
        wgu[2 * i + 1] = _img(np.ascontiguousarray(w_u[i * 128:(i + 1) * 128, :].T))
    wdn = np.ascontiguousarray(w_d.T).reshape(cfg.NIB, 128, DM)
    perm = np.concatenate([np.arange(0, 128, 2), np.arange(1, 128, 2)])
    wqkv = np.empty((cfg.NQKV, 128, DM), f32)
    bqkv_cols = np.empty((128, cfg.NQKV), f32)
    for j in range(cfg.NQKV):
        rows = np.arange(j * 128, (j + 1) * 128)
        if j < NH + NKV:
            rows = rows[perm]
        wqkv[j] = _img(np.ascontiguousarray(w_qkv[rows, :].T))
        bqkv_cols[:, j] = b_qkv[rows]
    bqkv_swap = np.concatenate([bqkv_cols[64:128, :], bqkv_cols[0:64, :]], 0)
    wo = np.ascontiguousarray(w_o.T).reshape(cfg.DMC, 128, DM)

    bias_cols = np.zeros((128, 2 * cfg.NIB), f32)
    for i in range(cfg.NIB):
        bias_cols[:, 2 * i] = b_g[i * 128:(i + 1) * 128]
        bias_cols[:, 2 * i + 1] = b_u[i * 128:(i + 1) * 128]
    brow_d = np.ascontiguousarray(b_d.reshape(1, DM))
    brow_o = np.ascontiguousarray(b_o.reshape(1, DM))

    in_maps = []
    inv_freq = 1.0 / cfg.theta ** (np.arange(0, 128, 2, dtype=np.float64) / 128)
    qs = 1.0 / math.sqrt(128.0)
    for c in range(NCORES):
        x_loc = np.ascontiguousarray(x[:, c::NCORES, :].reshape(cfg.LT, DM))
        pos = np.empty(cfg.LT, np.float64)
        for b in range(cfg.B):
            pos[b * cfg.L:(b + 1) * cfg.L] = (
                np.arange(cfg.L, dtype=np.float64) * NCORES + c)
        ang = inv_freq[:, None] * pos[None, :]
        cosv, sinv = np.cos(ang), np.sin(ang)
        cos_t = np.concatenate([cosv, cosv], 0).astype(f32)
        sin_t = np.concatenate([-sinv, sinv], 0).astype(f32)
        jj = np.arange(128)[:, None]
        uu = np.arange(128)[None, :]
        maskk = np.zeros((128, NCORES * 128), f32)
        for r in range(NCORES):
            maskk[:, r * 128:(r + 1) * 128] = (jj <= uu - (1 if r > c else 0))
        in_maps.append({
            "x_loc": x_loc, "wgu": wgu, "wdn": wdn, "wqkv": wqkv, "wo": wo,
            "bias_cols": bias_cols, "bqkv_cols": bqkv_cols,
            "bqkv_swap": np.ascontiguousarray(bqkv_swap),
            "brow_d": brow_d, "brow_o": brow_o,
            "cos_q": (cos_t * qs).astype(f32), "sin_q": (sin_t * qs).astype(f32),
            "cos_k": cos_t, "sin_k": sin_t,
            "maskk": maskk.astype(bfnp),
        })
    return in_maps


class Chan:
    __slots__ = ("sem", "n", "name")

    def __init__(self, nc, name):
        self.sem = nc.alloc_semaphore(name)
        self.n = 0
        self.name = name


def FR(*marks):
    out = []
    for m in marks:
        if m is None:
            continue
        if isinstance(m, list):
            out.extend(x for x in m if x is not None and x[0] is not None and x[1] > 0)
        elif m[0] is not None and m[1] > 0:
            out.append(m)
    return out


class Emit:
    ENGINES = ("sync", "scalar", "tensor", "vector", "gpsimd")

    def __init__(self, nc):
        self.nc = nc
        self.q = {e: [] for e in self.ENGINES}
        self.wm = {}
        self.self_chan = {}
        self.debug_src = False

    def chan(self, name):
        return Chan(self.nc, name)

    def op(self, eng, fn, waits=(), inc=None, inc_by=1):
        selfc = self.self_chan.get(eng)
        waits = list(waits)
        if selfc is not None and selfc.n > 0:
            waits.append((selfc, selfc.n))
        waits = tuple((c, v) for (c, v) in waits if v > 0)
        if inc is not None:
            inc.n += inc_by
        if self.debug_src:
            import traceback
            fr = traceback.extract_stack(limit=3)[1]
            self.q[eng].append((fn, waits, inc, inc_by, f"{fr.name}:{fr.lineno}"))
        else:
            self.q[eng].append((fn, waits, inc, inc_by))

    def emit_engine(self, eng_name, engine):
        wm = self.wm
        for fn, waits, inc, inc_by, *_ in self.q[eng_name]:
            for (c, v) in waits:
                key = (eng_name, c.name)
                if wm.get(key, 0) < v:
                    engine.wait_ge(c.sem, v)
                    wm[key] = v
            inst = fn(engine)
            if inc is not None:
                inst.then_inc(inc.sem, inc_by)


class Build:
    def __init__(self, cfg):
        self.cfg = cfg
        nc = bass.Bass("TRN2", target_bir_lowering=False,
                       dynamic_dma_scratch_size=16384)
        self.nc = nc
        c = cfg
        DM, LT = c.DM, c.LT
        dt = nc.dram_tensor

        self.x_loc = dt("x_loc", [LT, DM], F32, kind="ExternalInput").ap()
        self.wgu = dt("wgu", [2 * c.NIB, 128, DM], F32, kind="ExternalInput").ap()
        self.wdn = dt("wdn", [c.NIB, 128, DM], F32, kind="ExternalInput").ap()
        self.wqkv = dt("wqkv", [c.NQKV, 128, DM], F32, kind="ExternalInput").ap()
        self.wo = dt("wo", [c.DMC, 128, DM], F32, kind="ExternalInput").ap()
        self.bias_cols = dt("bias_cols", [128, 2 * c.NIB], F32, kind="ExternalInput").ap()
        self.bqkv_cols = dt("bqkv_cols", [128, c.NQKV], F32, kind="ExternalInput").ap()
        self.bqkv_swap = dt("bqkv_swap", [128, c.NQKV], F32, kind="ExternalInput").ap()
        self.brow_d = dt("brow_d", [1, DM], F32, kind="ExternalInput").ap()
        self.brow_o = dt("brow_o", [1, DM], F32, kind="ExternalInput").ap()
        self.cos_q = dt("cos_q", [128, LT], F32, kind="ExternalInput").ap()
        self.sin_q = dt("sin_q", [128, LT], F32, kind="ExternalInput").ap()
        self.cos_k = dt("cos_k", [128, LT], F32, kind="ExternalInput").ap()
        self.sin_k = dt("sin_k", [128, LT], F32, kind="ExternalInput").ap()
        self.maskk = dt("maskk", [128, NCORES * 128], BF16, kind="ExternalInput").ap()
        self.out = dt("out_loc", [LT, DM], F32, kind="ExternalOutput").ap()
        nslot_dbg = max(c.IH0, c.NIB - c.IH0)
        self.dbg = dt("dbg", [128, nslot_dbg * c.W], BF16, kind="ExternalOutput").ap()

        self.wdn_bf = dt("wdn_bf", [c.NIB, 128, DM], BF16).ap()
        self.wo_bf = dt("wo_bf", [c.DMC, 128, DM], BF16).ap()
        self.wgu_bf = dt("wgu_bf", [2 * c.NIB, 128, DM], BF16).ap()
        self.wqkv_bf = dt("wqkv_bf", [c.NQKV, 128, DM], BF16).ap()
        self.x1 = dt("x1", [LT, DM], F32).ap()
        self.xmid = dt("xmid", [LT, DM], F32).ap()
        self.x2 = dt("x2", [LT, DM], F32).ap()
        self.kt_in = [dt(f"kt_in{b}", [c.NKV * 128, c.L], BF16).ap() for b in range(2)]
        self.v_in = [dt(f"v_in{b}", [c.L, c.NKV * 128], BF16).ap() for b in range(2)]
        self.kn_in = [dt(f"kn_in{b}", [1, 16], F32).ap() for b in range(2)]
        self.kt_g = [dt(f"kt_g{b}", [NCORES * c.NKV * 128, c.L], BF16).ap() for b in range(2)]
        self.v_g = [dt(f"v_g{b}", [NCORES * c.L, c.NKV * 128], BF16).ap() for b in range(2)]
        self.kn_g = [dt(f"kn_g{b}", [NCORES, 16], F32).ap() for b in range(2)]

        self.i_ident = nc.inline_tensor(np.eye(128, dtype=bfnp), name="i_ident").ap()
        self.i_ones_r = nc.inline_tensor(np.ones((1, 128), bfnp), name="i_ones_r").ap()
        self.i_ones_c = nc.inline_tensor(np.ones((128, 1), bfnp), name="i_ones_c").ap()
        self.i_ones_m = nc.inline_tensor(np.ones((128, 128), bfnp), name="i_ones_m").ap()
        self.i_ones_rf = nc.inline_tensor(np.ones((1, 128), np.float32), name="i_ones_rf").ap()

        self._off = (nc.sbuf_base + 63) // 64 * 64
        alloc = []

        def A(name, shape, dtype, at=None):
            nbytes = int(np.prod(shape[1:])) * (2 if dtype == BF16 else 4)
            if at is None:
                at = self._off
                self._off += (nbytes + 63) // 64 * 64
            t = nc.alloc_sbuf_tensor_at(name, list(shape), dtype, offset=at)
            alloc.append((name, at, nbytes))
            setattr(self, name[0] == "s" and name or name, t)
            return t

        self.s_biasc = A("s_biasc", [128, 2 * c.NIB], F32)
        self.s_bqkvc = A("s_bqkvc", [128, c.NQKV], F32)
        self.s_bqkvs = A("s_bqkvs", [128, c.NQKV], F32)
        self.s_maskk = A("s_maskk", [128, NCORES * 128], BF16)
        self.s_ident = A("s_ident", [128, 128], BF16)
        self.s_ones_r = A("s_ones_r", [1, 128], BF16)
        self.s_ones_c = A("s_ones_c", [128, 1], BF16)
        self.s_ones_m = A("s_ones_m", [128, 128], BF16)
        self.s_ones_rf = A("s_ones_rf", [1, 128], F32)
        self.s_brow = A("s_brow", [1, DM], BF16)
        self.s_stats = A("s_stats", [128, 32], F32)
        self.s_small = A("s_small", [1, 128], F32)
        self.s_negc0 = A("s_negc0", [128, 1], F32)
        self.s_drow = A("s_drow", [1, max(c.W, 16 * NCORES)], F32)
        self.s_dbc = A("s_dbc", [128, c.W], F32)
        self.s_xp = [A(f"s_xp{i}", [128, c.XP], F32) for i in range(6)]
        self.s_probs = [A(f"s_probs{i}", [128, 1024], BF16) for i in range(3)]
        assert c.XP * 2 <= 2048
        self.s_lntok = [A(f"s_lntok{i}", [128, c.XP], BF16,
                          at=[a for a in alloc if a[0] == f"s_probs{i}"][0][1])
                        for i in range(2)]
        self.s_silu = [A(f"s_silu{i}", [128, c.W], F32) for i in range(2)]
        self.s_rsc = [A(f"s_rsc{i}", [128, c.W], F32,
                        at=[a for a in alloc if a[0] == f"s_silu{i}"][0][1])
                      for i in range(2)]
        self.s_rsc.append(A("s_rsc2", [128, c.W], F32))
        self.s_kstg = A("s_kstg", [128, c.W], BF16)
        self.s_sq = A("s_sq", [128, c.W], BF16)
        self.s_vstg = A("s_vstg", [128, c.TS * c.NKV * 128], BF16)
        self.s_wd = [A(f"s_wd{i}", [128, DM], BF16) for i in range(2)]
        self.s_xlnt = A("s_xlnt", [128, c.DMC * c.W], BF16)
        xlnt_off = [a for a in alloc if a[0] == "s_xlnt"][0][1]
        self.s_att = A("s_att", [128, c.DMC * c.W], BF16, at=xlnt_off)

        nslot = max(c.IH0, c.NIB - c.IH0)
        prod_sz = ((nslot * c.W * 2 + 63) // 64) * 64
        ring_sz = 4 * DM * 2
        qt_sz = c.NH * LT * 2
        big0 = self._off
        self._off += max(prod_sz + ring_sz, qt_sz + 2 * LT * 4)
        self.s_prod = A("s_prod", [128, nslot * c.W], BF16, at=big0)
        self.s_ring = [A(f"s_ring{i}", [128, DM], BF16, at=big0 + prod_sz + i * DM * 2)
                       for i in range(4)]
        self.s_qt = A("s_qt", [128, c.NH * LT], BF16, at=big0)
        self.s_cosq = A("s_cosq", [128, LT], F32, at=big0 + qt_sz)
        self.s_sinq = A("s_sinq", [128, LT], F32, at=big0 + qt_sz + LT * 4)

        kv0 = self._off
        ksz = ((NCORES * c.L * 2 + 63) // 64) * 64
        self._off += 4 * ksz
        self.s_ksb = [A(f"s_ksb{i}", [128, NCORES * c.L], BF16, at=kv0 + i * ksz)
                      for i in range(2)]
        self.s_vsb = [A(f"s_vsb{i}", [128, NCORES * c.L], BF16, at=kv0 + (2 + i) * ksz)
                      for i in range(2)]
        assert 2 * ksz >= 2 * LT * 4
        self.s_cosk = A("s_cosk", [128, LT], F32, at=kv0)
        self.s_sink = A("s_sink", [128, LT], F32, at=kv0 + LT * 4)
        assert ksz >= 2048 * 4
        self.s_wstg = [A(f"s_wstg{i}", [128, 2048], F32, at=kv0 + (2 + i) * ksz)
                       for i in range(2)]

        cap = nc.SBUF_PARTITION_SIZE_BYTES
        assert self._off <= cap, (self._off, cap)
        self.sbuf_used = self._off
        self.alloc_table = alloc

        self.ps6 = nc.alloc_psum_tensor("ps6", [128, 3072], F32)
        self.psb = nc.alloc_psum_tensor("psb", [128, 2048], BF16)
        self.ps67 = self.psb.bitcast(F32)

        E = self.E = Emit(nc)
        self.c_const = E.chan("c_const")
        self.c_wfree = E.chan("c_wfree")
        self.c_mm = E.chan("c_mm")
        self.c_act = E.chan("c_act")
        self.c_dve = E.chan("c_dve")
        self.c_lnt = E.chan("c_lnt")
        self.c_cc = [[E.chan(f"c_cc{b}_{i}") for i in range(3)] for b in range(2)]
        self.c_gdma = E.chan("c_gdma")
        self.c_wcg = [E.chan(f"c_wcg{i}") for i in range(4)]
        self.c_ring = [E.chan(f"c_ring{i}") for i in range(4)]
        self.c_wdl = [E.chan(f"c_wdl{i}") for i in range(2)]
        self.c_xp = [E.chan(f"c_xp{i}") for i in range(6)]
        self.c_ksb = [E.chan(f"c_ksb{i}") for i in range(2)]
        self.c_vsb = [E.chan(f"c_vsb{i}") for i in range(2)]
        self.c_qv = [E.chan(f"c_qv{i}") for i in range(2)]
        self.c_kstg = E.chan("c_kstg")
        self.c_vstg = E.chan("c_vstg")
        self.c_wgub = E.chan("c_wgub")
        self.c_stg = [E.chan(f"c_stg{i}") for i in range(2)]
        self.c_kn = E.chan("c_kn")
        self.c_attld = E.chan("c_attld")
        E.self_chan = {"scalar": self.c_act, "vector": self.c_dve}
        self.c_kv = E.chan("c_kv")

        self.wcast_mark = {}
        self.wgu_wb = {}
        self.wqkv_wb = {}
        self.ffn_call = 0
        self.wstg_free = [None, None]
        self.wstg_par = 0
        self.stage_done = None
        self.pending_casts = []
        self.cast_rot = 0
        self.xp_free = [None] * 6
        self.lntok_free = [None] * 2
        self.silu_free = [None] * 2
        self.rsc_free = [None] * 3
        self.probs_free = [None] * 3
        self.ksb_free = [None] * 2
        self.vsb_free = [None] * 2
        self.pv_free = {0: None, 1: None}
        self.head_par = 0
        self.norm_q = []
        self.kv_pre = set()
        self.ring_free = [None] * 4
        self.wd_free = [None] * 2
        self.bank_free = {}
        self.psb_free = None
        self.kstg_free = None
        self.vstg_free = None
        self.sq_free = None
        self.stats_free = None
        self.small_free = None
        self.drow_free = None
        self.dbc_free = None
        self.negc0_free = None
        self.brow_ready = None
        self.big_free = None
        self.xlnt_write_ok = None
        self.att_write_ok = None
        self.cosk_write_ok = None
        self.ln_ready = 0
        self.qt_done = {}
        self.xready = {}


def mark(ch):
    return (ch, ch.n)


def emit_consts(B):
    E = B.E
    loads = [
        (B.s_biasc[:, :], B.bias_cols), (B.s_bqkvc[:, :], B.bqkv_cols),
        (B.s_bqkvs[:, :], B.bqkv_swap),
        (B.s_maskk[:, :], B.maskk), (B.s_ident[:, :], B.i_ident),
        (B.s_ones_r[:, :], B.i_ones_r), (B.s_ones_c[:, :], B.i_ones_c),
        (B.s_ones_m[:, :], B.i_ones_m),
        (B.s_ones_rf[:, :], B.i_ones_rf),
        (B.s_cosk[:, :], B.cos_k), (B.s_sink[:, :], B.sin_k),
    ]
    for d, s in loads:
        E.op("sync", (lambda d, s: lambda eng: eng.dma_start(out=d, in_=s))(d, s),
             inc=B.c_const, inc_by=16)
    B.const_n = B.c_const.n
    E.op("gpsimd", lambda eng: eng.dma_start(out=B.s_brow[:, :], in_=B.brow_d),
         inc=B.c_gdma, inc_by=16)
    B.brow_ready = (B.c_gdma, B.c_gdma.n)


def load_brow(B, src_row):
    E = B.E
    E.op("gpsimd",
         (lambda s: lambda eng: eng.dma_start(out=B.s_brow[:, :], in_=s))(src_row),
         waits=[(B.c_mm, B.c_mm.n)], inc=B.c_gdma, inc_by=16)
    B.brow_ready = (B.c_gdma, B.c_gdma.n)


def emit_weight_cast(B, kind_jobs, pace=None):
    E, cfg = B.E, B.cfg
    GSZ = 8
    jobs = kind_jobs
    for gi in range(0, len(jobs), GSZ):
        grp = jobs[gi:gi + GSZ]
        sidx = (gi // GSZ) % 4
        ch = B.c_wcg[sidx]
        throttle = [(ch, ch.n)] if ch.n > 0 else []
        if pace is not None:
            throttle = throttle + FR(pace(gi))
        for n, (kind, i, dst, src) in enumerate(grp):
            E.op("gpsimd",
                 (lambda d, s: lambda eng: eng.dma_start(out=d, in_=s))(dst, src),
                 waits=(throttle if n == 0 else []), inc=ch, inc_by=16)
        for kind, i, dst, src in grp:
            B.wcast_mark[(kind, i)] = (ch, ch.n)


def emit_ln(B, src, half, src_ready):
    E, cfg = B.E, B.cfg
    st = B.s_stats
    inv_n = 1.0 / cfg.DM
    TS, NPC, XP = cfg.TS, cfg.NPC, cfg.XP
    last_sq_add = None

    for t in range(TS):
        row0 = half * cfg.W + t * 128
        for p in range(NPC):
            slot = 4 + ((t * NPC + p) % 2)
            buf = B.s_xp[slot]
            cx = B.c_xp[slot]
            E.op("sync",
                 (lambda bb, rr, pp: lambda eng: eng.dma_start(
                     out=bb[:, :], in_=src[rr:rr + 128, pp * XP:(pp + 1) * XP]))(buf, row0, p),
                 waits=FR(src_ready, B.xp_free[slot]), inc=cx, inc_by=16)
            ldn = cx.n
            E.op("vector",
                 (lambda bb, tt, pp: lambda eng: eng.reduce_sum(
                     st[:, (tt if pp == 0 else 16):(tt if pp == 0 else 16) + 1],
                     bb[:, :], axis=AX.X))(buf, t, p),
                 waits=[(cx, ldn)] + FR(B.stats_free), inc=B.c_dve)
            if p > 0:
                E.op("vector",
                     (lambda tt: lambda eng: eng.tensor_add(
                         st[:, tt:tt + 1], st[:, tt:tt + 1], st[:, 16:17]))(t),
                     inc=B.c_dve)
            sl = (t * NPC + p) % 2
            E.op("scalar",
                 (lambda bb, tt, pp, sl: lambda eng: eng.activation(
                     B.s_lntok[sl][:, :], bb[:, :], AF.Square,
                     accum_out=st[:, (4 + tt if pp == 0 else 17):
                                  (4 + tt if pp == 0 else 17) + 1]))(buf, t, p, sl),
                 waits=[(cx, ldn)] + FR(B.lntok_free[sl], last_sq_add,
                                        B.stats_free),
                 inc=B.c_act)
            sq_n = B.c_act.n
            if p > 0:
                E.op("vector",
                     (lambda tt: lambda eng: eng.tensor_add(
                         st[:, 4 + tt:5 + tt], st[:, 4 + tt:5 + tt], st[:, 17:18]))(t),
                     waits=[(B.c_act, sq_n)], inc=B.c_dve)
                last_sq_add = mark(B.c_dve)
            B.xp_free[slot] = [mark(B.c_act), mark(B.c_dve)]
        E.op("vector",
             (lambda tt: lambda eng: eng.tensor_scalar(
                 st[:, 8 + tt:9 + tt], st[:, tt:tt + 1], inv_n, None, ALU.mult))(t),
             waits=[(B.c_act, B.c_act.n)], inc=B.c_dve)
        E.op("vector",
             (lambda tt: lambda eng: eng.tensor_scalar(
                 st[:, 16:17], st[:, 8 + tt:9 + tt], st[:, 8 + tt:9 + tt],
                 cfg.eps, ALU.mult, ALU.subtract))(t),
             inc=B.c_dve)
        E.op("vector",
             (lambda tt: lambda eng: eng.scalar_tensor_tensor(
                 st[:, 12 + tt:13 + tt], st[:, 4 + tt:5 + tt], inv_n,
                 st[:, 16:17], ALU.mult, ALU.subtract))(t),
             inc=B.c_dve)
    veps_done = B.c_dve.n

    E.op("scalar",
         lambda eng: eng.activation(st[:, 18:18 + TS], st[:, 12:12 + TS], AF.Ln),
         waits=[(B.c_dve, veps_done)], inc=B.c_act)
    E.op("vector",
         lambda eng: eng.tensor_scalar(st[:, 18:18 + TS], st[:, 18:18 + TS],
                                       -0.5, None, ALU.mult),
         waits=[(B.c_act, B.c_act.n)], inc=B.c_dve)
    E.op("scalar",
         lambda eng: eng.activation(st[:, 18:18 + TS], st[:, 18:18 + TS], AF.Exp),
         waits=[(B.c_dve, B.c_dve.n)], inc=B.c_act)
    rstd_done = B.c_act.n
    E.op("vector",
         lambda eng: eng.tensor_mul(st[:, 24:24 + TS], st[:, 8:8 + TS],
                                    st[:, 18:18 + TS]),
         waits=[(B.c_act, rstd_done)], inc=B.c_dve)
    E.op("vector",
         lambda eng: eng.tensor_scalar(st[:, 24:24 + TS], st[:, 24:24 + TS],
                                       -1.0, None, ALU.mult),
         inc=B.c_dve)
    negmr_done = B.c_dve.n

    for t in range(TS):
        row0 = half * cfg.W + t * 128
        for p in range(NPC):
            slot = 4 + ((t * NPC + p) % 2)
            buf = B.s_xp[slot]
            cx = B.c_xp[slot]
            E.op("sync",
                 (lambda bb, rr, pp: lambda eng: eng.dma_start(
                     out=bb[:, :], in_=src[rr:rr + 128, pp * XP:(pp + 1) * XP]))(buf, row0, p),
                 waits=FR(B.xp_free[slot]), inc=cx, inc_by=16)
            ldn = cx.n
            sl = (t * NPC + p) % 2
            lnt = B.s_lntok[sl]
            E.op("vector",
                 (lambda bb, oo, tt: lambda eng: eng.tensor_scalar(
                     oo[:, :], bb[:, :], st[:, 18 + tt:19 + tt],
                     st[:, 24 + tt:25 + tt], ALU.mult, ALU.add))(buf, lnt, t),
                 waits=[(cx, ldn), (B.c_dve, negmr_done),
                        (B.c_act, rstd_done)] + FR(B.lntok_free[sl]),
                 inc=B.c_dve)
            ap_done = B.c_dve.n
            B.xp_free[slot] = mark(B.c_dve)
            nch = XP // 128
            for g in range(nch):
                E.op("tensor",
                     (lambda gg, ii: lambda eng: eng.transpose(
                         B.psb[:, gg * 128:(gg + 1) * 128], ii, B.s_ident[:, :]))(
                         g, lnt[:, g * 128:(g + 1) * 128]),
                     waits=[(B.c_dve, ap_done)] + FR(B.psb_free),
                     inc=(B.c_mm if g == nch - 1 else None))
            tr_done = B.c_mm.n
            B.lntok_free[sl] = mark(B.c_mm)

            def evac(eng, pp=p, nch=nch, tt=t):
                dst = B.s_xlnt[:, :].rearrange("p (c w) -> p c w", w=cfg.W)[
                    :, pp * nch:(pp + 1) * nch, tt * 128:tt * 128 + 128]
                srcap = B.psb[:, 0:nch * 128].rearrange("p (c u) -> p c u", u=128)
                return eng.tensor_copy(dst, srcap)
            E.op("vector", evac,
                 waits=[(B.c_mm, tr_done)] + FR(B.xlnt_write_ok), inc=B.c_lnt)
            B.psb_free = mark(B.c_lnt)
    B.ln_ready = B.c_lnt.n
    B.stats_free = mark(B.c_act)


def emit_ffn_half(B, half, src_res, dst, src_ready):
    E, cfg = B.E, B.cfg
    xl = B.s_xlnt
    Wd = cfg.W
    ln_need = B.ln_ready
    groups = [list(range(0, cfg.IH0)), list(range(cfg.IH0, cfg.NIB))]
    groups = [g for g in groups if g]
    occ = cfg.XP
    n_pass = cfg.DM // occ
    mid_mark = {}
    dst_marks = []

    for ih, blocks in enumerate(groups):
        last_ih = ih == len(groups) - 1
        mode = ("f32", "f32wb", "bf16", "bf16")[min(B.ffn_call, 3)]
        for slot, i in enumerate(blocks):
            mmk = [0, 0]
            for gu in range(2):
                ri = (2 * slot + gu) % 4
                rb = B.s_ring[ri]
                idx = 2 * i + gu
                if mode != "bf16":
                    rcm = [None, None]
                    for hc in range(2):
                        sl = B.wstg_par
                        B.wstg_par ^= 1
                        stb = B.s_wstg[sl]
                        cs = B.c_stg[sl]
                        E.op("sync",
                             (lambda stb, idx, hc: lambda eng: eng.dma_start(
                                 out=stb[:, :],
                                 in_=B.wgu[idx][:, hc * 2048:(hc + 1) * 2048]))(
                                 stb, idx, hc),
                             waits=FR(B.wstg_free[sl]), inc=cs, inc_by=16)
                        ldn = cs.n
                        cast_w = [(cs, ldn)] + FR(B.ring_free[ri], B.big_free)
                        if hc == 0:
                            E.op("vector",
                                 (lambda rb, stb, hc: lambda eng: eng.tensor_copy(
                                     rb[:, hc * 2048:(hc + 1) * 2048],
                                     stb[:, :]))(rb, stb, hc),
                                 waits=cast_w, inc=B.c_dve)
                            rcm[hc] = (B.c_dve, B.c_dve.n)
                        else:
                            E.op("scalar",
                                 (lambda rb, stb, hc: lambda eng: eng.activation(
                                     rb[:, hc * 2048:(hc + 1) * 2048],
                                     stb[:, :], AF.Copy))(rb, stb, hc),
                                 waits=cast_w, inc=B.c_act)
                            rcm[hc] = (B.c_act, B.c_act.n)
                        B.wstg_free[sl] = rcm[hc]
                    if mode == "f32wb":
                        E.op("sync",
                             (lambda rb, idx: lambda eng: eng.dma_start(
                                 out=B.wgu_bf[idx], in_=rb[:, :]))(rb, idx),
                             waits=FR(rcm[0], rcm[1]),
                             inc=B.c_wgub, inc_by=16)
                        B.wgu_wb[idx] = (B.c_wgub, B.c_wgub.n)
                        wb_mark = (B.c_wgub, B.c_wgub.n)
                    else:
                        wb_mark = None
                    wlo, whi = rcm[0], rcm[1]
                else:
                    E.op("sync",
                         (lambda rb, idx: lambda eng: eng.dma_start(
                             out=rb[:, :], in_=B.wgu_bf[idx]))(rb, idx),
                         waits=FR(B.ring_free[ri], B.big_free,
                                  B.wgu_wb[idx]),
                         inc=B.c_ring[ri], inc_by=16)
                    wlo = whi = (B.c_ring[ri], B.c_ring[ri].n)
                    wb_mark = None
                bank = gu
                for k in range(cfg.DMC):
                    E.op("tensor",
                         (lambda rb, kk, bank, first, last: lambda eng: eng.matmul(
                             B.ps6[:, bank * 512:bank * 512 + Wd],
                             rb[:, kk * 128:(kk + 1) * 128],
                             xl[:, kk * Wd:(kk + 1) * Wd],
                             start=first, stop=last))(rb, k, bank, k == 0,
                                                      k == cfg.DMC - 1),
                         waits=([wlo if k < 16 else whi, (B.c_lnt, ln_need)]
                                + (FR(B.bank_free.get(bank)) if k == 0 else [])),
                         inc=(B.c_mm if k == cfg.DMC - 1 else None))
                mmk[gu] = B.c_mm.n
                B.ring_free[ri] = FR(mark(B.c_mm), wb_mark)
            if B.pending_casts:
                kind, ci, cdst, csrc = B.pending_casts.pop(0)
                ch = B.c_wcg[B.cast_rot % 4]
                B.cast_rot += 1
                thr = ([(ch, ch.n)] if ch.n > 0 else []) + [(B.c_mm, mmk[0])]
                E.op("gpsimd",
                     (lambda d, s: lambda eng: eng.dma_start(out=d, in_=s))(
                         cdst, csrc),
                     waits=thr, inc=ch, inc_by=16)
                B.wcast_mark[(kind, ci)] = (ch, ch.n)
            ssl = slot % 2
            E.op("scalar",
                 (lambda ssl, i: lambda eng: eng.activation(
                     B.s_silu[ssl][:, :], B.ps6[:, 0:Wd], AF.Silu,
                     bias=B.s_biasc[:, 2 * i:2 * i + 1]))(ssl, i),
                 waits=[(B.c_mm, mmk[0])] + FR(B.silu_free[ssl]), inc=B.c_act)
            B.bank_free[0] = mark(B.c_act)
            silu_done = B.c_act.n
            E.op("vector",
                 (lambda ssl, i, slot: lambda eng: eng.scalar_tensor_tensor(
                     B.s_prod[:, slot * Wd:(slot + 1) * Wd],
                     B.ps6[:, 512:512 + Wd],
                     B.s_biasc[:, 2 * i + 1:2 * i + 2], B.s_silu[ssl][:, :],
                     ALU.add, ALU.mult))(ssl, i, slot),
                 waits=[(B.c_mm, mmk[1]), (B.c_act, silu_done)]
                 + FR(getattr(B, "prod_free", None), B.big_free), inc=B.c_dve)
            B.bank_free[1] = mark(B.c_dve)
            B.silu_free[ssl] = mark(B.c_dve)
        prod_done = B.c_dve.n
        if cfg.stop == "prod" and not getattr(B, "_dumped", False):
            B._dumped = True
            E.op("sync", lambda eng: eng.dma_start(out=B.dbg, in_=B.s_prod[:, :]),
                 waits=[(B.c_dve, prod_done)], inc=B.c_xp[0], inc_by=16)
            return
        B.xlnt_read_done = mark(B.c_mm)

        ts_mm_done = {}
        BB = max(1, min(8, cfg.DM // occ))
        for p in range(n_pass):
            for b0 in range(0, len(blocks), BB):
                bgrp = blocks[b0:b0 + BB]
                nb = len(bgrp)
                buf = (b0 // BB) % 2
                wdb = B.s_wd[buf]
                i0 = bgrp[0]
                E.op("sync",
                     (lambda wdb, i0, nb, p: lambda eng: eng.dma_start(
                         out=wdb[:, 0:nb * occ].rearrange(
                             "p (b u) -> p b u", u=occ),
                         in_=B.wdn_bf[i0:i0 + nb, :, p * occ:(p + 1) * occ]
                         .rearrange("b p u -> p b u")))(wdb, i0, nb, p),
                     waits=FR(B.wcast_mark[("dn", bgrp[-1])]) + FR(B.wd_free[buf]),
                     inc=B.c_wdl[buf], inc_by=16)
                wldn = (B.c_wdl[buf], B.c_wdl[buf].n)
                for bn, i in enumerate(bgrp):
                    bi = b0 + bn
                    last_bi = bi == len(blocks) - 1
                    for ts in range(cfg.TS):
                        base = ts * (occ // 512)
                        for oc2 in range(occ // 512):
                            tail = (ts == cfg.TS - 1 and oc2 == occ // 512 - 1
                                    and bn == nb - 1) \
                                or (last_bi and oc2 == occ // 512 - 1)
                            E.op("tensor",
                                 (lambda wdb, slot, bn, ts, oc2, base, first, stp:
                                  lambda eng: eng.matmul(
                                      B.ps6[:, (base + oc2) * 512:(base + oc2 + 1) * 512],
                                      B.s_prod[:, slot * Wd + ts * 128:
                                               slot * Wd + ts * 128 + 128],
                                      wdb[:, (bn * (occ // 512) + oc2) * 512:
                                          (bn * (occ // 512) + oc2 + 1) * 512],
                                      start=first, stop=stp))(wdb, bi, bn, ts, oc2,
                                                              base, bi == 0,
                                                              last_bi and not last_ih),
                                 waits=([wldn, (B.c_dve, prod_done)]
                                        + (FR(B.bank_free.get(base + oc2)) if bi == 0 else [])),
                                 inc=(B.c_mm if tail else None))
                        if last_bi:
                            ts_mm_done[ts] = B.c_mm.n
                B.wd_free[buf] = mark(B.c_mm)
            for ts in range(cfg.TS):
                base = ts * (occ // 512)
                if last_ih:
                    for oc2 in range(occ // 512):
                        E.op("tensor",
                             (lambda p, oc2, base: lambda eng: eng.matmul(
                                 B.ps6[:, (base + oc2) * 512:(base + oc2 + 1) * 512],
                                 B.s_ones_r[:, :],
                                 B.s_brow[:, p * occ + oc2 * 512:
                                          p * occ + (oc2 + 1) * 512],
                                 start=False, stop=True))(p, oc2, base),
                             waits=FR(B.brow_ready),
                             inc=(B.c_mm if oc2 == occ // 512 - 1 else None))
                    mm_done = B.c_mm.n
                else:
                    mm_done = ts_mm_done[ts]
                xslot = ts % 4
                xb = B.s_xp[xslot]
                row0 = half * Wd + ts * 128
                sx = src_res if ih == 0 else B.xmid
                dx = dst if last_ih else B.xmid
                cx = B.c_xp[xslot]
                E.op("scalar",
                     (lambda xb, row0, p, sx: lambda eng: eng.dma_start(
                         out=xb[:, 0:occ], in_=sx[row0:row0 + 128,
                                                  p * occ:(p + 1) * occ]))(xb, row0, p, sx),
                     waits=FR(B.xp_free[xslot],
                              src_ready if ih == 0 else mid_mark.get((p, ts))),
                     inc=cx, inc_by=16)
                ldn = cx.n
                E.op("vector",
                     (lambda xb, base: lambda eng: eng.scalar_tensor_tensor(
                         xb[:, 0:occ], B.ps6[:, base * 512:base * 512 + occ],
                         0.5, xb[:, 0:occ], ALU.mult, ALU.add))(xb, base),
                     waits=[(B.c_mm, mm_done), (cx, ldn)], inc=B.c_dve)
                for oc2 in range(occ // 512):
                    B.bank_free[base + oc2] = mark(B.c_dve)
                stt = B.c_dve.n
                E.op("scalar",
                     (lambda xb, row0, p, dx: lambda eng: eng.dma_start(
                         out=dx[row0:row0 + 128, p * occ:(p + 1) * occ],
                         in_=xb[:, 0:occ]))(xb, row0, p, dx),
                     waits=[(B.c_dve, stt)], inc=cx, inc_by=16)
                B.xp_free[xslot] = mark(cx)
                dst_marks.append(mark(cx))
                if not last_ih:
                    mid_mark[(p, ts)] = mark(cx)
        B.prod_free = mark(B.c_mm)
    B.big_free = mark(B.c_mm)
    B.dst_done = dst_marks
    if B.ffn_call == 1:
        B.stage_done = FR(B.wstg_free[0], B.wstg_free[1])
    B.ffn_call += 1


def emit_qkv_half(B, half):
    E, cfg = B.E, B.cfg
    Wd = cfg.W
    xl = B.s_xlnt
    ln_need = B.ln_ready
    NH, NKV = cfg.NH, cfg.NKV
    if half == 0:
        E.op("sync", lambda eng: eng.dma_start(out=B.s_cosq[:, :], in_=B.cos_q),
             waits=FR(B.big_free), inc=B.c_const, inc_by=16)
        E.op("sync", lambda eng: eng.dma_start(out=B.s_sinq[:, :], in_=B.sin_q),
             waits=FR(B.big_free), inc=B.c_const, inc_by=16)
        B.ropeq_n = B.c_const.n
    E.op("vector", lambda eng: eng.memset(B.s_small[0:1, 64:80], 0.0),
         waits=FR(B.small_free), inc=B.c_dve)

    for j in range(cfg.NQKV):
        wdb = B.s_wd[j % 2]
        if half == 0:
            E.op("gpsimd",
                 (lambda wdb, j: lambda eng: eng.dma_start(
                     out=wdb[:, :], in_=B.wqkv[j]))(wdb, j),
                 waits=FR(B.wd_free[j % 2]),
                 inc=B.c_qv[j % 2], inc_by=16)
            wldn = (B.c_qv[j % 2], B.c_qv[j % 2].n)
            E.op("sync",
                 (lambda wdb, j: lambda eng: eng.dma_start(
                     out=B.wqkv_bf[j], in_=wdb[:, :]))(wdb, j),
                 waits=[wldn], inc=B.c_wgub, inc_by=16)
            B.wqkv_wb[j] = (B.c_wgub, B.c_wgub.n)
            wb_mark = (B.c_wgub, B.c_wgub.n)
        else:
            E.op("sync",
                 (lambda wdb, j: lambda eng: eng.dma_start(
                     out=wdb[:, :], in_=B.wqkv_bf[j]))(wdb, j),
                 waits=FR(B.wd_free[j % 2], B.wqkv_wb[j]),
                 inc=B.c_qv[j % 2], inc_by=16)
            wldn = (B.c_qv[j % 2], B.c_qv[j % 2].n)
            wb_mark = None
        bank = j % 3
        for k in range(cfg.DMC):
            E.op("tensor",
                 (lambda wdb, kk, bank, first, last: lambda eng: eng.matmul(
                     B.ps6[:, bank * 512:bank * 512 + Wd],
                     wdb[:, kk * 128:(kk + 1) * 128],
                     xl[:, kk * Wd:(kk + 1) * Wd],
                     start=first, stop=last))(wdb, k, bank, k == 0, k == cfg.DMC - 1),
                 waits=([wldn, (B.c_lnt, ln_need)]
                        + (FR(B.bank_free.get(bank)) if k == 0 else [])),
                 inc=(B.c_mm if k == cfg.DMC - 1 else None))
        mmn = B.c_mm.n
        B.wd_free[j % 2] = FR(mark(B.c_mm), wb_mark)
        ps = B.ps6[:, bank * 512:bank * 512 + Wd]
        bcol = B.s_bqkvc[:, j:j + 1]

        if j < NH + NKV:
            if j < NH:
                cosT, sinT = B.s_cosq, B.s_sinq
                tabw = [(B.c_const, B.ropeq_n)]
            else:
                cosT, sinT = B.s_cosk, B.s_sink
                tabw = [(B.c_const, B.const_n)] + FR(getattr(B, "costab_alive", None))
            cs = cosT[:, half * Wd:(half + 1) * Wd]
            sn = sinT[:, half * Wd:(half + 1) * Wd]
            bswp = B.s_bqkvs[:, j:j + 1]
            E.op("vector",
                 (lambda ps, bcol, cs: lambda eng: eng.scalar_tensor_tensor(
                     B.s_rsc[1][:, :], ps, bcol, cs, ALU.add, ALU.mult))(ps, bcol, cs),
                 waits=[(B.c_mm, mmn)] + FR(B.rsc_free[1]) + tabw, inc=B.c_dve)
            m1_done = B.c_dve.n
            E.op("vector",
                 (lambda ps, bswp, sn: lambda eng: (
                     eng.scalar_tensor_tensor(
                         B.s_rsc[2][0:64, :], ps[64:128, :], bswp[0:64, :],
                         sn[0:64, :], ALU.add, ALU.mult),
                     eng.scalar_tensor_tensor(
                         B.s_rsc[2][64:128, :], ps[0:64, :], bswp[64:128, :],
                         sn[64:128, :], ALU.add, ALU.mult))[-1])(ps, bswp, sn),
                 waits=[(B.c_mm, mmn)] + FR(B.rsc_free[2]) + tabw, inc=B.c_dve)
            m2_done = B.c_dve.n
            B.bank_free[bank] = mark(B.c_dve)
            if j < NH:
                dstap = B.s_qt[:, j * cfg.LT + half * Wd:j * cfg.LT + (half + 1) * Wd]
                extra = FR(B.big_free)
            else:
                dstap = B.s_kstg[:, :]
                extra = FR(B.kstg_free)
            E.op("vector",
                 (lambda dstap: lambda eng: eng.tensor_add(
                     dstap, B.s_rsc[1][:, :], B.s_rsc[2][:, :]))(dstap),
                 waits=[(B.c_dve, m2_done)] + extra, inc=B.c_dve)
            rot_done = B.c_dve.n
            B.rsc_free[1] = mark(B.c_dve)
            B.rsc_free[2] = mark(B.c_dve)
            E.op("scalar",
                 (lambda dstap: lambda eng: eng.activation(
                     B.s_sq[:, :], dstap, AF.Square))(dstap),
                 waits=[(B.c_dve, rot_done)] + FR(B.sq_free), inc=B.c_act)
            sq_done = B.c_act.n
            E.op("tensor",
                 lambda eng: eng.matmul(B.ps6[0:1, 1536:1536 + Wd],
                                        B.s_ones_c[:, :], B.s_sq[:, :],
                                        start=True, stop=True),
                 waits=[(B.c_act, sq_done)] + FR(B.bank_free.get(3)), inc=B.c_mm)
            nrm_mm = B.c_mm.n
            B.sq_free = mark(B.c_mm)
            col = half * NH + j if j < NH else 64 + (j - NH)
            E.op("vector",
                 (lambda col: lambda eng: eng.reduce_max(
                     B.s_small[0:1, col:col + 1], B.ps6[0:1, 1536:1536 + Wd],
                     axis=AX.X))(col),
                 waits=[(B.c_mm, nrm_mm)], inc=B.c_dve)
            B.bank_free[3] = mark(B.c_dve)
            if j >= NH:
                E.op("sync",
                     (lambda j, half: lambda eng: eng.dma_start(
                         out=B.kt_in[half][(j - NH) * 128:(j - NH + 1) * 128, :],
                         in_=B.s_kstg[:, :]))(j, half),
                     waits=[(B.c_dve, rot_done)], inc=B.c_kstg, inc_by=16)
                B.kstg_free = [mark(B.c_kstg), mark(B.c_act)]
        else:
            cidx = j - NH - NKV
            E.op("vector",
                 (lambda ps, bcol: lambda eng: eng.tensor_scalar(
                     B.s_kstg[:, :], ps, bcol, None, ALU.add))(ps, bcol),
                 waits=[(B.c_mm, mmn)] + FR(B.kstg_free), inc=B.c_dve)
            ev_done = B.c_dve.n
            B.bank_free[bank] = mark(B.c_dve)
            for ts in range(cfg.TS):
                E.op("tensor",
                     (lambda ts: lambda eng: eng.transpose(
                         B.psb[:, ts * 128:(ts + 1) * 128],
                         B.s_kstg[:, ts * 128:(ts + 1) * 128],
                         B.s_ident[:, :]))(ts),
                     waits=[(B.c_dve, ev_done)] + FR(B.psb_free),
                     inc=(B.c_mm if ts == cfg.TS - 1 else None))
            tr_done = B.c_mm.n
            B.kstg_free = mark(B.c_mm)

            def evacv(eng, cidx=cidx):
                dstv = B.s_vstg[:, :].rearrange(
                    "p (t c u) -> p t c u", c=cfg.NKV, u=128)[:, :, cidx, :]
                srcv = B.psb[:, 0:cfg.TS * 128].rearrange("p (t u) -> p t u", u=128)
                return eng.tensor_copy(dstv, srcv)
            E.op("vector", evacv,
                 waits=[(B.c_mm, tr_done)] + FR(B.vstg_free if cidx == 0 else None),
                 inc=B.c_dve)
            B.psb_free = mark(B.c_dve)
            if cidx == cfg.NKV - 1:
                vdone = B.c_dve.n
                for ts in range(cfg.TS):
                    E.op("sync",
                         (lambda ts, half: lambda eng: eng.dma_start(
                             out=B.v_in[half][ts * 128:(ts + 1) * 128, :],
                             in_=B.s_vstg[:, ts * cfg.NKV * 128:
                                          (ts + 1) * cfg.NKV * 128]))(ts, half),
                         waits=[(B.c_dve, vdone)], inc=B.c_vstg, inc_by=16)
                B.vstg_free = mark(B.c_vstg)
    E.op("sync",
         (lambda half: lambda eng: eng.dma_start(
             out=B.kn_in[half][:, :], in_=B.s_small[0:1, 64:80]))(half),
         waits=[(B.c_dve, B.c_dve.n)], inc=B.c_kn, inc_by=16)
    B.small_free = mark(B.c_kn)
    B.qkv_ctrb = [mark(B.c_kstg), mark(B.c_vstg), mark(B.c_kn)]
    B.qt_done[half] = mark(B.c_dve)
    B.costab_alive = None
    if half == 1:
        B.cosk_write_ok = mark(B.c_dve)
        B.xlnt_write_ok_att = mark(B.c_mm)


def emit_gather(B, half):
    marks = B.qkv_ctrb
    pairs = [(B.kt_in[half], B.kt_g[half]), (B.v_in[half], B.v_g[half]),
             (B.kn_in[half], B.kn_g[half])]
    for i, (src_t, dst_t) in enumerate(pairs):
        B.E.op("gpsimd",
               (lambda s, d: lambda eng: eng.collective_compute(
                   "AllGather", ALU.bypass,
                   replica_groups=[list(range(NCORES))],
                   ins=[s[:, :]], outs=[d[:, :]]))(src_t, dst_t),
               waits=FR(marks[i]), inc=B.c_cc[half][i])


def emit_kv_prefetch(B, half, g, want_k=True, want_v=True):
    E, cfg = B.E, B.cfg
    L, TS = cfg.L, cfg.TS
    buf = g % 2
    ksb, vsb = B.s_ksb[buf], B.s_vsb[buf]
    if want_k:
        kex = (FR(B.ksb_free[buf])
               + (FR(B.cosk_write_ok) if buf == 0 else []))
        for r in range(NCORES):
            E.op("gpsimd",
                 (lambda half, g, r, ksb: lambda eng: eng.dma_start(
                     out=ksb[:, r * L:(r + 1) * L],
                     in_=B.kt_g[half][r * cfg.NKV * 128 + g * 128:
                                      r * cfg.NKV * 128 + (g + 1) * 128, :]))(
                     half, g, r, ksb),
                 waits=[(B.c_cc[half][0], 1)] + kex,
                 inc=B.c_ksb[buf], inc_by=16)
        B.kv_pre.add((half, g, 'k'))
    if want_v:
        vex = FR(B.vsb_free[buf], B.stage_done)
        for r in range(NCORES):
            for ts in range(TS):
                E.op("gpsimd",
                     (lambda half, g, r, ts, vsb: lambda eng: eng.dma_start(
                         out=vsb[:, (r * TS + ts) * 128:(r * TS + ts + 1) * 128],
                         in_=B.v_g[half][r * L + ts * 128:r * L + (ts + 1) * 128,
                                         g * 128:(g + 1) * 128]))(half, g, r, ts, vsb),
                     waits=[(B.c_cc[half][1], 1)] + vex,
                     inc=B.c_vsb[buf], inc_by=16)
        B.kv_pre.add((half, g, 'v'))


def emit_attention_half(B, half):
    E, cfg = B.E, B.cfg
    L, MT, TS, Wd = cfg.L, cfg.MT, cfg.TS, cfg.W
    E.op("sync",
         (lambda half: lambda eng: eng.dma_start(
             out=B.s_drow[0:1, 0:16 * NCORES],
             in_=B.kn_g[half][:, :].rearrange("(o r) c -> o (r c)", o=1)))(half),
         waits=[(B.c_cc[half][2], 1)] + FR(B.drow_free), inc=B.c_attld, inc_by=16)
    knl = B.c_attld.n
    E.op("vector",
         (lambda half: lambda eng: (
             eng.reduce_max(B.s_small[0:1, 90:91], B.s_drow[0:1, 0:16 * NCORES],
                            axis=AX.X),
             eng.reduce_max(B.s_small[0:1, 91:92],
                            B.s_small[0:1, half * cfg.NH:half * cfg.NH + cfg.NH],
                            axis=AX.X))[-1])(half),
         waits=[(B.c_attld, knl)] + [B.qt_done[half]], inc=B.c_dve)
    mx = B.c_dve.n
    E.op("scalar",
         lambda eng: (eng.activation(B.s_small[0:1, 92:93], B.s_small[0:1, 90:91], AF.Ln),
                      eng.activation(B.s_small[0:1, 93:94], B.s_small[0:1, 91:92], AF.Ln))[-1],
         waits=[(B.c_dve, mx)], inc=B.c_act)
    lns = B.c_act.n
    E.op("vector",
         lambda eng: eng.tensor_add(B.s_small[0:1, 94:95], B.s_small[0:1, 92:93],
                                    B.s_small[0:1, 93:94]),
         waits=[(B.c_act, lns)], inc=B.c_dve)
    E.op("vector",
         lambda eng: eng.tensor_scalar(B.s_small[0:1, 94:95], B.s_small[0:1, 94:95],
                                       0.5, None, ALU.mult),
         inc=B.c_dve)
    addn = B.c_dve.n
    E.op("scalar",
         lambda eng: eng.activation(B.s_small[0:1, 95:96], B.s_small[0:1, 94:95],
                                    AF.Exp),
         waits=[(B.c_dve, addn)], inc=B.c_act)
    bn = B.c_act.n
    E.op("vector",
         lambda eng: eng.tensor_scalar(B.s_small[0:1, 96:97], B.s_small[0:1, 95:96],
                                       -40.0, 0.0, ALU.add, ALU.max),
         waits=[(B.c_act, bn)], inc=B.c_dve)
    E.op("vector",
         lambda eng: eng.tensor_scalar(B.s_small[0:1, 96:97], B.s_small[0:1, 96:97],
                                       -1.0, None, ALU.mult),
         inc=B.c_dve)
    negn = B.c_dve.n
    E.op("tensor",
         lambda eng: eng.matmul(B.ps6[:, 0:1], B.s_ones_rf[:, :],
                                B.s_small[0:1, 96:97], start=True, stop=True),
         waits=[(B.c_dve, negn)] + FR(B.bank_free.get(0)), inc=B.c_mm)
    bcmm = B.c_mm.n
    E.op("vector",
         lambda eng: eng.tensor_copy(B.s_negc0[:, :], B.ps6[:, 0:1]),
         waits=[(B.c_mm, bcmm)] + FR(B.negc0_free), inc=B.c_dve)
    negc0_ready = mark(B.c_dve)
    B.bank_free[0] = mark(B.c_dve)
    B.drow_free = mark(B.c_dve)

    groups_sr = [(s, rp) for s in range(MT) for rp in range(NCORES // 2)]
    NG = len(groups_sr)
    if B.pv_free[1] is None:
        B.pv_free[1] = FR(B.psb_free)

    def sum_acc(q, lo):
        return (B.ps6[:, 2048 + lo:2048 + Wd] if q == 0
                else B.ps67[:, lo:Wd])

    def sum_full(q):
        return B.ps6[:, 2048:2048 + Wd] if q == 0 else B.ps67[:, 0:Wd]

    def pv_acc(q, lo):
        return (B.ps6[:, 2560 + lo:2560 + Wd] if q == 0
                else B.ps67[:, 512 + lo:1024])

    for g in range(cfg.NKV):
        buf = g % 2
        ksb, vsb = B.s_ksb[buf], B.s_vsb[buf]
        if (half, g, 'k') not in B.kv_pre:
            kex = (FR(B.ksb_free[buf])
                   + (FR(B.cosk_write_ok) if buf == 0 else []))
            for r in range(NCORES):
                E.op("sync",
                     (lambda half, g, r, ksb: lambda eng: eng.dma_start(
                         out=ksb[:, r * L:(r + 1) * L],
                         in_=B.kt_g[half][r * cfg.NKV * 128 + g * 128:
                                          r * cfg.NKV * 128 + (g + 1) * 128, :]))(
                         half, g, r, ksb),
                     waits=[(B.c_cc[half][0], 1)] + kex,
                     inc=B.c_ksb[buf], inc_by=16)
        if (half, g, 'v') not in B.kv_pre:
            vex = FR(B.vsb_free[buf], B.stage_done)
            for r in range(NCORES):
                for ts in range(TS):
                    E.op("sync",
                         (lambda half, g, r, ts, vsb: lambda eng: eng.dma_start(
                             out=vsb[:, (r * TS + ts) * 128:(r * TS + ts + 1) * 128],
                             in_=B.v_g[half][r * L + ts * 128:r * L + (ts + 1) * 128,
                                             g * 128:(g + 1) * 128]))(half, g, r, ts, vsb),
                         waits=[(B.c_cc[half][1], 1)] + vex,
                         inc=B.c_vsb[buf], inc_by=16)
        kld = (B.c_ksb[buf], B.c_ksb[buf].n)
        vld = (B.c_vsb[buf], B.c_vsb[buf].n)
        for h in range(g * cfg.GRP, (g + 1) * cfg.GRP):
            q = B.head_par & 1
            B.head_par += 1
            mskn = {}

            def emit_scores(gi):
                s, rp = groups_sr[gi]
                span = (MT - s) * 128
                sb0 = 2 * (gi % 2)
                pb = gi % 3
                prb = B.s_probs[pb]
                scm = [0, 0]
                for r2 in range(2):
                    r = 2 * rp + r2
                    E.op("tensor",
                         (lambda ksb, h, half, r, s, span, bank:
                          lambda eng: eng.matmul(
                              B.ps6[:, bank * 512:bank * 512 + span],
                              ksb[:, r * L + s * 128:r * L + (s + 1) * 128],
                              B.s_qt[:, h * cfg.LT + half * Wd + s * 128:
                                     h * cfg.LT + (half + 1) * Wd],
                              start=True, stop=True))(ksb, h, half, r, s, span,
                                                      sb0 + r2),
                         waits=[kld, B.qt_done[half]]
                         + FR(B.bank_free.get(sb0 + r2)),
                         inc=B.c_mm)
                    scm[r2] = B.c_mm.n
                B._sc_mm_last = scm[1]
                for r2 in range(2):
                    E.op("scalar",
                         (lambda prb, sb0, r2, span: lambda eng: eng.activation(
                             prb[:, r2 * span:(r2 + 1) * span],
                             B.ps6[:, (sb0 + r2) * 512:(sb0 + r2) * 512 + span],
                             AF.Exp, bias=B.s_negc0[:, :]))(prb, sb0, r2, span),
                         waits=[(B.c_mm, scm[r2]), negc0_ready]
                         + (FR(B.probs_free[pb]) if r2 == 0 else []),
                         inc=B.c_act)
                    expn = B.c_act.n
                    B.bank_free[sb0 + r2] = mark(B.c_act)
                    E.op("vector",
                         (lambda prb, rp, r2, span: lambda eng: eng.tensor_mul(
                             prb[:, r2 * span:r2 * span + 128],
                             prb[:, r2 * span:r2 * span + 128],
                             B.s_maskk[:, (2 * rp + r2) * 128:
                                       (2 * rp + r2 + 1) * 128]))(prb, rp, r2, span),
                         waits=[(B.c_act, expn)], inc=B.c_dve)
                    mskn[(gi, r2)] = B.c_dve.n

            def emit_sumpv(gi, first):
                s, rp = groups_sr[gi]
                span = (MT - s) * 128
                pb = gi % 3
                prb = B.s_probs[pb]
                last_g = gi == NG - 1
                for r2 in range(2):
                    r = 2 * rp + r2
                    stp = last_g and r2 == 1
                    st = first and r2 == 0
                    w = [(B.c_dve, mskn[(gi, r2)]), vld]
                    if st:
                        w = w + FR(B.pv_free[q])
                    E.op("tensor",
                         (lambda prb, q, r2, s, span, st, stp:
                          lambda eng: eng.matmul(
                              sum_acc(q, s * 128), B.s_ones_m[:, :],
                              prb[:, r2 * span:(r2 + 1) * span],
                              start=st, stop=stp))(prb, q, r2, s, span, st, stp),
                         waits=w, inc=None)
                    E.op("tensor",
                         (lambda prb, vsb, q, r, r2, s, span, st, stp:
                          lambda eng: eng.matmul(
                              pv_acc(q, s * 128),
                              vsb[:, (r * TS + s) * 128:(r * TS + s + 1) * 128],
                              prb[:, r2 * span:(r2 + 1) * span],
                              start=st, stop=stp))(prb, vsb, q, r, r2, s, span,
                                                   st, stp),
                         waits=w, inc=(B.c_mm if r2 == 1 else None))
                B.probs_free[pb] = mark(B.c_mm)

            for gi in range(NG):
                emit_scores(gi)
                if gi == 1 and B.norm_q:
                    B.norm_q.pop(0)()
                if gi >= 1:
                    emit_sumpv(gi - 1, first=(gi - 1 == 0))
            emit_sumpv(NG - 1, first=(NG == 1))
            B._pv_mm = B.c_mm.n
            pvn = B.c_mm.n

            def norm(h=h, q=q, pvn=pvn):
                E.op("vector",
                     (lambda q: lambda eng: eng.reciprocal(
                         B.s_dbc[:, :], sum_full(q)))(q),
                     waits=[(B.c_mm, pvn)] + FR(B.dbc_free), inc=B.c_dve)
                rcn = B.c_dve.n
                E.op("vector",
                     (lambda h, q: lambda eng: eng.tensor_mul(
                         B.s_att[:, h * Wd:(h + 1) * Wd],
                         pv_acc(q, 0), B.s_dbc[:, :]))(h, q),
                     waits=[(B.c_dve, rcn)] + FR(B.att_write_ok), inc=B.c_dve)
                B.dbc_free = mark(B.c_dve)
                B.pv_free[q] = mark(B.c_dve)
            B.norm_q.append(norm)
        B.ksb_free[buf] = (B.c_mm, B._sc_mm_last)
        B.vsb_free[buf] = (B.c_mm, B._pv_mm)
    while B.norm_q:
        B.norm_q.pop(0)()
    B.att_done = B.c_dve.n
    B.negc0_free = mark(B.c_act)
    B.psb_free = FR(B.pv_free[1])


def emit_wo_half(B, half, src_res, dst, src_ready):
    E, cfg = B.E, B.cfg
    Wd = cfg.W
    occ = cfg.XP
    n_pass = cfg.DM // occ
    att_need = B.att_done
    dst_marks = []
    BB = max(1, min(8, cfg.DM // occ))
    for p in range(n_pass):
        for k0 in range(0, cfg.DMC, BB):
            nb = min(BB, cfg.DMC - k0)
            buf = (k0 // BB) % 2
            wdb = B.s_wd[buf]
            E.op("sync",
                 (lambda wdb, k0, nb, p: lambda eng: eng.dma_start(
                     out=wdb[:, 0:nb * occ].rearrange("p (b u) -> p b u", u=occ),
                     in_=B.wo_bf[k0:k0 + nb, :, p * occ:(p + 1) * occ]
                     .rearrange("b p u -> p b u")))(wdb, k0, nb, p),
                 waits=FR(B.wcast_mark[("wo", k0 + nb - 1)]) + FR(B.wd_free[buf]),
                 inc=B.c_wdl[buf], inc_by=16)
            wldn = (B.c_wdl[buf], B.c_wdl[buf].n)
            for bn in range(nb):
                k = k0 + bn
                for ts in range(cfg.TS):
                    base = ts * (occ // 512)
                    for oc2 in range(occ // 512):
                        tail = (ts == cfg.TS - 1 and oc2 == occ // 512 - 1
                                and bn == nb - 1)
                        E.op("tensor",
                             (lambda wdb, k, bn, ts, oc2, base, first:
                              lambda eng: eng.matmul(
                                  B.ps6[:, (base + oc2) * 512:(base + oc2 + 1) * 512],
                                  B.s_att[:, k * Wd + ts * 128:k * Wd + ts * 128 + 128],
                                  wdb[:, (bn * (occ // 512) + oc2) * 512:
                                      (bn * (occ // 512) + oc2 + 1) * 512],
                                  start=first, stop=False))(wdb, k, bn, ts, oc2,
                                                            base, k == 0),
                             waits=([wldn, (B.c_dve, att_need)]
                                    + (FR(B.bank_free.get(base + oc2)) if k == 0 else [])),
                             inc=(B.c_mm if tail else None))
            B.wd_free[buf] = mark(B.c_mm)
        for ts in range(cfg.TS):
            base = ts * (occ // 512)
            for oc2 in range(occ // 512):
                E.op("tensor",
                     (lambda p, oc2, base: lambda eng: eng.matmul(
                         B.ps6[:, (base + oc2) * 512:(base + oc2 + 1) * 512],
                         B.s_ones_r[:, :],
                         B.s_brow[:, p * occ + oc2 * 512:p * occ + (oc2 + 1) * 512],
                         start=False, stop=True))(p, oc2, base),
                     waits=FR(B.brow_ready),
                     inc=(B.c_mm if oc2 == occ // 512 - 1 else None))
            mm_done = B.c_mm.n
            xslot = ts % 4
            xb = B.s_xp[xslot]
            cx = B.c_xp[xslot]
            row0 = half * Wd + ts * 128
            E.op("scalar",
                 (lambda xb, row0, p: lambda eng: eng.dma_start(
                     out=xb[:, 0:occ], in_=src_res[row0:row0 + 128,
                                                   p * occ:(p + 1) * occ]))(xb, row0, p),
                 waits=FR(B.xp_free[xslot], src_ready), inc=cx, inc_by=16)
            ldn = cx.n
            E.op("vector",
                 (lambda xb, base: lambda eng: eng.scalar_tensor_tensor(
                     xb[:, 0:occ], B.ps6[:, base * 512:base * 512 + occ],
                     1.0, xb[:, 0:occ], ALU.mult, ALU.add))(xb, base),
                 waits=[(B.c_mm, mm_done), (cx, ldn)], inc=B.c_dve)
            for oc2 in range(occ // 512):
                B.bank_free[base + oc2] = mark(B.c_dve)
            stt = B.c_dve.n
            E.op("scalar",
                 (lambda xb, row0, p: lambda eng: eng.dma_start(
                     out=dst[row0:row0 + 128, p * occ:(p + 1) * occ],
                     in_=xb[:, 0:occ]))(xb, row0, p),
                 waits=[(B.c_dve, stt)], inc=cx, inc_by=16)
            B.xp_free[xslot] = mark(cx)
            dst_marks.append(mark(cx))
    B.att_write_ok = mark(B.c_mm)
    B.dst_done = dst_marks


def _debug_out(B, src):
    E, cfg = B.E, B.cfg
    occ = cfg.XP
    done = B.dst_done
    for t in range(2 * cfg.TS):
        for p in range(cfg.DM // occ):
            xslot = (t * (cfg.DM // occ) + p) % 2
            xb = B.s_xp[xslot]
            cx = B.c_xp[xslot]
            E.op("sync",
                 (lambda xb, t, p: lambda eng: eng.dma_start(
                     out=xb[:, 0:occ], in_=src[t * 128:(t + 1) * 128,
                                               p * occ:(p + 1) * occ]))(xb, t, p),
                 waits=FR(B.xp_free[xslot], done), inc=cx, inc_by=16)
            ldn = cx.n
            E.op("sync",
                 (lambda xb, t, p: lambda eng: eng.dma_start(
                     out=B.out[t * 128:(t + 1) * 128, p * occ:(p + 1) * occ],
                     in_=xb[:, 0:occ]))(xb, t, p),
                 waits=[(cx, ldn)], inc=cx, inc_by=16)
            B.xp_free[xslot] = mark(cx)


def emit_tail(B):
    B.E.op("sync", lambda eng: eng.nop(),
           waits=[(c, c.n) for c in B.c_xp if c.n > 0])


def emit_all(B):
    cfg = B.cfg
    dn_jobs = [("dn", i, B.wdn_bf[i], B.wdn[i]) for i in range(cfg.NIB)]
    wo_jobs = [("wo", k, B.wo_bf[k], B.wo[k]) for k in range(cfg.DMC)]
    if cfg.stop == "prod":
        emit_consts(B)
        for e in ("tensor", "vector", "scalar"):
            B.E.op(e, lambda eng: eng.nop(), waits=[(B.c_const, B.const_n)])
        emit_weight_cast(B, dn_jobs + wo_jobs)
        emit_ln(B, B.x_loc, 0, None)
        emit_ffn_half(B, 0, B.x_loc, B.x1, None)
        emit_tail(B)
        return
    emit_consts(B)
    for e in ("tensor", "vector", "scalar"):
        B.E.op(e, lambda eng: eng.nop(), waits=[(B.c_const, B.const_n)])
    B.pending_casts.extend(dn_jobs)
    for b in range(2):
        emit_ln(B, B.x_loc, b, None)
        if b == 1:
            B.pending_casts.extend(wo_jobs)
        emit_ffn_half(B, b, B.x_loc, B.x1, None)
        B.xready[("x1", b)] = B.dst_done
    if cfg.stop == "ffn1":
        _debug_out(B, B.x1)
        emit_tail(B)
        return
    for b in range(2):
        emit_ln(B, B.x1, b, B.xready[("x1", b)])
        emit_qkv_half(B, b)
        emit_gather(B, b)
        if b == 0:
            emit_kv_prefetch(B, 0, 0, want_k=False)
            emit_kv_prefetch(B, 0, 1)
    B.att_write_ok = B.xlnt_write_ok_att
    load_brow(B, B.brow_o)
    for b in range(2):
        emit_attention_half(B, b)
        emit_wo_half(B, b, B.x1, B.x2, B.xready[("x1", b)])
        B.xready[("x2", b)] = B.dst_done
    if cfg.stop == "attn":
        _debug_out(B, B.x2)
        emit_tail(B)
        return
    load_brow(B, B.brow_d)
    B.xlnt_write_ok = B.att_write_ok
    for b in range(2):
        emit_ln(B, B.x2, b, B.xready[("x2", b)])
        emit_ffn_half(B, b, B.x2, B.out, B.xready[("x2", b)])
    emit_tail(B)


def build_program(cfg):
    B = Build(cfg)
    emit_all(B)
    nc, E = B.nc, B.E
    with nc.Block() as block:
        @block.sync
        def _(eng):
            E.emit_engine("sync", eng)

        @block.scalar
        def _(eng):
            E.emit_engine("scalar", eng)

        @block.tensor
        def _(eng):
            E.emit_engine("tensor", eng)

        @block.vector
        def _(eng):
            E.emit_engine("vector", eng)

        @block.gpsimd
        def _(eng):
            E.emit_engine("gpsimd", eng)
    return B, nc


def kernel(**inputs):
    cfg = Cfg()
    in_maps = host_prepare(cfg, inputs)
    B, nc = build_program(cfg)
    res = run_bass_kernel_spmd(nc, in_maps, core_ids=list(range(NCORES)))
    out = np.empty((cfg.B, cfg.S, cfg.DM), np.float32)
    for c in range(NCORES):
        loc = np.asarray(res.results[c]["out_loc"], dtype=np.float32)
        out[:, c::NCORES, :] = loc.reshape(cfg.B, cfg.L, cfg.DM)
    return out


if __name__ == "__main__":
    cfg = Cfg()
    B, nc = build_program(cfg)
    print("built:", nc.n_instructions(), "instructions; sbuf",
          B.sbuf_used, "/", nc.SBUF_PARTITION_SIZE_BYTES)


# version 17
# speedup vs baseline: 1.0834x; 1.0178x over previous
import math
import numpy as np
import ml_dtypes

import concourse.bass as bass
import concourse.mybir as mybir
from concourse.bass_utils import run_bass_kernel_spmd

F32 = mybir.dt.float32
BF16 = mybir.dt.bfloat16
AF = mybir.ActivationFunctionType
ALU = mybir.AluOpType
AX = mybir.AxisListType
bfnp = ml_dtypes.bfloat16

NCORES = 8


class Cfg:
    def __init__(self, dm=4096, inter=11008, nh=32, nkv=8, seq=4096, batch=2,
                 xp=None):
        self.DM, self.IN, self.NH, self.NKV, self.HD = dm, inter, nh, nkv, 128
        assert nh * 128 == dm and batch == 2
        self.B, self.S = batch, seq
        self.L = seq // NCORES
        assert self.L % 128 == 0
        self.MT = self.L // 128
        self.LT = self.B * self.L
        self.W = self.L
        self.TS = self.W // 128
        self.DMC = dm // 128
        self.NIB = inter // 128
        self.NQKV = nh + 2 * nkv
        self.GRP = nh // nkv
        self.eps = 1e-5
        self.theta = 10000.0
        self.IH0 = (self.NIB + 1) // 2
        self.XP = xp if xp is not None else min(512, dm)
        self.NPC = dm // self.XP
        self.stop = "full"
        assert self.TS * (self.XP // 512) <= 4


def _img(wt):
    kin = wt.shape[0]
    return np.ascontiguousarray(
        wt.reshape(kin // 128, 128, 128).transpose(1, 0, 2).reshape(128, kin))


def host_prepare(cfg, inputs):
    f32 = np.float32
    x = np.asarray(inputs["L0_attn_x"], f32)
    w_qkv = np.asarray(inputs["L0_attn_w_qkv"], f32)
    b_qkv = np.asarray(inputs["L0_attn_b_qkv"], f32)
    w_o = np.asarray(inputs["L0_attn_w_o"], f32)
    b_o = np.asarray(inputs["L0_attn_b_o"], f32)
    w_g = np.asarray(inputs["L0_ffn_w_gate"], f32)
    b_g = np.asarray(inputs["L0_ffn_b_gate"], f32)
    w_u = np.asarray(inputs["L0_ffn_w_up"], f32)
    b_u = np.asarray(inputs["L0_ffn_b_up"], f32)
    w_d = np.asarray(inputs["L0_ffn_w_down"], f32)
    b_d = np.asarray(inputs["L0_ffn_b_down"], f32)
    DM, NH, NKV = cfg.DM, cfg.NH, cfg.NKV

    wgu = np.empty((2 * cfg.NIB, 128, DM), f32)
    for i in range(cfg.NIB):
        wgu[2 * i] = _img(np.ascontiguousarray(w_g[i * 128:(i + 1) * 128, :].T))
        wgu[2 * i + 1] = _img(np.ascontiguousarray(w_u[i * 128:(i + 1) * 128, :].T))
    wdn = np.ascontiguousarray(w_d.T).reshape(cfg.NIB, 128, DM)
    perm = np.concatenate([np.arange(0, 128, 2), np.arange(1, 128, 2)])
    wqkv = np.empty((cfg.NQKV, 128, DM), f32)
    bqkv_cols = np.empty((128, cfg.NQKV), f32)
    for j in range(cfg.NQKV):
        rows = np.arange(j * 128, (j + 1) * 128)
        if j < NH + NKV:
            rows = rows[perm]
        wqkv[j] = _img(np.ascontiguousarray(w_qkv[rows, :].T))
        bqkv_cols[:, j] = b_qkv[rows]
    bqkv_swap = np.concatenate([bqkv_cols[64:128, :], bqkv_cols[0:64, :]], 0)
    wo = np.ascontiguousarray(w_o.T).reshape(cfg.DMC, 128, DM)

    bias_cols = np.zeros((128, 2 * cfg.NIB), f32)
    for i in range(cfg.NIB):
        bias_cols[:, 2 * i] = b_g[i * 128:(i + 1) * 128]
        bias_cols[:, 2 * i + 1] = b_u[i * 128:(i + 1) * 128]
    brow_d = np.ascontiguousarray(b_d.reshape(1, DM))
    brow_o = np.ascontiguousarray(b_o.reshape(1, DM))

    in_maps = []
    inv_freq = 1.0 / cfg.theta ** (np.arange(0, 128, 2, dtype=np.float64) / 128)
    qs = 1.0 / math.sqrt(128.0)
    for c in range(NCORES):
        x_loc = np.ascontiguousarray(x[:, c::NCORES, :].reshape(cfg.LT, DM))
        pos = np.empty(cfg.LT, np.float64)
        for b in range(cfg.B):
            pos[b * cfg.L:(b + 1) * cfg.L] = (
                np.arange(cfg.L, dtype=np.float64) * NCORES + c)
        ang = inv_freq[:, None] * pos[None, :]
        cosv, sinv = np.cos(ang), np.sin(ang)
        cos_t = np.concatenate([cosv, cosv], 0).astype(f32)
        sin_t = np.concatenate([-sinv, sinv], 0).astype(f32)
        jj = np.arange(128)[:, None]
        uu = np.arange(128)[None, :]
        maskk = np.zeros((128, NCORES * 128), f32)
        for r in range(NCORES):
            maskk[:, r * 128:(r + 1) * 128] = (jj <= uu - (1 if r > c else 0))
        in_maps.append({
            "x_loc": x_loc, "wgu": wgu, "wdn": wdn, "wqkv": wqkv, "wo": wo,
            "bias_cols": bias_cols, "bqkv_cols": bqkv_cols,
            "bqkv_swap": np.ascontiguousarray(bqkv_swap),
            "brow_d": brow_d, "brow_o": brow_o,
            "cos_q": (cos_t * qs).astype(f32), "sin_q": (sin_t * qs).astype(f32),
            "cos_k": cos_t, "sin_k": sin_t,
            "maskk": maskk.astype(bfnp),
        })
    return in_maps


class Chan:
    __slots__ = ("sem", "n", "name")

    def __init__(self, nc, name):
        self.sem = nc.alloc_semaphore(name)
        self.n = 0
        self.name = name


def FR(*marks):
    out = []
    for m in marks:
        if m is None:
            continue
        if isinstance(m, list):
            out.extend(x for x in m if x is not None and x[0] is not None and x[1] > 0)
        elif m[0] is not None and m[1] > 0:
            out.append(m)
    return out


class Emit:
    ENGINES = ("sync", "scalar", "tensor", "vector", "gpsimd")

    def __init__(self, nc):
        self.nc = nc
        self.q = {e: [] for e in self.ENGINES}
        self.wm = {}
        self.self_chan = {}
        self.debug_src = False

    def chan(self, name):
        return Chan(self.nc, name)

    def op(self, eng, fn, waits=(), inc=None, inc_by=1):
        selfc = self.self_chan.get(eng)
        waits = list(waits)
        if selfc is not None and selfc.n > 0:
            waits.append((selfc, selfc.n))
        waits = tuple((c, v) for (c, v) in waits if v > 0)
        if inc is not None:
            inc.n += inc_by
        if self.debug_src:
            import traceback
            fr = traceback.extract_stack(limit=3)[1]
            self.q[eng].append((fn, waits, inc, inc_by, f"{fr.name}:{fr.lineno}"))
        else:
            self.q[eng].append((fn, waits, inc, inc_by))

    def emit_engine(self, eng_name, engine):
        wm = self.wm
        for fn, waits, inc, inc_by, *_ in self.q[eng_name]:
            for (c, v) in waits:
                key = (eng_name, c.name)
                if wm.get(key, 0) < v:
                    engine.wait_ge(c.sem, v)
                    wm[key] = v
            inst = fn(engine)
            if inc is not None:
                inst.then_inc(inc.sem, inc_by)


class Build:
    def __init__(self, cfg):
        self.cfg = cfg
        nc = bass.Bass("TRN2", target_bir_lowering=False,
                       dynamic_dma_scratch_size=16384)
        self.nc = nc
        c = cfg
        DM, LT = c.DM, c.LT
        dt = nc.dram_tensor

        self.x_loc = dt("x_loc", [LT, DM], F32, kind="ExternalInput").ap()
        self.wgu = dt("wgu", [2 * c.NIB, 128, DM], F32, kind="ExternalInput").ap()
        self.wdn = dt("wdn", [c.NIB, 128, DM], F32, kind="ExternalInput").ap()
        self.wqkv = dt("wqkv", [c.NQKV, 128, DM], F32, kind="ExternalInput").ap()
        self.wo = dt("wo", [c.DMC, 128, DM], F32, kind="ExternalInput").ap()
        self.bias_cols = dt("bias_cols", [128, 2 * c.NIB], F32, kind="ExternalInput").ap()
        self.bqkv_cols = dt("bqkv_cols", [128, c.NQKV], F32, kind="ExternalInput").ap()
        self.bqkv_swap = dt("bqkv_swap", [128, c.NQKV], F32, kind="ExternalInput").ap()
        self.brow_d = dt("brow_d", [1, DM], F32, kind="ExternalInput").ap()
        self.brow_o = dt("brow_o", [1, DM], F32, kind="ExternalInput").ap()
        self.cos_q = dt("cos_q", [128, LT], F32, kind="ExternalInput").ap()
        self.sin_q = dt("sin_q", [128, LT], F32, kind="ExternalInput").ap()
        self.cos_k = dt("cos_k", [128, LT], F32, kind="ExternalInput").ap()
        self.sin_k = dt("sin_k", [128, LT], F32, kind="ExternalInput").ap()
        self.maskk = dt("maskk", [128, NCORES * 128], BF16, kind="ExternalInput").ap()
        self.out = dt("out_loc", [LT, DM], F32, kind="ExternalOutput").ap()
        nslot_dbg = max(c.IH0, c.NIB - c.IH0)
        self.dbg = dt("dbg", [128, nslot_dbg * c.W], BF16, kind="ExternalOutput").ap()

        self.wdn_bf = dt("wdn_bf", [c.NIB, 128, DM], BF16).ap()
        self.wo_bf = dt("wo_bf", [c.DMC, 128, DM], BF16).ap()
        self.wgu_bf = dt("wgu_bf", [2 * c.NIB, 128, DM], BF16).ap()
        self.wqkv_bf = dt("wqkv_bf", [c.NQKV, 128, DM], BF16).ap()
        self.x1 = dt("x1", [LT, DM], F32).ap()
        self.xmid = dt("xmid", [LT, DM], F32).ap()
        self.x2 = dt("x2", [LT, DM], F32).ap()
        self.kt_in = [dt(f"kt_in{b}", [c.NKV * 128, c.L], BF16).ap() for b in range(2)]
        self.v_in = [dt(f"v_in{b}", [c.L, c.NKV * 128], BF16).ap() for b in range(2)]
        self.kn_in = [dt(f"kn_in{b}", [1, 16], F32).ap() for b in range(2)]
        self.kt_g = [dt(f"kt_g{b}", [NCORES * c.NKV * 128, c.L], BF16).ap() for b in range(2)]
        self.v_g = [dt(f"v_g{b}", [NCORES * c.L, c.NKV * 128], BF16).ap() for b in range(2)]
        self.kn_g = [dt(f"kn_g{b}", [NCORES, 16], F32).ap() for b in range(2)]

        self.i_ident = nc.inline_tensor(np.eye(128, dtype=bfnp), name="i_ident").ap()
        self.i_ones_r = nc.inline_tensor(np.ones((1, 128), bfnp), name="i_ones_r").ap()
        self.i_ones_c = nc.inline_tensor(np.ones((128, 1), bfnp), name="i_ones_c").ap()
        self.i_ones_m = nc.inline_tensor(np.ones((128, 128), bfnp), name="i_ones_m").ap()
        self.i_ones_rf = nc.inline_tensor(np.ones((1, 128), np.float32), name="i_ones_rf").ap()

        self._off = (nc.sbuf_base + 63) // 64 * 64
        alloc = []

        def A(name, shape, dtype, at=None):
            nbytes = int(np.prod(shape[1:])) * (2 if dtype == BF16 else 4)
            if at is None:
                at = self._off
                self._off += (nbytes + 63) // 64 * 64
            t = nc.alloc_sbuf_tensor_at(name, list(shape), dtype, offset=at)
            alloc.append((name, at, nbytes))
            setattr(self, name[0] == "s" and name or name, t)
            return t

        self.s_biasc = A("s_biasc", [128, 2 * c.NIB], F32)
        self.s_bqkvc = A("s_bqkvc", [128, c.NQKV], F32)
        self.s_bqkvs = A("s_bqkvs", [128, c.NQKV], F32)
        self.s_maskk = A("s_maskk", [128, NCORES * 128], BF16)
        self.s_ident = A("s_ident", [128, 128], BF16)
        self.s_ones_r = A("s_ones_r", [1, 128], BF16)
        self.s_ones_c = A("s_ones_c", [128, 1], BF16)
        self.s_ones_m = A("s_ones_m", [128, 128], BF16)
        self.s_ones_rf = A("s_ones_rf", [1, 128], F32)
        self.s_brow = A("s_brow", [1, DM], BF16)
        self.s_stats = A("s_stats", [128, 32], F32)
        self.s_small = A("s_small", [1, 128], F32)
        self.s_negc0 = A("s_negc0", [128, 1], F32)
        self.s_drow = A("s_drow", [1, max(c.W, 16 * NCORES)], F32)
        self.s_dbc = A("s_dbc", [128, c.W], F32)
        self.s_xp = [A(f"s_xp{i}", [128, c.XP], F32) for i in range(6)]
        self.s_probs = [A(f"s_probs{i}", [128, 1024], BF16) for i in range(3)]
        assert c.XP * 2 <= 2048
        self.s_lntok = [A(f"s_lntok{i}", [128, c.XP], BF16,
                          at=[a for a in alloc if a[0] == f"s_probs{i}"][0][1])
                        for i in range(2)]
        self.s_silu = [A(f"s_silu{i}", [128, c.W], F32) for i in range(2)]
        self.s_rsc = [A(f"s_rsc{i}", [128, c.W], F32,
                        at=[a for a in alloc if a[0] == f"s_silu{i}"][0][1])
                      for i in range(2)]
        self.s_rsc.append(A("s_rsc2", [128, c.W], F32))
        self.s_kstg = A("s_kstg", [128, c.W], BF16)
        self.s_sq = A("s_sq", [128, c.W], BF16)
        self.s_vstg = A("s_vstg", [128, c.TS * c.NKV * 128], BF16)
        self.s_wd = [A(f"s_wd{i}", [128, DM], BF16) for i in range(2)]
        self.s_xlnt = A("s_xlnt", [128, c.DMC * c.W], BF16)
        xlnt_off = [a for a in alloc if a[0] == "s_xlnt"][0][1]
        self.s_att = A("s_att", [128, c.DMC * c.W], BF16, at=xlnt_off)

        nslot = max(c.IH0, c.NIB - c.IH0)
        prod_sz = ((nslot * c.W * 2 + 63) // 64) * 64
        ring_sz = 4 * DM * 2
        qt_sz = c.NH * LT * 2
        big0 = self._off
        self._off += max(prod_sz + ring_sz, qt_sz + 2 * LT * 4)
        self.s_prod = A("s_prod", [128, nslot * c.W], BF16, at=big0)
        self.s_ring = [A(f"s_ring{i}", [128, DM], BF16, at=big0 + prod_sz + i * DM * 2)
                       for i in range(4)]
        self.s_qt = A("s_qt", [128, c.NH * LT], BF16, at=big0)
        self.s_cosq = A("s_cosq", [128, LT], F32, at=big0 + qt_sz)
        self.s_sinq = A("s_sinq", [128, LT], F32, at=big0 + qt_sz + LT * 4)

        kv0 = self._off
        ksz = ((NCORES * c.L * 2 + 63) // 64) * 64
        self._off += 4 * ksz
        self.s_ksb = [A(f"s_ksb{i}", [128, NCORES * c.L], BF16, at=kv0 + i * ksz)
                      for i in range(2)]
        self.s_vsb = [A(f"s_vsb{i}", [128, NCORES * c.L], BF16, at=kv0 + (2 + i) * ksz)
                      for i in range(2)]
        assert 2 * ksz >= 2 * LT * 4
        self.s_cosk = A("s_cosk", [128, LT], F32, at=kv0)
        self.s_sink = A("s_sink", [128, LT], F32, at=kv0 + LT * 4)
        assert ksz >= 2048 * 4
        self.s_wstg = [A(f"s_wstg{i}", [128, 2048], F32, at=kv0 + (2 + i) * ksz)
                       for i in range(2)]

        cap = nc.SBUF_PARTITION_SIZE_BYTES
        assert self._off <= cap, (self._off, cap)
        self.sbuf_used = self._off
        self.alloc_table = alloc

        self.ps6 = nc.alloc_psum_tensor("ps6", [128, 3072], F32)
        self.psb = nc.alloc_psum_tensor("psb", [128, 2048], BF16)
        self.ps67 = self.psb.bitcast(F32)

        E = self.E = Emit(nc)
        self.c_const = E.chan("c_const")
        self.c_wfree = E.chan("c_wfree")
        self.c_mm = E.chan("c_mm")
        self.c_act = E.chan("c_act")
        self.c_dve = E.chan("c_dve")
        self.c_lnt = E.chan("c_lnt")
        self.c_cc = [[E.chan(f"c_cc{b}_{i}") for i in range(3)] for b in range(2)]
        self.c_gdma = E.chan("c_gdma")
        self.c_wcg = [E.chan(f"c_wcg{i}") for i in range(4)]
        self.c_ring = [E.chan(f"c_ring{i}") for i in range(4)]
        self.c_wdl = [E.chan(f"c_wdl{i}") for i in range(2)]
        self.c_xp = [E.chan(f"c_xp{i}") for i in range(6)]
        self.c_ksb = [E.chan(f"c_ksb{i}") for i in range(2)]
        self.c_vsb = [E.chan(f"c_vsb{i}") for i in range(2)]
        self.c_qv = [E.chan(f"c_qv{i}") for i in range(2)]
        self.c_kstg = E.chan("c_kstg")
        self.c_vstg = E.chan("c_vstg")
        self.c_wgub = E.chan("c_wgub")
        self.c_stg = [E.chan(f"c_stg{i}") for i in range(2)]
        self.c_kn = E.chan("c_kn")
        self.c_attld = E.chan("c_attld")
        E.self_chan = {"scalar": self.c_act, "vector": self.c_dve}
        self.c_kv = E.chan("c_kv")

        self.wcast_mark = {}
        self.wgu_wb = {}
        self.wqkv_wb = {}
        self.ffn_call = 0
        self.wstg_free = [None, None]
        self.wstg_par = 0
        self.stage_done = None
        self.pending_casts = []
        self.cast_rot = 0
        self.xp_free = [None] * 6
        self.lntok_free = [None] * 2
        self.silu_free = [None] * 2
        self.rsc_free = [None] * 3
        self.probs_free = [None] * 3
        self.ksb_free = [None] * 2
        self.vsb_free = [None] * 2
        self.pv_free = {0: None, 1: None}
        self.head_par = 0
        self.norm_q = []
        self.kv_pre = set()
        self.kld_r = {}
        self.vld_r = {}
        self.ring_free = [None] * 4
        self.wd_free = [None] * 2
        self.bank_free = {}
        self.psb_free = None
        self.kstg_free = None
        self.vstg_free = None
        self.sq_free = None
        self.stats_free = None
        self.small_free = None
        self.drow_free = None
        self.dbc_free = None
        self.negc0_free = None
        self.brow_ready = None
        self.big_free = None
        self.xlnt_write_ok = None
        self.att_write_ok = None
        self.cosk_write_ok = None
        self.ln_ready = 0
        self.qt_done = {}
        self.xready = {}


def mark(ch):
    return (ch, ch.n)


def emit_consts(B):
    E = B.E
    loads = [
        (B.s_biasc[:, :], B.bias_cols), (B.s_bqkvc[:, :], B.bqkv_cols),
        (B.s_bqkvs[:, :], B.bqkv_swap),
        (B.s_maskk[:, :], B.maskk), (B.s_ident[:, :], B.i_ident),
        (B.s_ones_r[:, :], B.i_ones_r), (B.s_ones_c[:, :], B.i_ones_c),
        (B.s_ones_m[:, :], B.i_ones_m),
        (B.s_ones_rf[:, :], B.i_ones_rf),
        (B.s_cosk[:, :], B.cos_k), (B.s_sink[:, :], B.sin_k),
    ]
    for d, s in loads:
        E.op("sync", (lambda d, s: lambda eng: eng.dma_start(out=d, in_=s))(d, s),
             inc=B.c_const, inc_by=16)
    B.const_n = B.c_const.n
    E.op("gpsimd", lambda eng: eng.dma_start(out=B.s_brow[:, :], in_=B.brow_d),
         inc=B.c_gdma, inc_by=16)
    B.brow_ready = (B.c_gdma, B.c_gdma.n)


def load_brow(B, src_row):
    E = B.E
    E.op("gpsimd",
         (lambda s: lambda eng: eng.dma_start(out=B.s_brow[:, :], in_=s))(src_row),
         waits=[(B.c_mm, B.c_mm.n)], inc=B.c_gdma, inc_by=16)
    B.brow_ready = (B.c_gdma, B.c_gdma.n)


def emit_weight_cast(B, kind_jobs, pace=None):
    E, cfg = B.E, B.cfg
    GSZ = 8
    jobs = kind_jobs
    for gi in range(0, len(jobs), GSZ):
        grp = jobs[gi:gi + GSZ]
        sidx = (gi // GSZ) % 4
        ch = B.c_wcg[sidx]
        throttle = [(ch, ch.n)] if ch.n > 0 else []
        if pace is not None:
            throttle = throttle + FR(pace(gi))
        for n, (kind, i, dst, src) in enumerate(grp):
            E.op("gpsimd",
                 (lambda d, s: lambda eng: eng.dma_start(out=d, in_=s))(dst, src),
                 waits=(throttle if n == 0 else []), inc=ch, inc_by=16)
        for kind, i, dst, src in grp:
            B.wcast_mark[(kind, i)] = (ch, ch.n)


def emit_ln(B, src, half, src_ready):
    E, cfg = B.E, B.cfg
    st = B.s_stats
    inv_n = 1.0 / cfg.DM
    TS, NPC, XP = cfg.TS, cfg.NPC, cfg.XP
    last_sq_add = None

    for t in range(TS):
        row0 = half * cfg.W + t * 128
        for p in range(NPC):
            slot = 4 + ((t * NPC + p) % 2)
            buf = B.s_xp[slot]
            cx = B.c_xp[slot]
            E.op("sync",
                 (lambda bb, rr, pp: lambda eng: eng.dma_start(
                     out=bb[:, :], in_=src[rr:rr + 128, pp * XP:(pp + 1) * XP]))(buf, row0, p),
                 waits=FR(src_ready, B.xp_free[slot]), inc=cx, inc_by=16)
            ldn = cx.n
            E.op("vector",
                 (lambda bb, tt, pp: lambda eng: eng.reduce_sum(
                     st[:, (tt if pp == 0 else 16):(tt if pp == 0 else 16) + 1],
                     bb[:, :], axis=AX.X))(buf, t, p),
                 waits=[(cx, ldn)] + FR(B.stats_free), inc=B.c_dve)
            if p > 0:
                E.op("vector",
                     (lambda tt: lambda eng: eng.tensor_add(
                         st[:, tt:tt + 1], st[:, tt:tt + 1], st[:, 16:17]))(t),
                     inc=B.c_dve)
            sl = (t * NPC + p) % 2
            E.op("scalar",
                 (lambda bb, tt, pp, sl: lambda eng: eng.activation(
                     B.s_lntok[sl][:, :], bb[:, :], AF.Square,
                     accum_out=st[:, (4 + tt if pp == 0 else 17):
                                  (4 + tt if pp == 0 else 17) + 1]))(buf, t, p, sl),
                 waits=[(cx, ldn)] + FR(B.lntok_free[sl], last_sq_add,
                                        B.stats_free),
                 inc=B.c_act)
            sq_n = B.c_act.n
            if p > 0:
                E.op("vector",
                     (lambda tt: lambda eng: eng.tensor_add(
                         st[:, 4 + tt:5 + tt], st[:, 4 + tt:5 + tt], st[:, 17:18]))(t),
                     waits=[(B.c_act, sq_n)], inc=B.c_dve)
                last_sq_add = mark(B.c_dve)
            B.xp_free[slot] = [mark(B.c_act), mark(B.c_dve)]
        E.op("vector",
             (lambda tt: lambda eng: eng.tensor_scalar(
                 st[:, 8 + tt:9 + tt], st[:, tt:tt + 1], inv_n, None, ALU.mult))(t),
             waits=[(B.c_act, B.c_act.n)], inc=B.c_dve)
        E.op("vector",
             (lambda tt: lambda eng: eng.tensor_scalar(
                 st[:, 16:17], st[:, 8 + tt:9 + tt], st[:, 8 + tt:9 + tt],
                 cfg.eps, ALU.mult, ALU.subtract))(t),
             inc=B.c_dve)
        E.op("vector",
             (lambda tt: lambda eng: eng.scalar_tensor_tensor(
                 st[:, 12 + tt:13 + tt], st[:, 4 + tt:5 + tt], inv_n,
                 st[:, 16:17], ALU.mult, ALU.subtract))(t),
             inc=B.c_dve)
    veps_done = B.c_dve.n

    E.op("scalar",
         lambda eng: eng.activation(st[:, 18:18 + TS], st[:, 12:12 + TS], AF.Ln),
         waits=[(B.c_dve, veps_done)], inc=B.c_act)
    E.op("vector",
         lambda eng: eng.tensor_scalar(st[:, 18:18 + TS], st[:, 18:18 + TS],
                                       -0.5, None, ALU.mult),
         waits=[(B.c_act, B.c_act.n)], inc=B.c_dve)
    E.op("scalar",
         lambda eng: eng.activation(st[:, 18:18 + TS], st[:, 18:18 + TS], AF.Exp),
         waits=[(B.c_dve, B.c_dve.n)], inc=B.c_act)
    rstd_done = B.c_act.n
    E.op("vector",
         lambda eng: eng.tensor_mul(st[:, 24:24 + TS], st[:, 8:8 + TS],
                                    st[:, 18:18 + TS]),
         waits=[(B.c_act, rstd_done)], inc=B.c_dve)
    E.op("vector",
         lambda eng: eng.tensor_scalar(st[:, 24:24 + TS], st[:, 24:24 + TS],
                                       -1.0, None, ALU.mult),
         inc=B.c_dve)
    negmr_done = B.c_dve.n

    for t in range(TS):
        row0 = half * cfg.W + t * 128
        for p in range(NPC):
            slot = 4 + ((t * NPC + p) % 2)
            buf = B.s_xp[slot]
            cx = B.c_xp[slot]
            E.op("sync",
                 (lambda bb, rr, pp: lambda eng: eng.dma_start(
                     out=bb[:, :], in_=src[rr:rr + 128, pp * XP:(pp + 1) * XP]))(buf, row0, p),
                 waits=FR(B.xp_free[slot]), inc=cx, inc_by=16)
            ldn = cx.n
            sl = (t * NPC + p) % 2
            lnt = B.s_lntok[sl]
            E.op("vector",
                 (lambda bb, oo, tt: lambda eng: eng.tensor_scalar(
                     oo[:, :], bb[:, :], st[:, 18 + tt:19 + tt],
                     st[:, 24 + tt:25 + tt], ALU.mult, ALU.add))(buf, lnt, t),
                 waits=[(cx, ldn), (B.c_dve, negmr_done),
                        (B.c_act, rstd_done)] + FR(B.lntok_free[sl]),
                 inc=B.c_dve)
            ap_done = B.c_dve.n
            B.xp_free[slot] = mark(B.c_dve)
            nch = XP // 128
            for g in range(nch):
                E.op("tensor",
                     (lambda gg, ii: lambda eng: eng.transpose(
                         B.psb[:, gg * 128:(gg + 1) * 128], ii, B.s_ident[:, :]))(
                         g, lnt[:, g * 128:(g + 1) * 128]),
                     waits=[(B.c_dve, ap_done)] + FR(B.psb_free),
                     inc=(B.c_mm if g == nch - 1 else None))
            tr_done = B.c_mm.n
            B.lntok_free[sl] = mark(B.c_mm)

            def evac(eng, pp=p, nch=nch, tt=t):
                dst = B.s_xlnt[:, :].rearrange("p (c w) -> p c w", w=cfg.W)[
                    :, pp * nch:(pp + 1) * nch, tt * 128:tt * 128 + 128]
                srcap = B.psb[:, 0:nch * 128].rearrange("p (c u) -> p c u", u=128)
                return eng.tensor_copy(dst, srcap)
            E.op("vector", evac,
                 waits=[(B.c_mm, tr_done)] + FR(B.xlnt_write_ok), inc=B.c_lnt)
            B.psb_free = mark(B.c_lnt)
    B.ln_ready = B.c_lnt.n
    B.stats_free = mark(B.c_act)


def emit_ffn_half(B, half, src_res, dst, src_ready):
    E, cfg = B.E, B.cfg
    xl = B.s_xlnt
    Wd = cfg.W
    ln_need = B.ln_ready
    groups = [list(range(0, cfg.IH0)), list(range(cfg.IH0, cfg.NIB))]
    groups = [g for g in groups if g]
    occ = cfg.XP
    n_pass = cfg.DM // occ
    mid_mark = {}
    dst_marks = []

    for ih, blocks in enumerate(groups):
        last_ih = ih == len(groups) - 1
        mode = ("f32", "f32wb", "bf16", "bf16")[min(B.ffn_call, 3)]
        for slot, i in enumerate(blocks):
            mmk = [0, 0]
            for gu in range(2):
                ri = (2 * slot + gu) % 4
                rb = B.s_ring[ri]
                idx = 2 * i + gu
                if mode != "bf16":
                    rcm = [None, None]
                    for hc in range(2):
                        sl = B.wstg_par
                        B.wstg_par ^= 1
                        stb = B.s_wstg[sl]
                        cs = B.c_stg[sl]
                        E.op("sync",
                             (lambda stb, idx, hc: lambda eng: eng.dma_start(
                                 out=stb[:, :],
                                 in_=B.wgu[idx][:, hc * 2048:(hc + 1) * 2048]))(
                                 stb, idx, hc),
                             waits=FR(B.wstg_free[sl], B.vsb_free[sl]),
                             inc=cs, inc_by=16)
                        ldn = cs.n
                        cast_w = [(cs, ldn)] + FR(B.ring_free[ri], B.big_free)
                        if hc == 0:
                            E.op("vector",
                                 (lambda rb, stb, hc: lambda eng: eng.tensor_copy(
                                     rb[:, hc * 2048:(hc + 1) * 2048],
                                     stb[:, :]))(rb, stb, hc),
                                 waits=cast_w, inc=B.c_dve)
                            rcm[hc] = (B.c_dve, B.c_dve.n)
                        else:
                            E.op("scalar",
                                 (lambda rb, stb, hc: lambda eng: eng.activation(
                                     rb[:, hc * 2048:(hc + 1) * 2048],
                                     stb[:, :], AF.Copy))(rb, stb, hc),
                                 waits=cast_w, inc=B.c_act)
                            rcm[hc] = (B.c_act, B.c_act.n)
                        B.wstg_free[sl] = rcm[hc]
                    if mode == "f32wb":
                        E.op("sync",
                             (lambda rb, idx: lambda eng: eng.dma_start(
                                 out=B.wgu_bf[idx], in_=rb[:, :]))(rb, idx),
                             waits=FR(rcm[0], rcm[1]),
                             inc=B.c_wgub, inc_by=16)
                        B.wgu_wb[idx] = (B.c_wgub, B.c_wgub.n)
                        wb_mark = (B.c_wgub, B.c_wgub.n)
                    else:
                        wb_mark = None
                    wlo, whi = rcm[0], rcm[1]
                else:
                    E.op("sync",
                         (lambda rb, idx: lambda eng: eng.dma_start(
                             out=rb[:, :], in_=B.wgu_bf[idx]))(rb, idx),
                         waits=FR(B.ring_free[ri], B.big_free,
                                  B.wgu_wb[idx]),
                         inc=B.c_ring[ri], inc_by=16)
                    wlo = whi = (B.c_ring[ri], B.c_ring[ri].n)
                    wb_mark = None
                bank = gu
                for k in range(cfg.DMC):
                    E.op("tensor",
                         (lambda rb, kk, bank, first, last: lambda eng: eng.matmul(
                             B.ps6[:, bank * 512:bank * 512 + Wd],
                             rb[:, kk * 128:(kk + 1) * 128],
                             xl[:, kk * Wd:(kk + 1) * Wd],
                             start=first, stop=last))(rb, k, bank, k == 0,
                                                      k == cfg.DMC - 1),
                         waits=([wlo if k < 16 else whi, (B.c_lnt, ln_need)]
                                + (FR(B.bank_free.get(bank)) if k == 0 else [])),
                         inc=(B.c_mm if k == cfg.DMC - 1 else None))
                mmk[gu] = B.c_mm.n
                B.ring_free[ri] = FR(mark(B.c_mm), wb_mark)
            if B.pending_casts:
                kind, ci, cdst, csrc = B.pending_casts.pop(0)
                ch = B.c_wcg[B.cast_rot % 4]
                B.cast_rot += 1
                thr = ([(ch, ch.n)] if ch.n > 0 else []) + [(B.c_mm, mmk[0])]
                E.op("gpsimd",
                     (lambda d, s: lambda eng: eng.dma_start(out=d, in_=s))(
                         cdst, csrc),
                     waits=thr, inc=ch, inc_by=16)
                B.wcast_mark[(kind, ci)] = (ch, ch.n)
            ssl = slot % 2
            E.op("scalar",
                 (lambda ssl, i: lambda eng: eng.activation(
                     B.s_silu[ssl][:, :], B.ps6[:, 0:Wd], AF.Silu,
                     bias=B.s_biasc[:, 2 * i:2 * i + 1]))(ssl, i),
                 waits=[(B.c_mm, mmk[0])] + FR(B.silu_free[ssl]), inc=B.c_act)
            B.bank_free[0] = mark(B.c_act)
            silu_done = B.c_act.n
            E.op("vector",
                 (lambda ssl, i, slot: lambda eng: eng.scalar_tensor_tensor(
                     B.s_prod[:, slot * Wd:(slot + 1) * Wd],
                     B.ps6[:, 512:512 + Wd],
                     B.s_biasc[:, 2 * i + 1:2 * i + 2], B.s_silu[ssl][:, :],
                     ALU.add, ALU.mult))(ssl, i, slot),
                 waits=[(B.c_mm, mmk[1]), (B.c_act, silu_done)]
                 + FR(getattr(B, "prod_free", None), B.big_free), inc=B.c_dve)
            B.bank_free[1] = mark(B.c_dve)
            B.silu_free[ssl] = mark(B.c_dve)
        prod_done = B.c_dve.n
        if cfg.stop == "prod" and not getattr(B, "_dumped", False):
            B._dumped = True
            E.op("sync", lambda eng: eng.dma_start(out=B.dbg, in_=B.s_prod[:, :]),
                 waits=[(B.c_dve, prod_done)], inc=B.c_xp[0], inc_by=16)
            return
        B.xlnt_read_done = mark(B.c_mm)

        ts_mm_done = {}
        BB = max(1, min(8, cfg.DM // occ))
        for p in range(n_pass):
            for b0 in range(0, len(blocks), BB):
                bgrp = blocks[b0:b0 + BB]
                nb = len(bgrp)
                buf = (b0 // BB) % 2
                wdb = B.s_wd[buf]
                i0 = bgrp[0]
                E.op("sync",
                     (lambda wdb, i0, nb, p: lambda eng: eng.dma_start(
                         out=wdb[:, 0:nb * occ].rearrange(
                             "p (b u) -> p b u", u=occ),
                         in_=B.wdn_bf[i0:i0 + nb, :, p * occ:(p + 1) * occ]
                         .rearrange("b p u -> p b u")))(wdb, i0, nb, p),
                     waits=FR(B.wcast_mark[("dn", bgrp[-1])]) + FR(B.wd_free[buf]),
                     inc=B.c_wdl[buf], inc_by=16)
                wldn = (B.c_wdl[buf], B.c_wdl[buf].n)
                for bn, i in enumerate(bgrp):
                    bi = b0 + bn
                    last_bi = bi == len(blocks) - 1
                    for ts in range(cfg.TS):
                        base = ts * (occ // 512)
                        for oc2 in range(occ // 512):
                            tail = (ts == cfg.TS - 1 and oc2 == occ // 512 - 1
                                    and bn == nb - 1) \
                                or (last_bi and oc2 == occ // 512 - 1)
                            E.op("tensor",
                                 (lambda wdb, slot, bn, ts, oc2, base, first, stp:
                                  lambda eng: eng.matmul(
                                      B.ps6[:, (base + oc2) * 512:(base + oc2 + 1) * 512],
                                      B.s_prod[:, slot * Wd + ts * 128:
                                               slot * Wd + ts * 128 + 128],
                                      wdb[:, (bn * (occ // 512) + oc2) * 512:
                                          (bn * (occ // 512) + oc2 + 1) * 512],
                                      start=first, stop=stp))(wdb, bi, bn, ts, oc2,
                                                              base, bi == 0,
                                                              last_bi and not last_ih),
                                 waits=([wldn, (B.c_dve, prod_done)]
                                        + (FR(B.bank_free.get(base + oc2)) if bi == 0 else [])),
                                 inc=(B.c_mm if tail else None))
                        if last_bi:
                            ts_mm_done[ts] = B.c_mm.n
                B.wd_free[buf] = mark(B.c_mm)
            for ts in range(cfg.TS):
                base = ts * (occ // 512)
                if last_ih:
                    for oc2 in range(occ // 512):
                        E.op("tensor",
                             (lambda p, oc2, base: lambda eng: eng.matmul(
                                 B.ps6[:, (base + oc2) * 512:(base + oc2 + 1) * 512],
                                 B.s_ones_r[:, :],
                                 B.s_brow[:, p * occ + oc2 * 512:
                                          p * occ + (oc2 + 1) * 512],
                                 start=False, stop=True))(p, oc2, base),
                             waits=FR(B.brow_ready),
                             inc=(B.c_mm if oc2 == occ // 512 - 1 else None))
                    mm_done = B.c_mm.n
                else:
                    mm_done = ts_mm_done[ts]
                xslot = ts % 4
                xb = B.s_xp[xslot]
                row0 = half * Wd + ts * 128
                sx = src_res if ih == 0 else B.xmid
                dx = dst if last_ih else B.xmid
                cx = B.c_xp[xslot]
                E.op("scalar",
                     (lambda xb, row0, p, sx: lambda eng: eng.dma_start(
                         out=xb[:, 0:occ], in_=sx[row0:row0 + 128,
                                                  p * occ:(p + 1) * occ]))(xb, row0, p, sx),
                     waits=FR(B.xp_free[xslot],
                              src_ready if ih == 0 else mid_mark.get((p, ts))),
                     inc=cx, inc_by=16)
                ldn = cx.n
                E.op("vector",
                     (lambda xb, base: lambda eng: eng.scalar_tensor_tensor(
                         xb[:, 0:occ], B.ps6[:, base * 512:base * 512 + occ],
                         0.5, xb[:, 0:occ], ALU.mult, ALU.add))(xb, base),
                     waits=[(B.c_mm, mm_done), (cx, ldn)], inc=B.c_dve)
                for oc2 in range(occ // 512):
                    B.bank_free[base + oc2] = mark(B.c_dve)
                stt = B.c_dve.n
                E.op("scalar",
                     (lambda xb, row0, p, dx: lambda eng: eng.dma_start(
                         out=dx[row0:row0 + 128, p * occ:(p + 1) * occ],
                         in_=xb[:, 0:occ]))(xb, row0, p, dx),
                     waits=[(B.c_dve, stt)], inc=cx, inc_by=16)
                B.xp_free[xslot] = mark(cx)
                dst_marks.append(mark(cx))
                if not last_ih:
                    mid_mark[(p, ts)] = mark(cx)
        B.prod_free = mark(B.c_mm)
    B.big_free = mark(B.c_mm)
    B.dst_done = dst_marks
    if B.ffn_call == 1:
        B.stage_done = FR(B.wstg_free[0], B.wstg_free[1])
    B.ffn_call += 1


def emit_qkv_half(B, half):
    E, cfg = B.E, B.cfg
    Wd = cfg.W
    xl = B.s_xlnt
    ln_need = B.ln_ready
    NH, NKV = cfg.NH, cfg.NKV
    if half == 0:
        E.op("sync", lambda eng: eng.dma_start(out=B.s_cosq[:, :], in_=B.cos_q),
             waits=FR(B.big_free), inc=B.c_const, inc_by=16)
        E.op("sync", lambda eng: eng.dma_start(out=B.s_sinq[:, :], in_=B.sin_q),
             waits=FR(B.big_free), inc=B.c_const, inc_by=16)
        B.ropeq_n = B.c_const.n
    E.op("vector", lambda eng: eng.memset(B.s_small[0:1, 64:80], 0.0),
         waits=FR(B.small_free), inc=B.c_dve)

    for j in range(cfg.NQKV):
        wdb = B.s_wd[j % 2]
        if half == 0:
            E.op("gpsimd",
                 (lambda wdb, j: lambda eng: eng.dma_start(
                     out=wdb[:, :], in_=B.wqkv[j]))(wdb, j),
                 waits=FR(B.wd_free[j % 2]),
                 inc=B.c_qv[j % 2], inc_by=16)
            wldn = (B.c_qv[j % 2], B.c_qv[j % 2].n)
            E.op("sync",
                 (lambda wdb, j: lambda eng: eng.dma_start(
                     out=B.wqkv_bf[j], in_=wdb[:, :]))(wdb, j),
                 waits=[wldn], inc=B.c_wgub, inc_by=16)
            B.wqkv_wb[j] = (B.c_wgub, B.c_wgub.n)
            wb_mark = (B.c_wgub, B.c_wgub.n)
        else:
            E.op("sync",
                 (lambda wdb, j: lambda eng: eng.dma_start(
                     out=wdb[:, :], in_=B.wqkv_bf[j]))(wdb, j),
                 waits=FR(B.wd_free[j % 2], B.wqkv_wb[j]),
                 inc=B.c_qv[j % 2], inc_by=16)
            wldn = (B.c_qv[j % 2], B.c_qv[j % 2].n)
            wb_mark = None
        bank = j % 3
        for k in range(cfg.DMC):
            E.op("tensor",
                 (lambda wdb, kk, bank, first, last: lambda eng: eng.matmul(
                     B.ps6[:, bank * 512:bank * 512 + Wd],
                     wdb[:, kk * 128:(kk + 1) * 128],
                     xl[:, kk * Wd:(kk + 1) * Wd],
                     start=first, stop=last))(wdb, k, bank, k == 0, k == cfg.DMC - 1),
                 waits=([wldn, (B.c_lnt, ln_need)]
                        + (FR(B.bank_free.get(bank)) if k == 0 else [])),
                 inc=(B.c_mm if k == cfg.DMC - 1 else None))
        mmn = B.c_mm.n
        B.wd_free[j % 2] = FR(mark(B.c_mm), wb_mark)
        ps = B.ps6[:, bank * 512:bank * 512 + Wd]
        bcol = B.s_bqkvc[:, j:j + 1]

        if j < NH + NKV:
            if j < NH:
                cosT, sinT = B.s_cosq, B.s_sinq
                tabw = [(B.c_const, B.ropeq_n)]
            else:
                cosT, sinT = B.s_cosk, B.s_sink
                tabw = [(B.c_const, B.const_n)] + FR(getattr(B, "costab_alive", None))
            cs = cosT[:, half * Wd:(half + 1) * Wd]
            sn = sinT[:, half * Wd:(half + 1) * Wd]
            bswp = B.s_bqkvs[:, j:j + 1]
            E.op("vector",
                 (lambda ps, bcol, cs: lambda eng: eng.scalar_tensor_tensor(
                     B.s_rsc[1][:, :], ps, bcol, cs, ALU.add, ALU.mult))(ps, bcol, cs),
                 waits=[(B.c_mm, mmn)] + FR(B.rsc_free[1]) + tabw, inc=B.c_dve)
            m1_done = B.c_dve.n
            E.op("vector",
                 (lambda ps, bswp, sn: lambda eng: (
                     eng.scalar_tensor_tensor(
                         B.s_rsc[2][0:64, :], ps[64:128, :], bswp[0:64, :],
                         sn[0:64, :], ALU.add, ALU.mult),
                     eng.scalar_tensor_tensor(
                         B.s_rsc[2][64:128, :], ps[0:64, :], bswp[64:128, :],
                         sn[64:128, :], ALU.add, ALU.mult))[-1])(ps, bswp, sn),
                 waits=[(B.c_mm, mmn)] + FR(B.rsc_free[2]) + tabw, inc=B.c_dve)
            m2_done = B.c_dve.n
            B.bank_free[bank] = mark(B.c_dve)
            if j < NH:
                dstap = B.s_qt[:, j * cfg.LT + half * Wd:j * cfg.LT + (half + 1) * Wd]
                extra = FR(B.big_free)
            else:
                dstap = B.s_kstg[:, :]
                extra = FR(B.kstg_free)
            E.op("vector",
                 (lambda dstap: lambda eng: eng.tensor_add(
                     dstap, B.s_rsc[1][:, :], B.s_rsc[2][:, :]))(dstap),
                 waits=[(B.c_dve, m2_done)] + extra, inc=B.c_dve)
            rot_done = B.c_dve.n
            B.rsc_free[1] = mark(B.c_dve)
            B.rsc_free[2] = mark(B.c_dve)
            E.op("scalar",
                 (lambda dstap: lambda eng: eng.activation(
                     B.s_sq[:, :], dstap, AF.Square))(dstap),
                 waits=[(B.c_dve, rot_done)] + FR(B.sq_free), inc=B.c_act)
            sq_done = B.c_act.n
            E.op("tensor",
                 lambda eng: eng.matmul(B.ps6[0:1, 1536:1536 + Wd],
                                        B.s_ones_c[:, :], B.s_sq[:, :],
                                        start=True, stop=True),
                 waits=[(B.c_act, sq_done)] + FR(B.bank_free.get(3)), inc=B.c_mm)
            nrm_mm = B.c_mm.n
            B.sq_free = mark(B.c_mm)
            col = half * NH + j if j < NH else 64 + (j - NH)
            E.op("vector",
                 (lambda col: lambda eng: eng.reduce_max(
                     B.s_small[0:1, col:col + 1], B.ps6[0:1, 1536:1536 + Wd],
                     axis=AX.X))(col),
                 waits=[(B.c_mm, nrm_mm)], inc=B.c_dve)
            B.bank_free[3] = mark(B.c_dve)
            if j >= NH:
                E.op("sync",
                     (lambda j, half: lambda eng: eng.dma_start(
                         out=B.kt_in[half][(j - NH) * 128:(j - NH + 1) * 128, :],
                         in_=B.s_kstg[:, :]))(j, half),
                     waits=[(B.c_dve, rot_done)], inc=B.c_kstg, inc_by=16)
                B.kstg_free = [mark(B.c_kstg), mark(B.c_act)]
        else:
            cidx = j - NH - NKV
            E.op("vector",
                 (lambda ps, bcol: lambda eng: eng.tensor_scalar(
                     B.s_kstg[:, :], ps, bcol, None, ALU.add))(ps, bcol),
                 waits=[(B.c_mm, mmn)] + FR(B.kstg_free), inc=B.c_dve)
            ev_done = B.c_dve.n
            B.bank_free[bank] = mark(B.c_dve)
            for ts in range(cfg.TS):
                E.op("tensor",
                     (lambda ts: lambda eng: eng.transpose(
                         B.psb[:, ts * 128:(ts + 1) * 128],
                         B.s_kstg[:, ts * 128:(ts + 1) * 128],
                         B.s_ident[:, :]))(ts),
                     waits=[(B.c_dve, ev_done)] + FR(B.psb_free),
                     inc=(B.c_mm if ts == cfg.TS - 1 else None))
            tr_done = B.c_mm.n
            B.kstg_free = mark(B.c_mm)

            def evacv(eng, cidx=cidx):
                dstv = B.s_vstg[:, :].rearrange(
                    "p (t c u) -> p t c u", c=cfg.NKV, u=128)[:, :, cidx, :]
                srcv = B.psb[:, 0:cfg.TS * 128].rearrange("p (t u) -> p t u", u=128)
                return eng.tensor_copy(dstv, srcv)
            E.op("vector", evacv,
                 waits=[(B.c_mm, tr_done)] + FR(B.vstg_free if cidx == 0 else None),
                 inc=B.c_dve)
            B.psb_free = mark(B.c_dve)
            if cidx == cfg.NKV - 1:
                vdone = B.c_dve.n
                for ts in range(cfg.TS):
                    E.op("sync",
                         (lambda ts, half: lambda eng: eng.dma_start(
                             out=B.v_in[half][ts * 128:(ts + 1) * 128, :],
                             in_=B.s_vstg[:, ts * cfg.NKV * 128:
                                          (ts + 1) * cfg.NKV * 128]))(ts, half),
                         waits=[(B.c_dve, vdone)], inc=B.c_vstg, inc_by=16)
                B.vstg_free = mark(B.c_vstg)
    E.op("sync",
         (lambda half: lambda eng: eng.dma_start(
             out=B.kn_in[half][:, :], in_=B.s_small[0:1, 64:80]))(half),
         waits=[(B.c_dve, B.c_dve.n)], inc=B.c_kn, inc_by=16)
    B.small_free = mark(B.c_kn)
    B.qkv_ctrb = [mark(B.c_kstg), mark(B.c_vstg), mark(B.c_kn)]
    B.qt_done[half] = mark(B.c_dve)
    B.costab_alive = None
    if half == 1:
        B.cosk_write_ok = mark(B.c_dve)
        B.xlnt_write_ok_att = mark(B.c_mm)


def emit_gather(B, half):
    marks = B.qkv_ctrb
    pairs = [(B.kt_in[half], B.kt_g[half]), (B.v_in[half], B.v_g[half]),
             (B.kn_in[half], B.kn_g[half])]
    for i, (src_t, dst_t) in enumerate(pairs):
        B.E.op("gpsimd",
               (lambda s, d: lambda eng: eng.collective_compute(
                   "AllGather", ALU.bypass,
                   replica_groups=[list(range(NCORES))],
                   ins=[s[:, :]], outs=[d[:, :]]))(src_t, dst_t),
               waits=FR(marks[i]), inc=B.c_cc[half][i])


def emit_kv_prefetch(B, half, g, want_k=True, want_v=True):
    E, cfg = B.E, B.cfg
    L, TS = cfg.L, cfg.TS
    buf = g % 2
    ksb, vsb = B.s_ksb[buf], B.s_vsb[buf]
    if want_k:
        kex = (FR(B.ksb_free[buf])
               + (FR(B.cosk_write_ok) if buf == 0 else []))
        B.kld_r[(half, g)] = []
        for r in range(NCORES):
            E.op("gpsimd",
                 (lambda half, g, r, ksb: lambda eng: eng.dma_start(
                     out=ksb[:, r * L:(r + 1) * L],
                     in_=B.kt_g[half][r * cfg.NKV * 128 + g * 128:
                                      r * cfg.NKV * 128 + (g + 1) * 128, :]))(
                     half, g, r, ksb),
                 waits=[(B.c_cc[half][0], 1)] + kex,
                 inc=B.c_ksb[buf], inc_by=16)
            B.kld_r[(half, g)].append((B.c_ksb[buf], B.c_ksb[buf].n))
        B.kv_pre.add((half, g, 'k'))
    if want_v:
        vex = FR(B.vsb_free[buf], B.stage_done)
        B.vld_r[(half, g)] = []
        for r in range(NCORES):
            for ts in range(TS):
                E.op("gpsimd",
                     (lambda half, g, r, ts, vsb: lambda eng: eng.dma_start(
                         out=vsb[:, (r * TS + ts) * 128:(r * TS + ts + 1) * 128],
                         in_=B.v_g[half][r * L + ts * 128:r * L + (ts + 1) * 128,
                                         g * 128:(g + 1) * 128]))(half, g, r, ts, vsb),
                     waits=[(B.c_cc[half][1], 1)] + vex,
                     inc=B.c_vsb[buf], inc_by=16)
            B.vld_r[(half, g)].append((B.c_vsb[buf], B.c_vsb[buf].n))
        B.kv_pre.add((half, g, 'v'))


def emit_attention_half(B, half):
    E, cfg = B.E, B.cfg
    L, MT, TS, Wd = cfg.L, cfg.MT, cfg.TS, cfg.W
    E.op("sync",
         (lambda half: lambda eng: eng.dma_start(
             out=B.s_drow[0:1, 0:16 * NCORES],
             in_=B.kn_g[half][:, :].rearrange("(o r) c -> o (r c)", o=1)))(half),
         waits=[(B.c_cc[half][2], 1)] + FR(B.drow_free), inc=B.c_attld, inc_by=16)
    knl = B.c_attld.n
    E.op("vector",
         (lambda half: lambda eng: (
             eng.reduce_max(B.s_small[0:1, 90:91], B.s_drow[0:1, 0:16 * NCORES],
                            axis=AX.X),
             eng.reduce_max(B.s_small[0:1, 91:92],
                            B.s_small[0:1, half * cfg.NH:half * cfg.NH + cfg.NH],
                            axis=AX.X))[-1])(half),
         waits=[(B.c_attld, knl)] + [B.qt_done[half]], inc=B.c_dve)
    mx = B.c_dve.n
    E.op("scalar",
         lambda eng: (eng.activation(B.s_small[0:1, 92:93], B.s_small[0:1, 90:91], AF.Ln),
                      eng.activation(B.s_small[0:1, 93:94], B.s_small[0:1, 91:92], AF.Ln))[-1],
         waits=[(B.c_dve, mx)], inc=B.c_act)
    lns = B.c_act.n
    E.op("vector",
         lambda eng: eng.tensor_add(B.s_small[0:1, 94:95], B.s_small[0:1, 92:93],
                                    B.s_small[0:1, 93:94]),
         waits=[(B.c_act, lns)], inc=B.c_dve)
    E.op("vector",
         lambda eng: eng.tensor_scalar(B.s_small[0:1, 94:95], B.s_small[0:1, 94:95],
                                       0.5, None, ALU.mult),
         inc=B.c_dve)
    addn = B.c_dve.n
    E.op("scalar",
         lambda eng: eng.activation(B.s_small[0:1, 95:96], B.s_small[0:1, 94:95],
                                    AF.Exp),
         waits=[(B.c_dve, addn)], inc=B.c_act)
    bn = B.c_act.n
    E.op("vector",
         lambda eng: eng.tensor_scalar(B.s_small[0:1, 96:97], B.s_small[0:1, 95:96],
                                       -40.0, 0.0, ALU.add, ALU.max),
         waits=[(B.c_act, bn)], inc=B.c_dve)
    E.op("vector",
         lambda eng: eng.tensor_scalar(B.s_small[0:1, 96:97], B.s_small[0:1, 96:97],
                                       -1.0, None, ALU.mult),
         inc=B.c_dve)
    negn = B.c_dve.n
    E.op("tensor",
         lambda eng: eng.matmul(B.ps6[:, 0:1], B.s_ones_rf[:, :],
                                B.s_small[0:1, 96:97], start=True, stop=True),
         waits=[(B.c_dve, negn)] + FR(B.bank_free.get(0)), inc=B.c_mm)
    bcmm = B.c_mm.n
    E.op("vector",
         lambda eng: eng.tensor_copy(B.s_negc0[:, :], B.ps6[:, 0:1]),
         waits=[(B.c_mm, bcmm)] + FR(B.negc0_free), inc=B.c_dve)
    negc0_ready = mark(B.c_dve)
    B.bank_free[0] = mark(B.c_dve)
    B.drow_free = mark(B.c_dve)

    groups_sr = [(s, rp) for s in range(MT) for rp in range(NCORES // 2)]
    NG = len(groups_sr)
    if B.pv_free[1] is None:
        B.pv_free[1] = FR(B.psb_free)

    def sum_acc(q, lo):
        return (B.ps6[:, 2048 + lo:2048 + Wd] if q == 0
                else B.ps67[:, lo:Wd])

    def sum_full(q):
        return B.ps6[:, 2048:2048 + Wd] if q == 0 else B.ps67[:, 0:Wd]

    def pv_acc(q, lo):
        return (B.ps6[:, 2560 + lo:2560 + Wd] if q == 0
                else B.ps67[:, 512 + lo:1024])

    for g in range(cfg.NKV):
        buf = g % 2
        ksb, vsb = B.s_ksb[buf], B.s_vsb[buf]
        if (half, g, 'k') not in B.kv_pre:
            kex = (FR(B.ksb_free[buf])
                   + (FR(B.cosk_write_ok) if buf == 0 else []))
            B.kld_r[(half, g)] = []
            for r in range(NCORES):
                E.op("sync",
                     (lambda half, g, r, ksb: lambda eng: eng.dma_start(
                         out=ksb[:, r * L:(r + 1) * L],
                         in_=B.kt_g[half][r * cfg.NKV * 128 + g * 128:
                                          r * cfg.NKV * 128 + (g + 1) * 128, :]))(
                         half, g, r, ksb),
                     waits=[(B.c_cc[half][0], 1)] + kex,
                     inc=B.c_ksb[buf], inc_by=16)
                B.kld_r[(half, g)].append((B.c_ksb[buf], B.c_ksb[buf].n))
        if (half, g, 'v') not in B.kv_pre:
            vex = FR(B.vsb_free[buf], B.stage_done)
            B.vld_r[(half, g)] = []
            for r in range(NCORES):
                for ts in range(TS):
                    E.op("sync",
                         (lambda half, g, r, ts, vsb: lambda eng: eng.dma_start(
                             out=vsb[:, (r * TS + ts) * 128:(r * TS + ts + 1) * 128],
                             in_=B.v_g[half][r * L + ts * 128:r * L + (ts + 1) * 128,
                                             g * 128:(g + 1) * 128]))(half, g, r, ts, vsb),
                         waits=[(B.c_cc[half][1], 1)] + vex,
                         inc=B.c_vsb[buf], inc_by=16)
                B.vld_r[(half, g)].append((B.c_vsb[buf], B.c_vsb[buf].n))
        kld = (B.c_ksb[buf], B.c_ksb[buf].n)
        vld = (B.c_vsb[buf], B.c_vsb[buf].n)
        for h in range(g * cfg.GRP, (g + 1) * cfg.GRP):
            q = B.head_par & 1
            B.head_par += 1
            mskn = {}

            def emit_scores(gi):
                s, rp = groups_sr[gi]
                span = (MT - s) * 128
                sb0 = 2 * (gi % 2)
                pb = gi % 3
                prb = B.s_probs[pb]
                scm = [0, 0]
                for r2 in range(2):
                    r = 2 * rp + r2
                    E.op("tensor",
                         (lambda ksb, h, half, r, s, span, bank:
                          lambda eng: eng.matmul(
                              B.ps6[:, bank * 512:bank * 512 + span],
                              ksb[:, r * L + s * 128:r * L + (s + 1) * 128],
                              B.s_qt[:, h * cfg.LT + half * Wd + s * 128:
                                     h * cfg.LT + (half + 1) * Wd],
                              start=True, stop=True))(ksb, h, half, r, s, span,
                                                      sb0 + r2),
                         waits=[kld, B.qt_done[half]]
                         + FR(B.bank_free.get(sb0 + r2)),
                         inc=B.c_mm)
                    scm[r2] = B.c_mm.n
                B._sc_mm_last = scm[1]
                for r2 in range(2):
                    E.op("scalar",
                         (lambda prb, sb0, r2, span: lambda eng: eng.activation(
                             prb[:, r2 * span:(r2 + 1) * span],
                             B.ps6[:, (sb0 + r2) * 512:(sb0 + r2) * 512 + span],
                             AF.Exp, bias=B.s_negc0[:, :]))(prb, sb0, r2, span),
                         waits=[(B.c_mm, scm[r2]), negc0_ready]
                         + (FR(B.probs_free[pb]) if r2 == 0 else []),
                         inc=B.c_act)
                    expn = B.c_act.n
                    B.bank_free[sb0 + r2] = mark(B.c_act)
                    E.op("vector",
                         (lambda prb, rp, r2, span: lambda eng: eng.tensor_mul(
                             prb[:, r2 * span:r2 * span + 128],
                             prb[:, r2 * span:r2 * span + 128],
                             B.s_maskk[:, (2 * rp + r2) * 128:
                                       (2 * rp + r2 + 1) * 128]))(prb, rp, r2, span),
                         waits=[(B.c_act, expn)], inc=B.c_dve)
                    mskn[(gi, r2)] = B.c_dve.n

            def emit_sumpv(gi, first):
                s, rp = groups_sr[gi]
                span = (MT - s) * 128
                pb = gi % 3
                prb = B.s_probs[pb]
                last_g = gi == NG - 1
                for r2 in range(2):
                    r = 2 * rp + r2
                    stp = last_g and r2 == 1
                    st = first and r2 == 0
                    w = [(B.c_dve, mskn[(gi, r2)]), vld]
                    if st:
                        w = w + FR(B.pv_free[q])
                    E.op("tensor",
                         (lambda prb, q, r2, s, span, st, stp:
                          lambda eng: eng.matmul(
                              sum_acc(q, s * 128), B.s_ones_m[:, :],
                              prb[:, r2 * span:(r2 + 1) * span],
                              start=st, stop=stp))(prb, q, r2, s, span, st, stp),
                         waits=w, inc=None)
                    E.op("tensor",
                         (lambda prb, vsb, q, r, r2, s, span, st, stp:
                          lambda eng: eng.matmul(
                              pv_acc(q, s * 128),
                              vsb[:, (r * TS + s) * 128:(r * TS + s + 1) * 128],
                              prb[:, r2 * span:(r2 + 1) * span],
                              start=st, stop=stp))(prb, vsb, q, r, r2, s, span,
                                                   st, stp),
                         waits=w, inc=(B.c_mm if r2 == 1 else None))
                B.probs_free[pb] = mark(B.c_mm)

            for gi in range(NG):
                emit_scores(gi)
                if gi == 1 and B.norm_q:
                    B.norm_q.pop(0)()
                if gi >= 1:
                    emit_sumpv(gi - 1, first=(gi - 1 == 0))
            emit_sumpv(NG - 1, first=(NG == 1))
            B._pv_mm = B.c_mm.n
            pvn = B.c_mm.n

            def norm(h=h, q=q, pvn=pvn):
                E.op("scalar",
                     (lambda q: lambda eng: eng.activation(
                         B.s_dbc[:, :], sum_full(q), AF.Ln))(q),
                     waits=[(B.c_mm, pvn)] + FR(B.dbc_free), inc=B.c_act)
                E.op("scalar",
                     lambda eng: eng.activation(B.s_dbc[:, :], B.s_dbc[:, :],
                                                AF.Exp, scale=-1.0),
                     inc=B.c_act)
                rcn = B.c_act.n
                E.op("vector",
                     (lambda h, q: lambda eng: eng.tensor_mul(
                         B.s_att[:, h * Wd:(h + 1) * Wd],
                         pv_acc(q, 0), B.s_dbc[:, :]))(h, q),
                     waits=[(B.c_act, rcn)] + FR(B.att_write_ok), inc=B.c_dve)
                B.dbc_free = mark(B.c_dve)
                B.pv_free[q] = mark(B.c_dve)
            B.norm_q.append(norm)
        B.ksb_free[buf] = (B.c_mm, B._sc_mm_last)
        B.vsb_free[buf] = (B.c_mm, B._pv_mm)
    while B.norm_q:
        B.norm_q.pop(0)()
    B.att_done = B.c_dve.n
    B.negc0_free = mark(B.c_act)
    B.psb_free = FR(B.pv_free[1])


def emit_wo_half(B, half, src_res, dst, src_ready):
    E, cfg = B.E, B.cfg
    Wd = cfg.W
    occ = cfg.XP
    n_pass = cfg.DM // occ
    att_need = B.att_done
    dst_marks = []
    BB = max(1, min(8, cfg.DM // occ))
    for p in range(n_pass):
        for k0 in range(0, cfg.DMC, BB):
            nb = min(BB, cfg.DMC - k0)
            buf = (k0 // BB) % 2
            wdb = B.s_wd[buf]
            E.op("sync",
                 (lambda wdb, k0, nb, p: lambda eng: eng.dma_start(
                     out=wdb[:, 0:nb * occ].rearrange("p (b u) -> p b u", u=occ),
                     in_=B.wo_bf[k0:k0 + nb, :, p * occ:(p + 1) * occ]
                     .rearrange("b p u -> p b u")))(wdb, k0, nb, p),
                 waits=FR(B.wcast_mark[("wo", k0 + nb - 1)]) + FR(B.wd_free[buf]),
                 inc=B.c_wdl[buf], inc_by=16)
            wldn = (B.c_wdl[buf], B.c_wdl[buf].n)
            for bn in range(nb):
                k = k0 + bn
                for ts in range(cfg.TS):
                    base = ts * (occ // 512)
                    for oc2 in range(occ // 512):
                        tail = (ts == cfg.TS - 1 and oc2 == occ // 512 - 1
                                and bn == nb - 1)
                        E.op("tensor",
                             (lambda wdb, k, bn, ts, oc2, base, first:
                              lambda eng: eng.matmul(
                                  B.ps6[:, (base + oc2) * 512:(base + oc2 + 1) * 512],
                                  B.s_att[:, k * Wd + ts * 128:k * Wd + ts * 128 + 128],
                                  wdb[:, (bn * (occ // 512) + oc2) * 512:
                                      (bn * (occ // 512) + oc2 + 1) * 512],
                                  start=first, stop=False))(wdb, k, bn, ts, oc2,
                                                            base, k == 0),
                             waits=([wldn, (B.c_dve, att_need)]
                                    + (FR(B.bank_free.get(base + oc2)) if k == 0 else [])),
                             inc=(B.c_mm if tail else None))
            B.wd_free[buf] = mark(B.c_mm)
        for ts in range(cfg.TS):
            base = ts * (occ // 512)
            for oc2 in range(occ // 512):
                E.op("tensor",
                     (lambda p, oc2, base: lambda eng: eng.matmul(
                         B.ps6[:, (base + oc2) * 512:(base + oc2 + 1) * 512],
                         B.s_ones_r[:, :],
                         B.s_brow[:, p * occ + oc2 * 512:p * occ + (oc2 + 1) * 512],
                         start=False, stop=True))(p, oc2, base),
                     waits=FR(B.brow_ready),
                     inc=(B.c_mm if oc2 == occ // 512 - 1 else None))
            mm_done = B.c_mm.n
            xslot = ts % 4
            xb = B.s_xp[xslot]
            cx = B.c_xp[xslot]
            row0 = half * Wd + ts * 128
            E.op("scalar",
                 (lambda xb, row0, p: lambda eng: eng.dma_start(
                     out=xb[:, 0:occ], in_=src_res[row0:row0 + 128,
                                                   p * occ:(p + 1) * occ]))(xb, row0, p),
                 waits=FR(B.xp_free[xslot], src_ready), inc=cx, inc_by=16)
            ldn = cx.n
            E.op("vector",
                 (lambda xb, base: lambda eng: eng.scalar_tensor_tensor(
                     xb[:, 0:occ], B.ps6[:, base * 512:base * 512 + occ],
                     1.0, xb[:, 0:occ], ALU.mult, ALU.add))(xb, base),
                 waits=[(B.c_mm, mm_done), (cx, ldn)], inc=B.c_dve)
            for oc2 in range(occ // 512):
                B.bank_free[base + oc2] = mark(B.c_dve)
            stt = B.c_dve.n
            E.op("scalar",
                 (lambda xb, row0, p: lambda eng: eng.dma_start(
                     out=dst[row0:row0 + 128, p * occ:(p + 1) * occ],
                     in_=xb[:, 0:occ]))(xb, row0, p),
                 waits=[(B.c_dve, stt)], inc=cx, inc_by=16)
            B.xp_free[xslot] = mark(cx)
            dst_marks.append(mark(cx))
    B.att_write_ok = mark(B.c_mm)
    B.dst_done = dst_marks


def _debug_out(B, src):
    E, cfg = B.E, B.cfg
    occ = cfg.XP
    done = B.dst_done
    for t in range(2 * cfg.TS):
        for p in range(cfg.DM // occ):
            xslot = (t * (cfg.DM // occ) + p) % 2
            xb = B.s_xp[xslot]
            cx = B.c_xp[xslot]
            E.op("sync",
                 (lambda xb, t, p: lambda eng: eng.dma_start(
                     out=xb[:, 0:occ], in_=src[t * 128:(t + 1) * 128,
                                               p * occ:(p + 1) * occ]))(xb, t, p),
                 waits=FR(B.xp_free[xslot], done), inc=cx, inc_by=16)
            ldn = cx.n
            E.op("sync",
                 (lambda xb, t, p: lambda eng: eng.dma_start(
                     out=B.out[t * 128:(t + 1) * 128, p * occ:(p + 1) * occ],
                     in_=xb[:, 0:occ]))(xb, t, p),
                 waits=[(cx, ldn)], inc=cx, inc_by=16)
            B.xp_free[xslot] = mark(cx)


def emit_tail(B):
    B.E.op("sync", lambda eng: eng.nop(),
           waits=[(c, c.n) for c in B.c_xp if c.n > 0])


def emit_all(B):
    cfg = B.cfg
    dn_jobs = [("dn", i, B.wdn_bf[i], B.wdn[i]) for i in range(cfg.NIB)]
    wo_jobs = [("wo", k, B.wo_bf[k], B.wo[k]) for k in range(cfg.DMC)]
    if cfg.stop == "prod":
        emit_consts(B)
        for e in ("tensor", "vector", "scalar"):
            B.E.op(e, lambda eng: eng.nop(), waits=[(B.c_const, B.const_n)])
        emit_weight_cast(B, dn_jobs + wo_jobs)
        emit_ln(B, B.x_loc, 0, None)
        emit_ffn_half(B, 0, B.x_loc, B.x1, None)
        emit_tail(B)
        return
    emit_consts(B)
    for e in ("tensor", "vector", "scalar"):
        B.E.op(e, lambda eng: eng.nop(), waits=[(B.c_const, B.const_n)])
    B.pending_casts.extend(dn_jobs)
    for b in range(2):
        emit_ln(B, B.x_loc, b, None)
        if b == 1:
            B.pending_casts.extend(wo_jobs)
        emit_ffn_half(B, b, B.x_loc, B.x1, None)
        B.xready[("x1", b)] = B.dst_done
    if cfg.stop == "ffn1":
        _debug_out(B, B.x1)
        emit_tail(B)
        return
    for b in range(2):
        emit_ln(B, B.x1, b, B.xready[("x1", b)])
        emit_qkv_half(B, b)
        emit_gather(B, b)
        if b == 0:
            emit_kv_prefetch(B, 0, 0, want_k=False)
            emit_kv_prefetch(B, 0, 1)
    B.att_write_ok = B.xlnt_write_ok_att
    load_brow(B, B.brow_o)
    for b in range(2):
        emit_attention_half(B, b)
        if b == 0:
            emit_kv_prefetch(B, 1, 0)
            emit_kv_prefetch(B, 1, 1)
        emit_wo_half(B, b, B.x1, B.x2, B.xready[("x1", b)])
        B.xready[("x2", b)] = B.dst_done
    if cfg.stop == "attn":
        _debug_out(B, B.x2)
        emit_tail(B)
        return
    load_brow(B, B.brow_d)
    B.xlnt_write_ok = B.att_write_ok
    for b in range(2):
        emit_ln(B, B.x2, b, B.xready[("x2", b)])
        emit_ffn_half(B, b, B.x2, B.out, B.xready[("x2", b)])
    emit_tail(B)


def build_program(cfg):
    B = Build(cfg)
    emit_all(B)
    nc, E = B.nc, B.E
    with nc.Block() as block:
        @block.sync
        def _(eng):
            E.emit_engine("sync", eng)

        @block.scalar
        def _(eng):
            E.emit_engine("scalar", eng)

        @block.tensor
        def _(eng):
            E.emit_engine("tensor", eng)

        @block.vector
        def _(eng):
            E.emit_engine("vector", eng)

        @block.gpsimd
        def _(eng):
            E.emit_engine("gpsimd", eng)
    return B, nc


def kernel(**inputs):
    cfg = Cfg()
    in_maps = host_prepare(cfg, inputs)
    B, nc = build_program(cfg)
    res = run_bass_kernel_spmd(nc, in_maps, core_ids=list(range(NCORES)))
    out = np.empty((cfg.B, cfg.S, cfg.DM), np.float32)
    for c in range(NCORES):
        loc = np.asarray(res.results[c]["out_loc"], dtype=np.float32)
        out[:, c::NCORES, :] = loc.reshape(cfg.B, cfg.L, cfg.DM)
    return out


if __name__ == "__main__":
    cfg = Cfg()
    B, nc = build_program(cfg)
    print("built:", nc.n_instructions(), "instructions; sbuf",
          B.sbuf_used, "/", nc.SBUF_PARTITION_SIZE_BYTES)
